# version 03
# baseline (speedup 1.0000x reference)
import numpy as np
from contextlib import ExitStack

import concourse.bass as bass
import concourse.mybir as mybir
from concourse.bass_utils import run_bass_kernel_spmd

F32 = mybir.dt.float32
BF16 = mybir.dt.bfloat16
I32 = mybir.dt.int32
U8 = mybir.dt.uint8
AF = mybir.ActivationFunctionType
ALU = mybir.AluOpType
AX = mybir.AxisListType

MAGIC = 12582912.0
D = 1024
S = 4096
ND = D // 128
TT = 512
NTT = S // TT
LN_EPS = 1e-5


def pack_w(w):
    O, K = w.shape
    nO, nK = O // 128, K // 128
    a = np.ascontiguousarray(w.T)
    a = a.reshape(nK // 2, 2, 128, nO, 128)
    a = a.transpose(3, 0, 2, 1, 4)
    return np.ascontiguousarray(a.reshape(nO, nK // 2, 128, 256))


def feat_major(x2d):
    return np.ascontiguousarray(x2d.T.astype(np.float32, copy=False))


def build_consts():
    c = {}
    c["ones_col"] = np.full((128, 1), 1.0 / D, np.float32)
    c["ones_row"] = np.ones((1, 128), np.float32)
    c["eps_ln"] = np.full((1, 1), LN_EPS, np.float32)
    c["ident_bf16"] = np.eye(128, dtype=np.float32)
    c["ident_f32"] = np.eye(128, dtype=np.float32)
    e32 = np.zeros((32, 1024), np.float32)
    for d in range(1024):
        e32[(d // 32) % 32, d] = 1.0
    c["e32"] = e32
    e16x = np.zeros((64, 1024), np.float32)
    for d in range(1024):
        e16x[d // 16, d] = 1.0
    c["e16x"] = e16x
    e176a = np.zeros((128, 16 * 128), np.float32)
    for jj in range(16):
        for p in range(128):
            e176a[8 * jj + p // 16, 128 * jj + p] = 1.0
    c["e176a"] = e176a
    e176b = np.zeros((48, 6 * 128), np.float32)
    for jj in range(6):
        for p in range(128):
            e176b[8 * jj + p // 16, 128 * jj + p] = 1.0
    c["e176b"] = e176b
    ehq = np.zeros((128, 4 * 8), np.float32)
    for i in range(4):
        for p in range(128):
            ehq[p, 8 * i + 2 * i + p // 64] = 0.125
    c["ehq"] = ehq
    ehb = np.zeros((8, 4 * 128), np.float32)
    for i in range(4):
        for p in range(128):
            ehb[2 * i + p // 64, 128 * i + p] = 1.0
    c["ehb"] = ehb
    c["eps_ln2"] = np.full((1, 1), 1e-6, np.float32)
    ehq16 = np.zeros((128, 8 * 16), np.float32)
    for i in range(8):
        for p in range(128):
            ehq16[p, 16 * i + 2 * i + p // 64] = 1.0
    c["ehq16"] = ehq16
    ehb16 = np.zeros((16, 8 * 128), np.float32)
    for i in range(8):
        for p in range(128):
            ehb16[2 * i + p // 64, 128 * i + p] = 1.0
    c["ehb16"] = ehb16
    tri = np.tril(np.ones((128, 128), np.float32))
    c["tri"] = np.triu(np.ones((128, 128), np.float32)).astype(np.float32)
    c["ones_col1"] = np.ones((128, 1), np.float32)
    return c


CONST_SPECS = {
    "ones_col": ([128, 1], F32),
    "ones_row": ([1, 128], F32),
    "eps_ln": ([1, 1], F32),
    "ident_bf16": ([128, 128], F32),
    "ident_f32": ([128, 128], F32),
    "e32": ([32, 1024], F32),
    "e16x": ([64, 1024], F32),
    "e176a": ([128, 16 * 128], F32),
    "e176b": ([48, 6 * 128], F32),
    "ehq": ([128, 4 * 8], F32),
    "ehb": ([8, 4 * 128], F32),
    "eps_ln2": ([1, 1], F32),
    "ehq16": ([128, 8 * 16], F32),
    "ehb16": ([16, 8 * 128], F32),
    "tri": ([128, 128], F32),
    "ones_col1": ([128, 1], F32),
}



class Em:

    EKEYS = ("sy", "v", "sc", "t", "gp")

    def __init__(self, nc, stack, pool_bytes=None):
        self.nc = nc
        self.stack = stack
        if pool_bytes is not None:
            self.POOL_BYTES = pool_bytes
        self.q = {k: [] for k in self.EKEYS}
        self.sems = {}
        self.cnt = {}
        self.sb_tensors = {}
        self.ps_tensors = {}

    def cntv(self, name):
        self.sem(name)
        return self.cnt[name]

    def sem(self, name):
        if name not in self.sems:
            self.sems[name] = self.stack.enter_context(self.nc.semaphore(name))
            self.cnt[name] = 0
        return self.sems[name]

    def sb(self, name, shape, dtype):
        if name not in self.sb_tensors:
            st = getattr(self, "_scope", None) or self.stack
            self.sb_tensors[name] = st.enter_context(
                self.nc.sbuf_tensor(name, shape, dtype))
            if getattr(self, "_scope", None) is not None:
                self._scope_names.append(name)
        return self.sb_tensors[name]

    def sb_global(self, name, shape, dtype):
        if name not in self.sb_tensors:
            self.sb_tensors[name] = self.stack.enter_context(
                self.nc.sbuf_tensor(name, shape, dtype))
            self._global_names = getattr(self, "_global_names", set())
            self._global_names.add(name)
        return self.sb_tensors[name]

    def push_scope(self):
        self._scope = ExitStack()
        self._scope_names = []

    def pop_scope(self):
        for n in self._scope_names:
            if n in self.sb_tensors:
                del self.sb_tensors[n]
        self._scope.close()
        self._scope = None
        self._scope_names = []

    def join(self):
        for ek in self.EKEYS:
            for name in list(self.sems.keys()):
                if self.cnt[name] > 0:
                    self.wait(ek, name, self.cnt[name])

    POOL_BYTES = 118784

    def pool(self):
        if hasattr(self, "sb_global"):
            return self.sb_global("g_pool", [128, self.POOL_BYTES], U8)
        return self.sb("g_pool", [128, self.POOL_BYTES], U8)

    def carve(self, off_bytes, cols, dtype):
        esz = mybir.dt.size(dtype)
        p = self.pool()
        return p[:, off_bytes:off_bytes + cols * esz].bitcast(dtype)

    def ps(self, name, shape=(128, 512), dtype=F32):
        if name not in self.ps_tensors:
            self.ps_tensors[name] = self.stack.enter_context(
                self.nc.psum_tensor(name, list(shape), dtype))
        return self.ps_tensors[name]

    def op(self, ek, fn):
        self.q[ek].append(fn)

    def opi(self, ek, fn, sem_name, k=1):
        h = self.sem(sem_name)
        self.cnt[sem_name] += k
        tot = self.cnt[sem_name]
        self.q[ek].append(lambda e, fn=fn, h=h, k=k: fn(e).then_inc(h, k))
        return tot

    def vbar(self, ek="v", sem_name="dve"):
        h = self.sem(sem_name)
        self.cnt[sem_name] += 1
        tot = self.cnt[sem_name]
        self.q[ek].append(lambda e, h=h: e.engine_nop().then_inc(h, 1))
        self.q[ek].append(lambda e, h=h, tot=tot: e.wait_ge(h, tot))
        return tot

    def wait(self, ek, sem_name, val):
        h = self.sem(sem_name)
        self.q[ek].append(lambda e, h=h, val=val: e.wait_ge(h, val))

    def wait_cur(self, ek, sem_name):
        self.wait(ek, sem_name, self.cnt.get(sem_name, 0))

    def finalize(self):
        block = self.stack.enter_context(self.nc.Block())
        for key, attr in (("sy", "sync"), ("v", "vector"), ("sc", "scalar"),
                          ("t", "tensor"), ("gp", "gpsimd")):
            ops = self.q[key]
            if ops:
                getattr(block, attr)(lambda e, ops=ops: [fn(e) for fn in ops])


BASE_CONSTS = ("ones_col", "ones_row", "eps_ln")
DRAM_ONLY_CONSTS = ("e32", "e16x", "e176a", "e176b", "ehb16")


def declare_consts(nc, em, need=(), dram_only=DRAM_ONLY_CONSTS):
    aps = {}
    keep = set(BASE_CONSTS) | set(need)
    for name, (shape, dt) in CONST_SPECS.items():
        if name not in keep:
            continue
        if name in dram_only:
            aps["@" + name] = nc.dram_tensor("c_" + name, list(shape), dt,
                                             kind="ExternalInput").ap()
            aps["#" + name] = (shape, dt)
            continue
        dram = nc.dram_tensor("c_" + name, list(shape), dt, kind="ExternalInput").ap()
        sdt = BF16 if name == "ident_bf16" else dt
        sbt = em.sb("cs_" + name, shape, sdt)
        if sdt != dt:
            em.opi("gp", lambda e, d=sbt, s=dram: e.dma_start(out=d[:], in_=s), "c_ldg", 16)
        else:
            em.opi("sy", lambda e, d=sbt, s=dram: e.dma_start(out=d[:], in_=s), "c_ld", 16)
        aps[name] = sbt
    oc_b = em.sb("cs_ones_col_b", [128, 1], BF16)
    em.wait("v", "c_ld", em.cntv("c_ld"))
    em.opi("v", lambda e, d=oc_b, s=aps["ones_col"]: e.tensor_copy(out=d[:], in_=s[:]),
           "c_prep", 1)
    aps["ones_col_b"] = oc_b
    return aps


_PAT_N = [0]


def load_pattern(em, C, name):
    shape, dt = C["#" + name]
    _PAT_N[0] += 1
    t = em.sb(f"pat_{name}_{_PAT_N[0]}", list(shape), dt)
    em.opi("sy", lambda e, t=t, s=C["@" + name]: e.dma_start(out=t[:], in_=s), "xld", 16)
    n = em.cntv("xld")
    for ek in ("t", "v", "sc"):
        em.wait(ek, "xld", n)
    return t



def emit_ln_tile(em, C, xt, xcb, tag, extra_wait=(), out_f32=False, t1ap=None,
                 xb16ap=None, xsqbap=None, eps_key="eps_ln",
                 ln_banks=("ps_lnS", "ps_lnB", "ps_lnR")):
    nc = em.nc
    xb16 = _T(xb16ap) if xb16ap is not None else em.sb_global("ln_xb16", [128, ND * TT], BF16)
    xsqb = _T(xsqbap) if xsqbap is not None else em.sb_global("ln_xsqb", [128, ND * TT], BF16)
    st_s1 = em.sb_global("ln_s1", [1, TT], F32)
    st_var = em.sb_global("ln_var", [1, TT], F32)
    st_rstd = em.sb_global("ln_rstd", [1, TT], F32)
    scr = em.sb_global("ln_scr", [1, TT], F32)
    t1 = None if t1ap is not None else em.sb_global("ln_t1", [128, ND * TT], F32)
    psS = em.ps(ln_banks[0])
    psB = em.ps(ln_banks[1])
    psR = em.ps(ln_banks[2])

    em.opi("v", lambda e: e.tensor_copy(out=xb16[:], in_=xt[:]), "dve")
    v_xb = em.cntv("dve")
    em.opi("sc", lambda e: e.activation(out=xsqb[:], in_=xt[:], func=AF.Square), "act")
    a_sq = em.cntv("act")

    em.wait("t", "dve", v_xb)
    em.wait("t", "act", a_sq)
    oc = C["ones_col_b"]
    for i in range(ND):
        fn = lambda e, i=i: e.matmul(psS[0:1, 0:TT], oc[:], xb16[:, i * TT:(i + 1) * TT],
                                     start=(i == 0), stop=(i == ND - 1))
        if i == ND - 1:
            em.opi("t", fn, "pe")
        else:
            em.op("t", fn)
    for i in range(ND):
        fn = lambda e, i=i: e.matmul(psS[32:33, 0:TT], oc[:], xsqb[:, i * TT:(i + 1) * TT],
                                     start=(i == 0), stop=(i == ND - 1))
        if i == ND - 1:
            em.opi("t", fn, "pe")
        else:
            em.op("t", fn)
    pe_stats = em.cntv("pe")

    em.wait("v", "pe", pe_stats)
    em.opi("v", lambda e: e.tensor_copy(out=st_s1[:], in_=psS[0:1, 0:TT]), "dve")
    em.vbar()
    em.opi("v", lambda e: e.tensor_mul(out=scr[:], in0=st_s1[:], in1=st_s1[:]), "dve")
    em.vbar()
    em.opi("v", lambda e: e.tensor_tensor(out=st_var[:], in0=psS[32:33, 0:TT], in1=scr[:],
                                          op=ALU.subtract), "dve")
    v_var = em.vbar()
    em.wait("sc", "dve", v_var)
    em.opi("sc", lambda e: e.activation(out=scr[:], in_=st_var[:], func=AF.Ln,
                                        bias=C[eps_key][:]), "act")
    em.wait("sc", "act", em.cntv("act"))
    em.opi("sc", lambda e: e.activation(out=st_rstd[:], in_=scr[:], func=AF.Exp,
                                        scale=-0.5), "act")
    a_rstd = em.cntv("act")

    orow = C["ones_row"]
    em.wait("t", "dve", v_var)
    em.opi("t", lambda e: e.matmul(psB[:, 0:TT], orow[:], st_s1[:], start=True, stop=True),
           "pe")
    em.wait("t", "act", a_rstd)
    em.opi("t", lambda e: e.matmul(psR[:, 0:TT], orow[:], st_rstd[:], start=True, stop=True),
           "pe")
    pe_bcast = em.cntv("pe")

    t1v = t1ap if t1ap is not None else t1[:]
    em.wait("v", "pe", pe_bcast)
    for i in range(ND):
        sl = slice(i * TT, (i + 1) * TT)
        em.op("v", lambda e, sl=sl: e.tensor_tensor(
            out=t1v[:, sl], in0=xt[:, sl],
            in1=psB[:, 0:TT], op=ALU.subtract))
        em.opi("v", lambda e, sl=sl: e.tensor_tensor(
            out=xcb[:, sl], in0=t1v[:, sl], in1=psR[:, 0:TT], op=ALU.mult), "dve")
    return em.cntv("dve")



def emit_load_x_tile(em, xt, xin_ap, tt):
    src = xin_ap.rearrange("(i p) t -> p i t", p=128)[:, :, tt * TT:(tt + 1) * TT]
    dst = xt[:].rearrange("p (i t) -> p i t", t=TT)
    return em.opi("sy", lambda e: e.dma_start(out=dst, in_=src), "xld", 16)


def emit_store_x_tile(em, xt, xout_ap, tt):
    dst = xout_ap.rearrange("(i p) t -> p i t", p=128)[:, :, tt * TT:(tt + 1) * TT]
    src = xt[:].rearrange("p (i t) -> p i t", t=TT)
    return em.opi("sy", lambda e: e.dma_start(out=dst, in_=src), "xst", 16)



def emit_l1_ffn(em, C, xin_ap, xout_ap, wg_i32, wu_i32, wd_i32, tag="l1", wstate=None):
    nc = em.nc
    H = 3072
    nJ = H // 128
    if wstate is not None and "wgb" in wstate:
        wgb, wub, wdb = wstate["wgb"], wstate["wub"], wstate["wdb"]
    else:
        wgb = nc.dram_tensor(tag + "_wgb", [nJ, 4, 128, 256], BF16, kind="Internal").ap()
        wub = nc.dram_tensor(tag + "_wub", [nJ, 4, 128, 256], BF16, kind="Internal").ap()
        wdb = nc.dram_tensor(tag + "_wdb", [ND, nJ // 2, 128, 256], BF16, kind="Internal").ap()
        for dst, srcw in ((wgb, wg_i32), (wub, wu_i32), (wdb, wd_i32)):
            em.opi("gp", lambda e, d=dst, s=srcw: e.dma_start(out=d, in_=s), "wcast", 16)
        if wstate is not None:
            wstate.update(wgb=wgb, wub=wub, wdb=wdb)
    em.wait("sy", "wcast", em.cntv("wcast"))

    xt = em.sb(tag + "_xt", [128, ND * TT], F32)
    xcb = _T(em.carve(0, ND * TT, BF16))
    hbuf = _T(em.carve(8192, nJ * TT, BF16))
    abuf = [em.sb(tag + f"_a{i}", [128, TT], F32) for i in range(2)]
    wbufA = [em.sb(tag + f"_wA{i}", [128, 8 * 128], BF16) for i in range(2)]
    wbufB = [em.sb(tag + f"_wB{i}", [128, 8 * 128], BF16) for i in range(2)]
    wbufD = [em.sb(tag + f"_wD{i}", [128, nJ * 128], BF16) for i in range(2)]
    psG = [em.ps("ps_g0"), em.ps("ps_g1")]
    psU = [em.ps("ps_u0"), em.ps("ps_u1")]

    wA_free = [0, 0]; wB_free = [0, 0]; wD_free = [0, 0]
    psG_ev = [0, 0]
    psU_ev = [0, 0]
    a_free = [0, 0]

    for tt in range(NTT):
        if tt > 0:
            em.wait("sy", "xst", em.cntv("xst"))
        x_ld = emit_load_x_tile(em, xt, xin_ap, tt)
        em.wait("v", "xld", x_ld)
        em.wait("sc", "xld", x_ld)
        if tt > 0:
            em.wait("v", "pe", em.cntv("pe"))
        emit_ln_tile(em, C, xt, xcb, f"{tag}_{tt}",
                     t1ap=em.carve(49152, ND * TT, F32),
                     xb16ap=em.carve(32768, ND * TT, BF16),
                     xsqbap=em.carve(40960, ND * TT, BF16))
        xcb_ready = em.cntv("dve")
        em.wait("t", "dve", xcb_ready)

        for j in range(nJ):
            b = j % 2
            em.wait("sy", "pe", wA_free[b])
            wlA = em.opi("sy", lambda e, j=j, b=b: e.dma_start(
                out=wbufA[b][:].rearrange("p (k o) -> p k o", o=256),
                in_=wgb[j].rearrange("k p o -> p k o")), "wld", 16)
            wlB = em.opi("sy", lambda e, j=j, b=b: e.dma_start(
                out=wbufB[b][:].rearrange("p (k o) -> p k o", o=256),
                in_=wub[j].rearrange("k p o -> p k o")), "wld", 16)
            em.wait("t", "wld", wlB)
            em.wait("t", "act", psG_ev[b])
            em.wait("t", "dve", psU_ev[b])
            for i in range(ND):
                fn = lambda e, b=b, i=i: e.matmul(
                    psG[b][:, 0:TT], wbufA[b][:, i * 128:(i + 1) * 128],
                    xcb[:, i * TT:(i + 1) * TT],
                    start=(i == 0), stop=(i == ND - 1))
                peg = em.opi("t", fn, "pe") if i == ND - 1 else em.op("t", fn)
            for i in range(ND):
                fn = lambda e, b=b, i=i: e.matmul(
                    psU[b][:, 0:TT], wbufB[b][:, i * 128:(i + 1) * 128],
                    xcb[:, i * TT:(i + 1) * TT],
                    start=(i == 0), stop=(i == ND - 1))
                peu = em.opi("t", fn, "pe") if i == ND - 1 else em.op("t", fn)
            wA_free[b] = wB_free[b] = peu
            em.wait("sc", "pe", peg)
            em.wait("sc", "dve", a_free[b])
            sca = em.opi("sc", lambda e, b=b: e.activation(
                out=abuf[b][:], in_=psG[b][:, 0:TT], func=AF.Silu,
                scale=1.0 / 127.0), "act")
            psG_ev[b] = sca
            em.wait("v", "pe", peu)
            em.wait("v", "act", sca)
            hj = em.opi("v", lambda e, j=j, b=b: e.scalar_tensor_tensor(
                out=hbuf[:, j * TT:(j + 1) * TT], in0=psU[b][:, 0:TT],
                scalar=1.0 / 127.0, in1=abuf[b][:], op0=ALU.mult, op1=ALU.mult), "dve")
            psU_ev[b] = a_free[b] = hj
        h_ready = em.cntv("dve")

        em.wait("t", "dve", h_ready)
        psD_ev = psG_ev
        for jd in range(ND):
            b = jd % 2
            em.wait("sy", "pe", wD_free[b])
            wlD = em.opi("sy", lambda e, jd=jd, b=b: e.dma_start(
                out=wbufD[b][:].rearrange("p (k o) -> p k o", o=256),
                in_=wdb[jd].rearrange("k p o -> p k o")), "wld", 16)
            em.wait("t", "wld", wlD)
            em.wait("t", "act", psD_ev[b])
            em.wait("t", "dve", psU_ev[b])
            for i in range(nJ):
                fn = lambda e, b=b, i=i: e.matmul(
                    psG[b][:, 0:TT], wbufD[b][:, i * 128:(i + 1) * 128],
                    hbuf[:, i * TT:(i + 1) * TT],
                    start=(i == 0), stop=(i == nJ - 1))
                ped = em.opi("t", fn, "pe") if i == nJ - 1 else em.op("t", fn)
            wD_free[b] = ped
            em.wait("v", "pe", ped)
            ev = em.opi("v", lambda e, jd=jd, b=b: e.scalar_tensor_tensor(
                out=xt[:, jd * TT:(jd + 1) * TT], in0=psG[b][:, 0:TT],
                scalar=0.5 / 127.0, in1=xt[:, jd * TT:(jd + 1) * TT],
                op0=ALU.mult, op1=ALU.add), "dve")
            psU_ev[b] = ev
            psD_ev = [max(psD_ev[0], 0), max(psD_ev[1], 0)]
        em.wait("sy", "dve", em.cntv("dve"))
        emit_store_x_tile(em, xt, xout_ap, tt)
    em.wait("sy", "xst", em.cntv("xst"))



def emit_l0_ffn(em, C, xin_ap, xout_ap, w1p, w2p, h1p, h2p, w3p,
                s1_ap, s2_ap, sh1_ap, sh2_ap, s3_ap, tag="l0", wstate=None):
    nc = em.nc
    H = 4096
    nJ = H // 128
    if wstate is not None and "w1b" in wstate:
        w1b, w2b, h1b, h2b, w3b = (wstate["w1b"], wstate["w2b"], wstate["h1b"],
                                   wstate["h2b"], wstate["w3b"])
    else:
        w1b = nc.dram_tensor(tag + "_w1b", [nJ, 4, 128, 256], BF16, kind="Internal").ap()
        w2b = nc.dram_tensor(tag + "_w2b", [nJ, 4, 128, 256], BF16, kind="Internal").ap()
        h1b = nc.dram_tensor(tag + "_h1b", [nJ, 16, 128, 256], BF16, kind="Internal").ap()
        h2b = nc.dram_tensor(tag + "_h2b", [nJ, 16, 128, 256], BF16, kind="Internal").ap()
        w3b = nc.dram_tensor(tag + "_w3b", [ND, 16, 128, 256], BF16, kind="Internal").ap()
        for dst, srcw in ((w1b, w1p), (w2b, w2p), (h1b, h1p), (h2b, h2p), (w3b, w3p)):
            em.opi("gp", lambda e, d=dst, s=srcw: e.dma_start(out=d, in_=s), "wcast", 16)
        if wstate is not None:
            wstate.update(w1b=w1b, w2b=w2b, h1b=h1b, h2b=h2b, w3b=w3b)
    em.wait("sy", "wcast", em.cntv("wcast"))

    xt = em.sb(tag + "_xt", [128, ND * TT], F32)
    xcb = _T(em.carve(0, ND * TT, BF16))
    hbuf = _T(em.carve(8192, nJ * TT, BF16))
    cbuf = _T(em.carve(40960, nJ * TT, BF16))
    abuf = [em.sb(tag + f"_a{i}", [128, TT], F32) for i in range(2)]
    s1sb = em.sb(tag + "_s1", [128, nJ], F32)
    s2sb = em.sb(tag + "_s2", [128, nJ], F32)
    sh1sb = em.sb(tag + "_sh1", [128, nJ], F32)
    sh2sb = em.sb(tag + "_sh2", [128, nJ], F32)
    s3sb = em.sb(tag + "_s3", [128, ND], F32)
    for sb_t, ap_v, n in ((s1sb, s1_ap, nJ), (s2sb, s2_ap, nJ), (sh1sb, sh1_ap, nJ),
                          (sh2sb, sh2_ap, nJ), (s3sb, s3_ap, ND)):
        for j in range(n):
            em.opi("sy", lambda e, sb_t=sb_t, ap_v=ap_v, j=j: e.dma_start(
                out=sb_t[:, j:j + 1],
                in_=ap_v[128 * j:128 * (j + 1)].rearrange("(p o) -> p o", o=1)), "xld", 16)
    sld = em.cntv("xld")
    em.wait("v", "xld", sld)
    em.opi("v", lambda e: e.tensor_scalar(out=s3sb[:], in0=s3sb[:], scalar1=0.5,
                                          scalar2=None, op0=ALU.mult), "dve")
    em.vbar()
    wbuf1 = [em.sb(tag + f"_wb1{i}", [128, 8 * 128], BF16) for i in range(2)]
    wbuf2 = [em.sb(tag + f"_wb2{i}", [128, 8 * 128], BF16) for i in range(2)]
    wbufH = [em.sb(tag + f"_wbh{i}", [128, 32 * 128], BF16) for i in range(2)]
    psA = [em.ps("ps_g0"), em.ps("ps_g1")]
    psB = [em.ps("ps_u0"), em.ps("ps_u1")]

    wz_free = {id(b): 0 for b in wbuf1 + wbuf2 + wbufH}
    psA_ev = [0, 0]
    psB_ev = [0, 0]
    a_free = [0, 0]

    def mm_group(bank, wbuf, rhs_buf, nK, j):
        last = None
        for i in range(nK):
            fn = lambda e, bank=bank, wbuf=wbuf, rhs_buf=rhs_buf, i=i: e.matmul(
                bank[:, 0:TT], wbuf[:, i * 128:(i + 1) * 128],
                rhs_buf[:, i * TT:(i + 1) * TT],
                start=(i == 0), stop=(i == nK - 1))
            if i == nK - 1:
                last = em.opi("t", fn, "pe")
            else:
                em.op("t", fn)
        return last

    for tt in range(NTT):
        if tt > 0:
            em.wait("sy", "xst", em.cntv("xst"))
        x_ld = emit_load_x_tile(em, xt, xin_ap, tt)
        em.wait("v", "xld", x_ld)
        em.wait("sc", "xld", x_ld)
        if tt > 0:
            em.wait("v", "pe", em.cntv("pe"))
        emit_ln_tile(em, C, xt, xcb, f"{tag}_{tt}",
                     t1ap=em.carve(90112, ND * TT, F32),
                     xb16ap=em.carve(73728, ND * TT, BF16),
                     xsqbap=em.carve(81920, ND * TT, BF16))
        em.wait("t", "dve", em.cntv("dve"))

        for j in range(nJ):
            b = j % 2
            em.wait("sy", "pe", max(wz_free[id(wbuf1[b])], wz_free[id(wbuf2[b])]))
            em.opi("sy", lambda e, j=j, b=b: e.dma_start(
                out=wbuf1[b][:].rearrange("p (k o) -> p k o", o=256),
                in_=w1b[j].rearrange("k p o -> p k o")), "wld", 16)
            wl = em.opi("sy", lambda e, j=j, b=b: e.dma_start(
                out=wbuf2[b][:].rearrange("p (k o) -> p k o", o=256),
                in_=w2b[j].rearrange("k p o -> p k o")), "wld", 16)
            em.wait("t", "wld", wl)
            em.wait("t", "act", psA_ev[b])
            em.wait("t", "dve", psB_ev[b])
            peg = mm_group(psA[b], wbuf1[b], xcb, ND, j)
            peu = mm_group(psB[b], wbuf2[b], xcb, ND, j)
            wz_free[id(wbuf1[b])] = wz_free[id(wbuf2[b])] = peu
            em.wait("sc", "pe", peg)
            em.wait("sc", "dve", a_free[b])
            sca = em.opi("sc", lambda e, j=j, b=b: e.activation(
                out=abuf[b][:], in_=psA[b][:, 0:TT], func=AF.Silu,
                scale=s1sb[:, j:j + 1]), "act")
            psA_ev[b] = sca
            em.wait("v", "pe", peu)
            em.wait("v", "act", sca)
            hj = em.opi("v", lambda e, j=j, b=b: e.scalar_tensor_tensor(
                out=hbuf[:, j * TT:(j + 1) * TT], in0=psB[b][:, 0:TT],
                scalar=s2sb[:, j:j + 1], in1=abuf[b][:],
                op0=ALU.mult, op1=ALU.mult), "dve")
            psB_ev[b] = a_free[b] = hj

        em.wait("t", "dve", em.cntv("dve"))
        for j in range(nJ):
            b = j % 2
            em.wait("sy", "pe", wz_free[id(wbufH[b])])
            wl = em.opi("sy", lambda e, j=j, b=b: e.dma_start(
                out=wbufH[b][:].rearrange("p (k o) -> p k o", o=256),
                in_=h1b[j].rearrange("k p o -> p k o")), "wld", 16)
            em.wait("t", "wld", wl)
            em.wait("t", "act", psA_ev[b])
            pej = mm_group(psA[b], wbufH[b], hbuf, nJ, j)
            wz_free[id(wbufH[b])] = pej
            em.wait("sc", "pe", pej)
            sca = em.opi("sc", lambda e, j=j, b=b: e.activation(
                out=cbuf[:, j * TT:(j + 1) * TT], in_=psA[b][:, 0:TT], func=AF.Gelu,
                scale=sh1sb[:, j:j + 1]), "act")
            psA_ev[b] = sca

        em.wait("t", "act", em.cntv("act"))
        for j in range(nJ):
            b = j % 2
            em.wait("sy", "pe", wz_free[id(wbufH[b])])
            wl = em.opi("sy", lambda e, j=j, b=b: e.dma_start(
                out=wbufH[b][:].rearrange("p (k o) -> p k o", o=256),
                in_=h2b[j].rearrange("k p o -> p k o")), "wld", 16)
            em.wait("t", "wld", wl)
            em.wait("t", "act", psA_ev[b])
            pej = mm_group(psA[b], wbufH[b], cbuf, nJ, j)
            wz_free[id(wbufH[b])] = pej
            em.wait("sc", "pe", pej)
            sca = em.opi("sc", lambda e, j=j, b=b: e.activation(
                out=hbuf[:, j * TT:(j + 1) * TT], in_=psA[b][:, 0:TT], func=AF.Gelu,
                scale=sh2sb[:, j:j + 1]), "act")
            psA_ev[b] = sca

        em.wait("t", "act", em.cntv("act"))
        for jd in range(ND):
            b = jd % 2
            em.wait("sy", "pe", wz_free[id(wbufH[b])])
            wl = em.opi("sy", lambda e, jd=jd, b=b: e.dma_start(
                out=wbufH[b][:, 0:16 * 256].rearrange("p (k o) -> p k o", o=256),
                in_=w3b[jd].rearrange("k p o -> p k o")), "wld", 16)
            em.wait("t", "wld", wl)
            em.wait("t", "act", psA_ev[b])
            em.wait("t", "dve", psB_ev[b])
            pej = mm_group(psA[b], wbufH[b], hbuf, nJ, jd)
            wz_free[id(wbufH[b])] = pej
            em.wait("v", "pe", pej)
            ev = em.opi("v", lambda e, jd=jd, b=b: e.scalar_tensor_tensor(
                out=xt[:, jd * TT:(jd + 1) * TT], in0=psA[b][:, 0:TT],
                scalar=s3sb[:, jd:jd + 1], in1=xt[:, jd * TT:(jd + 1) * TT],
                op0=ALU.mult, op1=ALU.add), "dve")
            psB_ev[b] = ev
            psA_ev[b] = em.cntv("act")
        em.wait("sy", "dve", em.cntv("dve"))
        emit_store_x_tile(em, xt, xout_ap, tt)
    em.wait("sy", "xst", em.cntv("xst"))



def load_pp_vec(em, sbuf_t, ap_v, n, sem="xld"):
    for j in range(n):
        em.opi("sy", lambda e, j=j: e.dma_start(
            out=sbuf_t[:, j:j + 1],
            in_=ap_v[128 * j:128 * (j + 1)].rearrange("(p o) -> p o", o=1)), sem, 16)
    return em.cntv(sem)


def emit_l0_attn(em, C, xin_ap, xout_ap, wqp, wkp, wvp, wop,
                 bq_ap, bk_ap, bv_ap, bo_ap, alpha_ap, augp_ap,
                 lns_ap, lnb_ap, tag="l0a"):
    nc = em.nc
    NO = 4
    wq = em.sb(tag + "_wq", [128, NO * 1024], BF16)
    wk = em.sb(tag + "_wk", [128, NO * 1024], BF16)
    wv = em.sb(tag + "_wv", [128, NO * 1024], BF16)
    wo = em.sb(tag + "_wo", [128, ND * 512], BF16)
    for dst, srcw, n in ((wq, wqp, NO), (wk, wkp, NO), (wv, wvp, NO), (wo, wop, ND)):
        em.opi("gp", lambda e, dst=dst, srcw=srcw: e.dma_start(
            out=dst[:].rearrange("p (j kk c) -> p j kk c",
                                 j=srcw.tensor.shape[0], c=256),
            in_=srcw.rearrange("j kk p c -> p j kk c")), "wcast", 16)
    wc = em.cntv("wcast")
    bqc = em.sb(tag + "_bq", [128, NO], F32)
    bkc = em.sb(tag + "_bk", [128, NO], F32)
    bvc = em.sb(tag + "_bv", [128, NO], F32)
    boc = em.sb(tag + "_bo", [128, ND], F32)
    alc = em.sb(tag + "_al", [128, NO], F32)
    agc = em.sb(tag + "_ag", [128, NO], F32)
    lsc = em.sb(tag + "_ls", [128, ND], F32)
    lbc = em.sb(tag + "_lb", [128, ND], F32)
    for t_, a_, n_ in ((bqc, bq_ap, NO), (bkc, bk_ap, NO), (bvc, bv_ap, NO),
                       (boc, bo_ap, ND), (alc, alpha_ap, NO), (agc, augp_ap, NO),
                       (lsc, lns_ap, ND), (lbc, lnb_ap, ND)):
        load_pp_vec(em, t_, a_, n_)
    vld = em.cntv("xld")
    dec = em.sb(tag + "_dec", [128, NO], F32)
    omd = em.sb(tag + "_omd", [128, NO], F32)
    bvo = em.sb(tag + "_bvo", [128, NO], F32)
    dec_rep = em.sb(tag + "_decr", [128, NO * TT], F32)
    em.wait("sc", "xld", vld)
    em.opi("sc", lambda e: e.activation(out=dec[:], in_=alc[:], func=AF.Sigmoid), "act")
    em.wait("v", "act", em.cntv("act"))
    em.opi("v", lambda e: e.tensor_scalar(out=omd[:], in0=dec[:], scalar1=-1.0,
                                          scalar2=1.0, op0=ALU.mult, op1=ALU.add), "dve")
    em.vbar()
    em.opi("v", lambda e: e.tensor_mul(out=bvo[:], in0=bvc[:], in1=omd[:]), "dve")
    vb = em.vbar()
    em.opi("gp", lambda e: e.memset(dec_rep[:], 0.0), "gpd", 1)
    em.wait("sc", "dve", vb)
    em.wait("sc", "gpd", em.cntv("gpd"))
    for i in range(NO):
        em.opi("sc", lambda e, i=i: e.activation(
            out=dec_rep[:, i * TT:(i + 1) * TT], in_=dec_rep[:, i * TT:(i + 1) * TT],
            func=AF.Identity, scale=0.0, bias=dec[:, i:i + 1]), "act")
    dr = em.cntv("act")

    xt = em.sb(tag + "_xt", [128, ND * TT], F32)
    xcb = _T(em.carve(0, ND * TT, BF16))
    qb = _T(em.carve(8192, NO * TT, F32))
    kb = _T(em.carve(16384, NO * TT, F32))
    vp = _T(em.carve(24576, NO * TT, F32))
    ub = _T(em.carve(32768, NO * TT, F32))
    stb = _T(em.carve(40960, NO * TT, F32))
    oib = _T(em.carve(49152, NO * TT, BF16))
    outb = _T(em.carve(53248, ND * TT, F32))
    ocn = _T(em.carve(69632, ND * TT, F32))
    prod = _T(em.carve(86016, NO * TT, F32))
    carry = em.sb(tag + "_carry", [128, NO], F32)
    probs = em.sb(tag + "_probs", [128, 4 * 8], F32)
    attnf = em.sb(tag + "_attnf", [128, 4 * 8], F32)
    attnT = em.sb(tag + "_attnT", [8, TT], F32)
    nmx = em.sb(tag + "_nmx", [128, 1], F32)
    ssum = em.sb(tag + "_ssum", [128, 1], F32)
    psQ = [em.ps("ps_g0"), em.ps("ps_g1")]
    psS = em.ps("ps_u0")
    psB = em.ps("ps_u1")

    em.wait("t", "wcast", wc)
    em.wait("t", "act", dr)

    for tt in range(NTT):
        if tt > 0:
            em.wait("sy", "xst", em.cntv("xst"))
        x_ld = emit_load_x_tile(em, xt, xin_ap, tt)
        em.wait("v", "xld", x_ld)
        em.wait("sc", "xld", x_ld)
        if tt > 0:
            em.wait("v", "pe", em.cntv("pe"))
        emit_ln_tile(em, C, xt, xcb, f"{tag}_{tt}",
                     t1ap=em.carve(102400, ND * TT, F32),
                     xb16ap=em.carve(53248, ND * TT, BF16),
                     xsqbap=em.carve(61440, ND * TT, BF16))
        em.wait("t", "dve", em.cntv("dve"))

        for (wbuf, dstb, bias_t, scale_t) in ((wq, qb, bqc, None), (wk, kb, bkc, None),
                                              (wv, vp, bvo, omd)):
            for j in range(NO):
                b = j % 2
                em.wait("t", "act", em.cntv("act") if tt == 0 and j < 2 else em.cntv("act"))
                for i in range(ND):
                    fn = lambda e, wbuf=wbuf, j=j, b=b, i=i: e.matmul(
                        psQ[b][:, 0:TT], wbuf[:, j * 1024 + i * 128: j * 1024 + (i + 1) * 128],
                        xcb[:, i * TT:(i + 1) * TT], start=(i == 0), stop=(i == ND - 1))
                    pj = em.opi("t", fn, "pe") if i == ND - 1 else em.op("t", fn)
                em.wait("sc", "pe", pj)
                if scale_t is None:
                    em.opi("sc", lambda e, dstb=dstb, bias_t=bias_t, j=j, b=b: e.activation(
                        out=dstb[:, j * TT:(j + 1) * TT], in_=psQ[b][:, 0:TT],
                        func=AF.Identity, bias=bias_t[:, j:j + 1]), "act")
                else:
                    em.opi("sc", lambda e, dstb=dstb, bias_t=bias_t, scale_t=scale_t, j=j, b=b:
                           e.activation(
                        out=dstb[:, j * TT:(j + 1) * TT], in_=psQ[b][:, 0:TT],
                        func=AF.Identity, scale=scale_t[:, j:j + 1],
                        bias=bias_t[:, j:j + 1]), "act")
        qkv_done = em.cntv("act")

        em.wait("v", "act", qkv_done)
        for i in range(NO):
            em.opi("v", lambda e, i=i: e.tensor_tensor(
                out=prod[:, i * TT:(i + 1) * TT], in0=qb[:, i * TT:(i + 1) * TT],
                in1=kb[:, i * TT:(i + 1) * TT], op=ALU.mult), "dve")
        pr = em.cntv("dve")
        em.wait("t", "dve", pr)
        for c in range(4):
            for i in range(NO):
                fn = lambda e, c=c, i=i: e.matmul(
                    psS[0:128, 8 * c:8 * (c + 1)], prod[:, i * TT + 128 * c: i * TT + 128 * (c + 1)],
                    C["ehq"][:, 8 * i:8 * (i + 1)], start=(i == 0), stop=(i == NO - 1))
                ps_ = em.opi("t", fn, "pe") if i == NO - 1 else em.op("t", fn)
        em.wait("v", "pe", em.cntv("pe"))
        em.wait("sc", "pe", em.cntv("pe"))
        for c in range(4):
            em.opi("v", lambda e, c=c: e.tensor_reduce(
                out=nmx[:], in_=psS[0:128, 8 * c:8 * (c + 1)], axis=AX.X, op=ALU.max,
                negate=True), "dve")
            nm = em.vbar()
            em.wait("sc", "dve", nm)
            ex = em.opi("sc", lambda e, c=c: e.activation(
                out=probs[:, 8 * c:8 * (c + 1)], in_=psS[0:128, 8 * c:8 * (c + 1)],
                func=AF.Exp, bias=nmx[:]), "act")
            em.wait("v", "act", ex)
            em.opi("v", lambda e, c=c: e.tensor_reduce(
                out=ssum[:], in_=probs[:, 8 * c:8 * (c + 1)], axis=AX.X, op=ALU.add), "dve")
            em.vbar()
            em.opi("v", lambda e: e.reciprocal(out=ssum[:], in_=ssum[:]), "dve")
            em.vbar()
            em.opi("v", lambda e, c=c: e.tensor_scalar(
                out=attnf[:, 8 * c:8 * (c + 1)], in0=probs[:, 8 * c:8 * (c + 1)],
                scalar1=ssum[:], scalar2=None, op0=ALU.mult), "dve")
        at = em.cntv("dve")
        em.wait("t", "dve", at)
        for c in range(4):
            em.opi("t", lambda e, c=c: e.transpose(
                psB[0:8, 128 * c:128 * (c + 1)], attnf[:, 8 * c:8 * (c + 1)],
                C["ident_f32"][:]), "pe")
        tp = em.cntv("pe")
        em.wait("v", "pe", tp)
        em.opi("v", lambda e: e.tensor_copy(out=attnT[:], in_=psB[0:8, 0:TT]), "dve")
        atT = em.vbar()
        em.wait("t", "dve", atT)
        for i in range(NO):
            bb = i % 2
            em.wait("t", "dve", em.cntv("dve"))
            pb = em.opi("t", lambda e, i=i, bb=bb: e.matmul(
                psQ[bb][:, 0:TT], C["ehb"][0:8, 128 * i:128 * (i + 1)], attnT[:],
                start=True, stop=True), "pe")
            em.wait("v", "pe", pb)
            em.opi("v", lambda e, i=i, bb=bb: e.tensor_tensor(
                out=ub[:, i * TT:(i + 1) * TT], in0=psQ[bb][:, 0:TT],
                in1=vp[:, i * TT:(i + 1) * TT], op=ALU.mult), "dve")
        u_done = em.cntv("dve")

        for i in range(NO):
            if tt == 0:
                em.opi("v", lambda e, i=i: e.tensor_tensor_scan(
                    out=stb[:, i * TT:(i + 1) * TT], data0=dec_rep[:, i * TT:(i + 1) * TT],
                    data1=ub[:, i * TT:(i + 1) * TT], initial=0.0,
                    op0=ALU.mult, op1=ALU.add), "dve")
            else:
                em.opi("v", lambda e, i=i: e.tensor_tensor_scan(
                    out=stb[:, i * TT:(i + 1) * TT], data0=dec_rep[:, i * TT:(i + 1) * TT],
                    data1=ub[:, i * TT:(i + 1) * TT], initial=carry[:, i:i + 1],
                    op0=ALU.mult, op1=ALU.add), "dve")
        em.vbar()
        for i in range(NO):
            em.opi("v", lambda e, i=i: e.tensor_copy(
                out=carry[:, i:i + 1], in_=stb[:, (i + 1) * TT - 1:(i + 1) * TT]), "dve")
        em.vbar()

        em.wait("sc", "dve", em.cntv("dve"))
        for i in range(NO):
            sl = slice(i * TT, (i + 1) * TT)
            em.opi("sc", lambda e, sl=sl, i=i: e.activation(
                out=prod[:, sl], in_=qb[:, sl], func=AF.Sigmoid,
                scale=agc[:, i:i + 1]), "act")
        ag = em.cntv("act")
        em.wait("v", "act", ag)
        for i in range(NO):
            sl = slice(i * TT, (i + 1) * TT)
            em.op("v", lambda e, sl=sl: e.tensor_tensor(
                out=ub[:, sl], in0=qb[:, sl], in1=stb[:, sl], op=ALU.mult))
            em.opi("v", lambda e, sl=sl: e.tensor_tensor(
                out=oib[:, sl], in0=ub[:, sl], in1=prod[:, sl], op=ALU.add), "dve")
        oi_done = em.cntv("dve")

        em.wait("t", "dve", oi_done)
        for j in range(ND):
            b = j % 2
            em.wait("t", "act", em.cntv("act"))
            for i in range(NO):
                fn = lambda e, j=j, b=b, i=i: e.matmul(
                    psQ[b][:, 0:TT], wo[:, j * 512 + i * 128: j * 512 + (i + 1) * 128],
                    oib[:, i * TT:(i + 1) * TT], start=(i == 0), stop=(i == NO - 1))
                pj = em.opi("t", fn, "pe") if i == NO - 1 else em.op("t", fn)
            em.wait("sc", "pe", pj)
            em.opi("sc", lambda e, j=j, b=b: e.activation(
                out=outb[:, j * TT:(j + 1) * TT], in_=psQ[b][:, 0:TT],
                func=AF.Identity, bias=boc[:, j:j + 1]), "act")
        ob_done = em.cntv("act")

        em.wait("v", "act", ob_done)
        em.wait("sc", "act", ob_done)
        emit_ln_tile(em, C, outb, ocn, f"{tag}_iln{tt}", out_f32=True,
                     t1ap=em.carve(102400, ND * TT, F32),
                     xb16ap=em.carve(0, ND * TT, BF16),
                     xsqbap=em.carve(8192, ND * TT, BF16),
                     eps_key="eps_ln2")
        for j in range(ND):
            sl = slice(j * TT, (j + 1) * TT)
            em.op("v", lambda e, sl=sl, j=j: e.tensor_scalar(
                out=ocn[:, sl], in0=ocn[:, sl], scalar1=lsc[:, j:j + 1],
                scalar2=lbc[:, j:j + 1], op0=ALU.mult, op1=ALU.add))
            em.opi("v", lambda e, sl=sl: e.tensor_tensor(
                out=xt[:, sl], in0=xt[:, sl], in1=ocn[:, sl], op=ALU.add), "dve")
        em.wait("sy", "dve", em.cntv("dve"))
        emit_store_x_tile(em, xt, xout_ap, tt)
    em.wait("sy", "xst", em.cntv("xst"))



def emit_l3_attn(em, C, xin_ap, xout_ap, wqp, wkp, wvp, wop,
                 bq_ap, bk_ap, bv_ap, bo_ap, tag="l3a"):
    nc = em.nc
    kv_loc = nc.dram_tensor(tag + "_kvloc", [2048, S], BF16, kind="Internal").ap()
    kv_all = nc.dram_tensor(tag + "_kvall", [8 * 2048, S], BF16, kind="Internal").ap()
    qloc = nc.dram_tensor(tag + "_qloc", [D, S], BF16, kind="Internal").ap()

    wq = em.sb(tag + "_wq", [128, ND * 1024], BF16)
    wk = em.sb(tag + "_wk", [128, ND * 1024], BF16)
    wv = em.sb(tag + "_wv", [128, ND * 1024], BF16)
    wo = em.sb(tag + "_wo", [128, ND * 1024], BF16)
    for dst, srcw in ((wq, wqp), (wk, wkp), (wv, wvp), (wo, wop)):
        em.opi("gp", lambda e, dst=dst, srcw=srcw: e.dma_start(
            out=dst[:].rearrange("p (j kk c) -> p j kk c", j=8, c=256),
            in_=srcw.rearrange("j kk p c -> p j kk c")), "wcast", 16)
    wc = em.cntv("wcast")
    bqc = em.sb(tag + "_bq", [128, ND], F32)
    bkc = em.sb(tag + "_bk", [128, ND], F32)
    bvc = em.sb(tag + "_bv", [128, ND], F32)
    boc = em.sb(tag + "_bo", [128, ND], F32)
    for t_, a_ in ((bqc, bq_ap), (bkc, bk_ap), (bvc, bv_ap), (boc, bo_ap)):
        load_pp_vec(em, t_, a_, ND)
    vld = em.cntv("xld")
    em.wait("v", "xld", vld)
    em.opi("v", lambda e: e.tensor_scalar(out=bqc[:], in0=bqc[:], scalar1=0.125,
                                          scalar2=None, op0=ALU.mult), "dve")
    em.vbar()
    ehb16f = load_pattern(em, C, "ehb16")
    ehq16b = em.sb(tag + "_ehqb", [128, 8 * 16], BF16)
    ehb16b = em.sb(tag + "_ehbb", [16, 8 * 128], BF16)
    em.wait("v", "c_ld", em.cntv("c_ld"))
    em.wait("v", "c_ldg", em.cntv("c_ldg"))
    em.opi("v", lambda e: e.tensor_copy(out=ehq16b[:], in_=C["ehq16"][:]), "dve")
    em.opi("v", lambda e: e.tensor_copy(out=ehb16b[:], in_=ehb16f[:]), "dve")
    epr = em.cntv("dve")

    xt = _T(em.carve(90112, ND * TT, F32))
    xcb = _T(em.carve(0, ND * TT, BF16))
    qtile = _T(em.carve(8192, ND * TT, BF16))
    ksb = [_T(em.carve(16384 + 8192 * i, ND * TT, BF16)) for i in range(2)]
    vsb = ksb
    ctx = _T(em.carve(32768, ND * TT, F32))
    ctxb = _T(em.carve(65536, ND * TT, BF16))
    prod = _T(em.carve(73728, ND * TT, BF16))
    sc_s = _T(em.carve(32768, 8 * TT, F32)[0:16, :])
    eb = _T(em.carve(81920, 8 * TT, BF16)[0:16, :])
    mxs = _T(em.carve(61440, TT, F32)[0:16, :])
    rsum = _T(em.carve(57344, TT, F32)[0:16, :])
    escr = _T(em.carve(59392, TT, F32)[0:16, :])
    psQ = [em.ps("ps_g0"), em.ps("ps_g1")]
    psS = em.ps("ps_u0")
    psB = [em.ps("ps_u1"), em.ps("ps_lnR")]

    em.wait("t", "wcast", wc)
    em.wait("t", "dve", epr)

    for ct in range(NTT):
        x_ld = emit_load_x_tile(em, xt, xin_ap, ct)
        em.wait("v", "xld", x_ld)
        em.wait("sc", "xld", x_ld)
        em.wait("v", "pe", em.cntv("pe"))
        emit_ln_tile(em, C, xt, xcb, f"{tag}_{ct}",
                     t1ap=em.carve(32768, ND * TT, F32),
                     xb16ap=em.carve(49152, ND * TT, BF16),
                     xsqbap=em.carve(57344, ND * TT, BF16))
        em.wait("t", "dve", em.cntv("dve"))
        for (wbuf, bias_t, is_q, dst_dram, row0) in (
                (wq, bqc, True, qloc, 0), (wk, bkc, False, kv_loc, 0),
                (wv, bvc, False, kv_loc, 1024)):
            ev = 0
            for j in range(ND):
                b = j % 2
                em.wait("t", "act", em.cntv("act"))
                for i in range(ND):
                    fn = lambda e, wbuf=wbuf, j=j, b=b, i=i: e.matmul(
                        psQ[b][:, 0:TT], wbuf[:, j * 1024 + i * 128: j * 1024 + (i + 1) * 128],
                        xcb[:, i * TT:(i + 1) * TT], start=(i == 0), stop=(i == ND - 1))
                    pj = em.opi("t", fn, "pe") if i == ND - 1 else em.op("t", fn)
                em.wait("sc", "pe", pj)
                em.wait("sc", "xst", em.cntv("xst"))
                if is_q:
                    ev = em.opi("sc", lambda e, j=j, b=b, bias_t=bias_t: e.activation(
                        out=qtile[:, j * TT:(j + 1) * TT], in_=psQ[b][:, 0:TT],
                        func=AF.Identity, scale=0.125, bias=bias_t[:, j:j + 1]), "act")
                else:
                    ev = em.opi("sc", lambda e, j=j, b=b, bias_t=bias_t: e.activation(
                        out=qtile[:, j * TT:(j + 1) * TT], in_=psQ[b][:, 0:TT],
                        func=AF.Identity, bias=bias_t[:, j:j + 1]), "act")
            em.wait("sy", "act", ev)
            if row0 == 0:
                dstap = dst_dram.rearrange("(i p) t -> p i t", p=128)[
                    :, 0:8, ct * TT:(ct + 1) * TT]
            else:
                dstap = dst_dram.rearrange("(i p) t -> p i t", p=128)[
                    :, 8:16, ct * TT:(ct + 1) * TT]
            em.opi("sy", lambda e, dstap=dstap: e.dma_start(
                out=dstap, in_=qtile[:].rearrange("p (i t) -> p i t", t=TT)), "xst", 16)
    em.wait("gp", "xst", em.cntv("xst"))

    cc = em.opi("gp", lambda e: e.collective_compute(
        "AllGather", ALU.bypass, ins=[kv_loc.opt()], outs=[kv_all.opt()],
        replica_groups=[[0, 1, 2, 3, 4, 5, 6, 7]]), "cc", 1)
    em.wait("sy", "cc", cc)

    kv_free = [0, 0]
    for ct in range(NTT):
        if ct > 0:
            em.wait("sy", "xst", em.cntv("xst"))
        x_ld = emit_load_x_tile(em, xt, xin_ap, ct)
        ql = em.opi("sy", lambda e, ct=ct: e.dma_start(
            out=qtile[:].rearrange("p (i t) -> p i t", t=TT),
            in_=qloc.rearrange("(i p) t -> p i t", p=128)[:, :, ct * TT:(ct + 1) * TT]),
            "xld", 16)
        for s in range(8):
            b = s % 2
            em.wait("sy", "pe", kv_free[b])
            kl = em.opi("sy", lambda e, s=s, ct=ct, b=b: e.dma_start(
                out=ksb[b][:].rearrange("p (i t) -> p i t", t=TT),
                in_=kv_all.rearrange("(g p) t -> p g t", p=128)[
                    :, 16 * s:16 * s + 8, ct * TT:(ct + 1) * TT]), "xld", 16)
            em.wait("v", "xld", max(ql, kl))
            em.wait("v", "pe", em.cntv("pe"))
            for i in range(ND):
                em.opi("v", lambda e, i=i, b=b: e.tensor_tensor(
                    out=prod[:, i * TT:(i + 1) * TT], in0=qtile[:, i * TT:(i + 1) * TT],
                    in1=ksb[b][:, i * TT:(i + 1) * TT], op=ALU.mult), "dve")
            prd = em.cntv("dve")
            em.wait("t", "dve", prd)
            for i in range(ND):
                fn = lambda e, i=i: e.matmul(
                    psS[0:16, 0:TT], ehq16b[:, 16 * i:16 * (i + 1)],
                    prod[:, i * TT:(i + 1) * TT], start=(i == 0), stop=(i == ND - 1))
                pss = em.opi("t", fn, "pe") if i == ND - 1 else em.op("t", fn)
            kv_free[b] = pss
            em.wait("v", "pe", pss)
            em.opi("v", lambda e, s=s: e.tensor_copy(
                out=sc_s[:, s * TT:(s + 1) * TT], in_=psS[0:16, 0:TT]), "dve")
        em.vbar()
        em.opi("v", lambda e: e.tensor_tensor(out=mxs[:], in0=sc_s[:, 0:TT],
                                              in1=sc_s[:, TT:2 * TT], op=ALU.max), "dve")
        for s in range(2, 8):
            em.opi("v", lambda e, s=s: e.tensor_tensor(
                out=mxs[:], in0=mxs[:], in1=sc_s[:, s * TT:(s + 1) * TT], op=ALU.max), "dve")
        em.vbar()
        for s in range(8):
            em.opi("v", lambda e, s=s: e.tensor_tensor(
                out=escr[:], in0=sc_s[:, s * TT:(s + 1) * TT], in1=mxs[:],
                op=ALU.subtract), "dve")
            eve = em.vbar()
            em.wait("sc", "dve", eve)
            ex = em.opi("sc", lambda e, s=s: e.activation(
                out=eb[:, s * TT:(s + 1) * TT], in_=escr[:], func=AF.Exp), "act")
            em.wait("v", "act", ex)
        em.opi("v", lambda e: e.tensor_tensor(out=rsum[:], in0=eb[:, 0:TT],
                                              in1=eb[:, TT:2 * TT], op=ALU.add), "dve")
        for s in range(2, 8):
            em.opi("v", lambda e, s=s: e.tensor_tensor(
                out=rsum[:], in0=rsum[:], in1=eb[:, s * TT:(s + 1) * TT], op=ALU.add), "dve")
        em.vbar()
        em.opi("v", lambda e: e.reciprocal(out=rsum[:], in_=rsum[:]), "dve")
        rs = em.vbar()
        v_free = [0, 0]
        for s in range(8):
            b = s % 2
            em.wait("sy", "pe", kv_free[b])
            em.wait("sy", "dve", v_free[b])
            vlb = em.opi("sy", lambda e, s=s, ct=ct, b=b: e.dma_start(
                out=vsb[b][:].rearrange("p (i t) -> p i t", t=TT),
                in_=kv_all.rearrange("(g p) t -> p g t", p=128)[
                    :, 16 * s + 8:16 * s + 16, ct * TT:(ct + 1) * TT]), "xld", 16)
            em.wait("t", "xld", vlb)
            em.wait("t", "dve", em.cntv("dve"))
            for i in range(ND):
                bb = i % 2
                em.wait("t", "dve", em.cntv("dve"))
                pb = em.opi("t", lambda e, s=s, i=i, bb=bb: e.matmul(
                    psB[bb][:, 0:TT], ehb16b[0:16, 128 * i:128 * (i + 1)],
                    eb[:, s * TT:(s + 1) * TT], start=True, stop=True), "pe")
                em.wait("v", "pe", pb)
                em.wait("v", "xld", vlb)
                if s == 0:
                    em.opi("v", lambda e, i=i, bb=bb, b=b: e.tensor_tensor(
                        out=ctx[:, i * TT:(i + 1) * TT], in0=psB[bb][:, 0:TT],
                        in1=vsb[b][:, i * TT:(i + 1) * TT], op=ALU.mult), "dve")
                else:
                    em.op("v", lambda e, i=i, bb=bb, b=b: e.tensor_tensor(
                        out=prod[:, 0:TT], in0=psB[bb][:, 0:TT],
                        in1=vsb[b][:, i * TT:(i + 1) * TT], op=ALU.mult))
                    em.opi("v", lambda e, i=i: e.tensor_tensor(
                        out=ctx[:, i * TT:(i + 1) * TT], in0=ctx[:, i * TT:(i + 1) * TT],
                        in1=prod[:, 0:TT], op=ALU.add), "dve")
            v_free[b] = em.cntv("dve")
        ctx_done = em.cntv("dve")
        em.wait("t", "dve", ctx_done)
        for i in range(ND):
            bb = i % 2
            em.wait("t", "dve", em.cntv("dve"))
            pb = em.opi("t", lambda e, i=i, bb=bb: e.matmul(
                psB[bb][:, 0:TT], ehb16f[0:16, 128 * i:128 * (i + 1)], rsum[:],
                start=True, stop=True), "pe")
            em.wait("v", "pe", pb)
            em.opi("v", lambda e, i=i, bb=bb: e.tensor_tensor(
                out=ctxb[:, i * TT:(i + 1) * TT], in0=ctx[:, i * TT:(i + 1) * TT],
                in1=psB[bb][:, 0:TT], op=ALU.mult), "dve")
        cn = em.cntv("dve")
        em.wait("t", "dve", cn)
        em.wait("v", "xld", x_ld)
        for j in range(ND):
            b = j % 2
            em.wait("t", "dve", em.cntv("dve"))
            for i in range(ND):
                fn = lambda e, j=j, b=b, i=i: e.matmul(
                    psQ[b][:, 0:TT], wo[:, j * 1024 + i * 128: j * 1024 + (i + 1) * 128],
                    ctxb[:, i * TT:(i + 1) * TT], start=(i == 0), stop=(i == ND - 1))
                pj = em.opi("t", fn, "pe") if i == ND - 1 else em.op("t", fn)
            em.wait("v", "pe", pj)
            em.op("v", lambda e, j=j, b=b: e.tensor_scalar(
                out=prod[:, 0:TT * 2].bitcast(F32),
                in0=psQ[b][:, 0:TT], scalar1=1.0, scalar2=boc[:, j:j + 1],
                op0=ALU.mult, op1=ALU.add))
            em.opi("v", lambda e, j=j: e.tensor_tensor(
                out=xt[:, j * TT:(j + 1) * TT], in0=xt[:, j * TT:(j + 1) * TT],
                in1=prod[:, 0:TT * 2].bitcast(F32), op=ALU.add), "dve")
        em.wait("sy", "dve", em.cntv("dve"))
        emit_store_x_tile(em, xt, xout_ap, ct)
    em.wait("sy", "xst", em.cntv("xst"))



def emit_l1_attn(em, C, xin_ap, xout_ap, wqkvp, bq_ap, bk_ap, bv_ap,
                 gam_ap, bet_ap, tag="l1a"):
    nc = em.nc
    kv_loc = nc.dram_tensor(tag + "_kvloc", [2048, S], BF16, kind="Internal").ap()
    kv_all = nc.dram_tensor(tag + "_kvall", [8 * 2048, S], BF16, kind="Internal").ap()
    qloc = nc.dram_tensor(tag + "_qloc", [D, S], BF16, kind="Internal").ap()

    wqkv = em.sb(tag + "_wqkv", [128, 24 * 1024], BF16)
    em.opi("gp", lambda e: e.dma_start(
        out=wqkv[:].rearrange("p (j kk c) -> p j kk c", j=24, c=256),
        in_=wqkvp.rearrange("j kk p c -> p j kk c")), "wcast", 16)
    wc = em.cntv("wcast")
    bqc = em.sb(tag + "_bq", [128, ND], F32)
    bkc = em.sb(tag + "_bk", [128, ND], F32)
    bvc = em.sb(tag + "_bv", [128, ND], F32)
    gmc = em.sb(tag + "_gm", [16, 1], F32)
    btc = em.sb(tag + "_bt", [16, 1], F32)
    for t_, a_ in ((bqc, bq_ap), (bkc, bk_ap), (bvc, bv_ap)):
        load_pp_vec(em, t_, a_, ND)
    em.opi("sy", lambda e: e.dma_start(
        out=gmc[:], in_=gam_ap.rearrange("(p o) -> p o", o=1)), "xld", 16)
    em.opi("sy", lambda e: e.dma_start(
        out=btc[:], in_=bet_ap.rearrange("(p o) -> p o", o=1)), "xld", 16)
    vld = em.cntv("xld")
    em.wait("v", "xld", vld)
    em.opi("v", lambda e: e.tensor_scalar(out=bqc[:], in0=bqc[:], scalar1=0.125,
                                          scalar2=None, op0=ALU.mult), "dve")
    em.vbar()
    ehb16f = load_pattern(em, C, "ehb16")
    ehq16b = em.sb(tag + "_ehqb", [128, 8 * 16], BF16)
    ehb16b = em.sb(tag + "_ehbb", [16, 8 * 128], BF16)
    em.wait("v", "c_ld", em.cntv("c_ld"))
    em.opi("v", lambda e: e.tensor_copy(out=ehq16b[:], in_=C["ehq16"][:]), "dve")
    em.opi("v", lambda e: e.tensor_copy(out=ehb16b[:], in_=ehb16f[:]), "dve")
    epr = em.cntv("dve")

    xt = _T(em.carve(90112, ND * TT, F32))
    xcb = _T(em.carve(0, ND * TT, BF16))
    qtile = _T(em.carve(8192, ND * TT, BF16))
    ksb = [_T(em.carve(16384 + 8192 * i, ND * TT, BF16)) for i in range(2)]
    vsb = ksb
    ctx = _T(em.carve(32768, ND * TT, F32))
    sc_s = _T(em.carve(49152, 8 * TT, F32)[0:16, :])
    pb16 = _T(em.carve(65536, 8 * TT, BF16)[0:16, :])
    prod = _T(em.carve(73728, ND * TT, BF16))
    pscr = _T(em.carve(81920, TT, F32)[0:16, :])
    pscr2 = _T(em.carve(84480 - 512, TT, F32)[0:16, :])
    psQ = [em.ps("ps_g0"), em.ps("ps_g1")]
    psS = em.ps("ps_u0")
    psB = [em.ps("ps_u1"), em.ps("ps_lnR")]

    em.wait("t", "wcast", wc)
    em.wait("t", "dve", epr)

    for ct in range(NTT):
        x_ld = emit_load_x_tile(em, xt, xin_ap, ct)
        em.wait("v", "xld", x_ld)
        em.wait("sc", "xld", x_ld)
        em.wait("v", "pe", em.cntv("pe"))
        emit_ln_tile(em, C, xt, xcb, f"{tag}_{ct}",
                     t1ap=em.carve(32768, ND * TT, F32),
                     xb16ap=em.carve(49152, ND * TT, BF16),
                     xsqbap=em.carve(57344, ND * TT, BF16))
        em.wait("t", "dve", em.cntv("dve"))
        for part in range(3):
            bias_t = (bqc, bkc, bvc)[part]
            ev = 0
            for j in range(ND):
                b = j % 2
                jj = part * ND + j
                em.wait("t", "act", em.cntv("act"))
                for i in range(ND):
                    fn = lambda e, jj=jj, b=b, i=i: e.matmul(
                        psQ[b][:, 0:TT], wqkv[:, jj * 1024 + i * 128: jj * 1024 + (i + 1) * 128],
                        xcb[:, i * TT:(i + 1) * TT], start=(i == 0), stop=(i == ND - 1))
                    pj = em.opi("t", fn, "pe") if i == ND - 1 else em.op("t", fn)
                em.wait("sc", "pe", pj)
                em.wait("sc", "xst", em.cntv("xst"))
                if part == 0:
                    ev = em.opi("sc", lambda e, j=j, b=b, bias_t=bias_t: e.activation(
                        out=qtile[:, j * TT:(j + 1) * TT], in_=psQ[b][:, 0:TT],
                        func=AF.Identity, scale=0.125, bias=bias_t[:, j:j + 1]), "act")
                else:
                    ev = em.opi("sc", lambda e, j=j, b=b, bias_t=bias_t: e.activation(
                        out=qtile[:, j * TT:(j + 1) * TT], in_=psQ[b][:, 0:TT],
                        func=AF.Identity, bias=bias_t[:, j:j + 1]), "act")
            em.wait("sy", "act", ev)
            if part == 0:
                dstap = qloc.rearrange("(i p) t -> p i t", p=128)[:, :, ct * TT:(ct + 1) * TT]
            elif part == 1:
                dstap = kv_loc.rearrange("(i p) t -> p i t", p=128)[:, 0:8, ct * TT:(ct + 1) * TT]
            else:
                dstap = kv_loc.rearrange("(i p) t -> p i t", p=128)[:, 8:16, ct * TT:(ct + 1) * TT]
            em.opi("sy", lambda e, dstap=dstap: e.dma_start(
                out=dstap, in_=qtile[:].rearrange("p (i t) -> p i t", t=TT)), "xst", 16)
    em.wait("gp", "xst", em.cntv("xst"))

    cc = em.opi("gp", lambda e: e.collective_compute(
        "AllGather", ALU.bypass, ins=[kv_loc.opt()], outs=[kv_all.opt()],
        replica_groups=[[0, 1, 2, 3, 4, 5, 6, 7]]), "cc", 1)
    em.wait("sy", "cc", cc)

    kv_free = [0, 0]
    for ct in range(NTT):
        if ct > 0:
            em.wait("sy", "xst", em.cntv("xst"))
        x_ld = emit_load_x_tile(em, xt, xin_ap, ct)
        ql = em.opi("sy", lambda e, ct=ct: e.dma_start(
            out=qtile[:].rearrange("p (i t) -> p i t", t=TT),
            in_=qloc.rearrange("(i p) t -> p i t", p=128)[:, :, ct * TT:(ct + 1) * TT]),
            "xld", 16)
        for s in range(8):
            b = s % 2
            em.wait("sy", "pe", kv_free[b])
            kl = em.opi("sy", lambda e, s=s, ct=ct, b=b: e.dma_start(
                out=ksb[b][:].rearrange("p (i t) -> p i t", t=TT),
                in_=kv_all.rearrange("(g p) t -> p g t", p=128)[
                    :, 16 * s:16 * s + 8, ct * TT:(ct + 1) * TT]), "xld", 16)
            em.wait("v", "xld", max(ql, kl))
            em.wait("v", "pe", em.cntv("pe"))
            for i in range(ND):
                em.opi("v", lambda e, i=i, b=b: e.tensor_tensor(
                    out=prod[:, i * TT:(i + 1) * TT], in0=qtile[:, i * TT:(i + 1) * TT],
                    in1=ksb[b][:, i * TT:(i + 1) * TT], op=ALU.mult), "dve")
            prd = em.cntv("dve")
            em.wait("t", "dve", prd)
            for i in range(ND):
                fn = lambda e, i=i: e.matmul(
                    psS[0:16, 0:TT], ehq16b[:, 16 * i:16 * (i + 1)],
                    prod[:, i * TT:(i + 1) * TT], start=(i == 0), stop=(i == ND - 1))
                pss = em.opi("t", fn, "pe") if i == ND - 1 else em.op("t", fn)
            kv_free[b] = pss
            em.wait("v", "pe", pss)
            em.opi("v", lambda e, s=s: e.tensor_copy(
                out=sc_s[:, s * TT:(s + 1) * TT], in_=psS[0:16, 0:TT]), "dve")
        em.vbar()
        for s in range(8):
            sl = slice(s * TT, (s + 1) * TT)
            em.op("v", lambda e, sl=sl: e.tensor_tensor(
                out=pscr[:], in0=sc_s[:, sl], in1=sc_s[:, sl], op=ALU.mult))
            em.op("v", lambda e: e.tensor_scalar(
                out=pscr2[:], in0=pscr[:], scalar1=1.0 / 480.0, scalar2=-1.0 / 48.0,
                op0=ALU.mult, op1=ALU.add))
            em.op("v", lambda e: e.tensor_tensor(
                out=pscr2[:], in0=pscr2[:], in1=pscr[:], op=ALU.mult))
            em.op("v", lambda e: e.tensor_scalar(
                out=pscr2[:], in0=pscr2[:], scalar1=1.0, scalar2=0.25,
                op0=ALU.mult, op1=ALU.add))
            em.op("v", lambda e, sl=sl: e.tensor_tensor(
                out=pscr2[:], in0=pscr2[:], in1=sc_s[:, sl], op=ALU.mult))
            em.op("v", lambda e: e.tensor_scalar(
                out=pscr2[:], in0=pscr2[:], scalar1=1.0, scalar2=0.5,
                op0=ALU.mult, op1=ALU.add))
            em.opi("v", lambda e, sl=sl: e.tensor_scalar(
                out=pb16[:, sl], in0=pscr2[:], scalar1=gmc[:], scalar2=btc[:],
                op0=ALU.mult, op1=ALU.add), "dve")
        em.vbar()
        v_free = [0, 0]
        for s in range(8):
            b = s % 2
            em.wait("sy", "pe", kv_free[b])
            em.wait("sy", "dve", v_free[b])
            vlb = em.opi("sy", lambda e, s=s, ct=ct, b=b: e.dma_start(
                out=vsb[b][:].rearrange("p (i t) -> p i t", t=TT),
                in_=kv_all.rearrange("(g p) t -> p g t", p=128)[
                    :, 16 * s + 8:16 * s + 16, ct * TT:(ct + 1) * TT]), "xld", 16)
            em.wait("t", "xld", vlb)
            em.wait("t", "dve", em.cntv("dve"))
            for i in range(ND):
                bb = i % 2
                em.wait("t", "dve", em.cntv("dve"))
                pb = em.opi("t", lambda e, s=s, i=i, bb=bb: e.matmul(
                    psB[bb][:, 0:TT], ehb16b[0:16, 128 * i:128 * (i + 1)],
                    pb16[:, s * TT:(s + 1) * TT], start=True, stop=True), "pe")
                em.wait("v", "pe", pb)
                em.wait("v", "xld", vlb)
                if s == 0:
                    em.opi("v", lambda e, i=i, bb=bb, b=b: e.tensor_tensor(
                        out=ctx[:, i * TT:(i + 1) * TT], in0=psB[bb][:, 0:TT],
                        in1=vsb[b][:, i * TT:(i + 1) * TT], op=ALU.mult), "dve")
                else:
                    em.op("v", lambda e, i=i, bb=bb, b=b: e.tensor_tensor(
                        out=prod[:, 0:TT], in0=psB[bb][:, 0:TT],
                        in1=vsb[b][:, i * TT:(i + 1) * TT], op=ALU.mult))
                    em.opi("v", lambda e, i=i: e.tensor_tensor(
                        out=ctx[:, i * TT:(i + 1) * TT], in0=ctx[:, i * TT:(i + 1) * TT],
                        in1=prod[:, 0:TT], op=ALU.add), "dve")
            v_free[b] = em.cntv("dve")
        em.wait("v", "xld", x_ld)
        for j in range(ND):
            em.opi("v", lambda e, j=j: e.tensor_tensor(
                out=xt[:, j * TT:(j + 1) * TT], in0=xt[:, j * TT:(j + 1) * TT],
                in1=ctx[:, j * TT:(j + 1) * TT], op=ALU.add), "dve")
        em.wait("sy", "dve", em.cntv("dve"))
        emit_store_x_tile(em, xt, xout_ap, ct)
    em.wait("sy", "xst", em.cntv("xst"))



def emit_l2_attn(em, C, xin_ap, xout_ap, wqp, wkp, wvp, wop,
                 bq_ap, bk_ap, bv_ap, bo_ap, cos_ap, sin_ap, tag="l2a", dbg=None):
    nc = em.nc
    NCH = TT // 128
    wbq = [em.sb(tag + f"_wbq{i}", [128, 8 * 128], BF16) for i in range(2)]
    wbk = [em.sb(tag + f"_wbk{i}", [128, 8 * 128], BF16) for i in range(2)]
    wbv = [em.sb(tag + f"_wbv{i}", [128, 8 * 128], BF16) for i in range(2)]
    wqb_d = nc.dram_tensor(tag + "_wqb", [8, 4, 128, 256], BF16, kind="Internal").ap()
    wkb_d = nc.dram_tensor(tag + "_wkb", [8, 4, 128, 256], BF16, kind="Internal").ap()
    wvb_d = nc.dram_tensor(tag + "_wvb", [8, 4, 128, 256], BF16, kind="Internal").ap()
    wob_d = nc.dram_tensor(tag + "_wob", [8, 4, 128, 256], BF16, kind="Internal").ap()
    for dst, srcw in ((wqb_d, wqp), (wkb_d, wkp), (wvb_d, wvp), (wob_d, wop)):
        em.opi("gp", lambda e, d=dst, s=srcw: e.dma_start(out=d, in_=s), "wcast", 16)
    em.wait("sy", "wcast", em.cntv("wcast"))
    bqc = em.sb(tag + "_bq", [128, ND], F32)
    bkc = em.sb(tag + "_bk", [128, ND], F32)
    bvc = em.sb(tag + "_bv", [128, ND], F32)
    boc = em.sb(tag + "_bo", [128, ND], F32)
    for t_, a_ in ((bqc, bq_ap), (bkc, bk_ap), (bvc, bv_ap), (boc, bo_ap)):
        load_pp_vec(em, t_, a_, ND)
    onec = em.sb(tag + "_onec", [128, 1], BF16)
    trib = C["tri"]
    em.wait("v", "c_ld", em.cntv("c_ld"))
    em.wait("v", "c_ldg", em.cntv("c_ldg"))
    em.opi("v", lambda e: e.tensor_copy(out=onec[:], in_=C["ones_col1"][:]), "dve")
    cpr = em.cntv("dve")

    xt = em.sb(tag + "_xt", [128, ND * TT], F32)
    xcb = _T(em.carve(0, ND * TT, BF16))
    Qc = _T(em.carve(8192, ND * TT, BF16))
    Qs = _T(em.carve(16384, ND * TT, BF16))
    Kc = _T(em.carve(24576, ND * TT, BF16))
    Ks = _T(em.carve(32768, ND * TT, BF16))
    Vb = _T(em.carve(40960, ND * TT, BF16))
    outf = _T(em.carve(49152, ND * TT, BF16))
    qp_scr = _T(em.carve(57344, TT, F32))
    costl = _T(em.carve(59392, TT, F32))
    sintl = _T(em.carve(61440, TT, F32))
    KcT = em.sb(tag + "_kct", [128, 8 * 128], BF16)
    KsT = em.sb(tag + "_kst", [128, 8 * 128], BF16)
    VT1 = em.sb(tag + "_vt1", [128, 16 * 65], BF16)
    Pm = em.sb(tag + "_pm", [128, 128], BF16)
    scc = em.sb(tag + "_scc", [128, 8 * 65], BF16)
    scs = em.sb(tag + "_scs2", [128, 8 * 65], BF16)
    sfc = em.sb(tag + "_sfc", [128, 512], F32)
    sfs = em.sb(tag + "_sfs", [128, 512], F32)
    zf = em.sb(tag + "_zf", [128, 16], F32)
    out_tm = em.sb(tag + "_otm", [128, 1024], BF16)
    dscr = em.sb(tag + "_dscr", [128, 1], F32)
    psQ = [em.ps("ps_g0"), em.ps("ps_g1")]
    psO = em.ps("ps_u0")
    psP = em.ps("ps_u1")
    SCOS = em.ps("ps_lnS")
    SSIN = em.ps("ps_lnB")
    ZB = em.ps("ps_lnR")

    em.opi("gp", lambda e: e.memset(sfc[:], 0.0), "gpd", 1)
    em.opi("gp", lambda e: e.memset(sfs[:], 0.0), "gpd", 1)
    em.opi("gp", lambda e: e.memset(zf[:], 0.0), "gpd", 1)
    em.wait("v", "gpd", em.cntv("gpd"))
    em.wait("v", "dve", cpr)
    for h in range(16):
        em.opi("v", lambda e, h=h: e.tensor_copy(
            out=VT1[:, 65 * h + 64:65 * h + 65], in_=onec[:]), "dve")
    em.vbar()

    wfree = {0: 0, 1: 0}
    for tt in range(NTT):
        if tt > 0:
            em.wait("sy", "xst", em.cntv("xst"))
        x_ld = emit_load_x_tile(em, xt, xin_ap, tt)
        cl = em.opi("sy", lambda e, tt=tt: e.dma_start(
            out=costl[:], in_=cos_ap[:, tt * TT:(tt + 1) * TT]), "xld", 16)
        sl_ = em.opi("sy", lambda e, tt=tt: e.dma_start(
            out=sintl[:], in_=sin_ap[:, tt * TT:(tt + 1) * TT]), "xld", 16)
        em.wait("v", "xld", max(x_ld, sl_))
        em.wait("sc", "xld", max(x_ld, sl_))
        em.wait("v", "pe", em.cntv("pe"))
        emit_ln_tile(em, C, xt, xcb, f"{tag}_{tt}",
                     t1ap=em.carve(8192, ND * TT, F32),
                     xb16ap=em.carve(24576, ND * TT, BF16),
                     xsqbap=em.carve(32768, ND * TT, BF16),
                     ln_banks=("ps_u0", "ps_g0", "ps_g1"))
        em.wait("t", "dve", em.cntv("dve"))

        if dbg is not None and tt == 0:
            pass
        for part, (wdram, wbufs, bias_t) in enumerate(
                ((wqb_d, wbq, bqc), (wkb_d, wbk, bkc), (wvb_d, wbv, bvc))):
            for j in range(ND):
                b = j % 2
                em.wait("sy", "pe", wfree[b])
                wl = em.opi("sy", lambda e, wdram=wdram, wbufs=wbufs, j=j, b=b: e.dma_start(
                    out=wbufs[b][:].rearrange("p (k o) -> p k o", o=256),
                    in_=wdram[j].rearrange("k p o -> p k o")), "wld", 16)
                em.wait("t", "wld", wl)
                em.wait("t", "act", em.cntv("act"))
                em.wait("t", "dve", em.cntv("dve"))
                for i in range(ND):
                    fn = lambda e, wbufs=wbufs, j=j, b=b, i=i: e.matmul(
                        psQ[b][:, 0:TT], wbufs[b][:, i * 128:(i + 1) * 128],
                        xcb[:, i * TT:(i + 1) * TT], start=(i == 0), stop=(i == ND - 1))
                    pj = em.opi("t", fn, "pe") if i == ND - 1 else em.op("t", fn)
                wfree[b] = pj
                em.wait("sc", "pe", pj)
                if part < 2:
                    em.wait("sc", "dve", em.cntv("dve"))
                    ev = em.opi("sc", lambda e, j=j, b=b, bias_t=bias_t: e.activation(
                        out=qp_scr[:], in_=psQ[b][:, 0:TT], func=AF.Relu,
                        bias=bias_t[:, j:j + 1]), "act")
                    em.wait("v", "act", ev)
                    dc = Qc if part == 0 else Kc
                    ds = Qs if part == 0 else Ks
                    em.op("v", lambda e, dc=dc, j=j: e.tensor_tensor(
                        out=dc[:, j * TT:(j + 1) * TT], in0=qp_scr[:], in1=costl[:],
                        op=ALU.mult))
                    em.opi("v", lambda e, ds=ds, j=j: e.tensor_tensor(
                        out=ds[:, j * TT:(j + 1) * TT], in0=qp_scr[:], in1=sintl[:],
                        op=ALU.mult), "dve")
                else:
                    ev = em.opi("sc", lambda e, j=j, b=b, bias_t=bias_t: e.activation(
                        out=Vb[:, j * TT:(j + 1) * TT], in_=psQ[b][:, 0:TT],
                        func=AF.Identity, bias=bias_t[:, j:j + 1]), "act")
        proj_done_v = em.cntv("dve")
        proj_done_a = em.cntv("act")
        if dbg is not None and tt == 0:
            em.wait("sy", "dve", proj_done_v)
            em.wait("sy", "act", proj_done_a)
            em.opi("sy", lambda e: e.dma_start(out=dbg["qc"], in_=Qc[:]), "xst", 16)
            em.opi("sy", lambda e: e.dma_start(out=dbg["kc"], in_=Kc[:]), "xst", 16)
            em.opi("sy", lambda e: e.dma_start(out=dbg["vb"], in_=Vb[:]), "xst", 16)

        for c in range(NCH):
            csl = lambda i, c=c: slice(i * TT + 128 * c, i * TT + 128 * (c + 1))
            em.wait("t", "dve", em.cntv("dve"))
            em.wait("t", "act", proj_done_a)
            for pr in range(ND):
                for (srcb, dstb, dstoff) in ((Kc, KcT, 128 * pr), (Ks, KsT, 128 * pr)):
                    em.wait("t", "dve", em.cntv("dve"))
                    pt = em.opi("t", lambda e, srcb=srcb, pr=pr, csl=csl: e.transpose(
                        psP[:].bitcast(BF16)[:, 0:128], srcb[:, csl(pr)],
                        C["ident_bf16"][:]), "pe")
                    em.wait("v", "pe", pt)
                    em.opi("v", lambda e, dstb=dstb, dstoff=dstoff: e.tensor_copy(
                        out=dstb[:, dstoff:dstoff + 128],
                        in_=psP[:].bitcast(BF16)[:, 0:128]), "dve")
                em.wait("t", "dve", em.cntv("dve"))
                pt = em.opi("t", lambda e, pr=pr, csl=csl: e.transpose(
                    psP[:].bitcast(BF16)[:, 0:128], Vb[:, csl(pr)],
                    C["ident_bf16"][:]), "pe")
                em.wait("v", "pe", pt)
                for hh in range(2):
                    em.opi("v", lambda e, pr=pr, hh=hh: e.tensor_copy(
                        out=VT1[:, 65 * (2 * pr + hh):65 * (2 * pr + hh) + 64],
                        in_=psP[:].bitcast(BF16)[:, 64 * hh:64 * hh + 64]), "dve")
            em.vbar()
            tr_done = em.cntv("dve")
            if tt == 0 and c == 0:
                em.opi("gp", lambda e: e.memset(scc[:], 0.0), "gpd", 1)
                em.opi("gp", lambda e: e.memset(scs[:], 0.0), "gpd", 1)
                gpd = em.cntv("gpd")
                em.wait("t", "gpd", gpd)
                em.wait("v", "gpd", gpd)
            else:
                em.wait("v", "pe", em.cntv("pe"))
                em.opi("v", lambda e: e.tensor_tensor(
                    out=sfc[:], in0=sfc[:], in1=SCOS[:], op=ALU.add), "dve")
                em.opi("v", lambda e: e.tensor_tensor(
                    out=sfs[:], in0=sfs[:], in1=SSIN[:], op=ALU.add), "dve")
                em.opi("v", lambda e: e.tensor_tensor(
                    out=zf[:], in0=zf[:], in1=ZB[:, 0:16], op=ALU.add), "dve")
                em.vbar()
                em.opi("v", lambda e: e.tensor_copy(
                    out=scc[:].rearrange("p (a b) -> p a b", b=65)[:, :, 0:64],
                    in_=sfc[:].rearrange("p (a b) -> p a b", b=64)), "dve")
                em.opi("v", lambda e: e.tensor_copy(
                    out=scc[:].rearrange("p (a b) -> p a b", b=65)[:, :, 64:65],
                    in_=zf[:, 0:8].rearrange("p (a b) -> p a b", b=1)), "dve")
                em.opi("v", lambda e: e.tensor_copy(
                    out=scs[:].rearrange("p (a b) -> p a b", b=65)[:, :, 0:64],
                    in_=sfs[:].rearrange("p (a b) -> p a b", b=64)), "dve")
                em.opi("v", lambda e: e.tensor_copy(
                    out=scs[:].rearrange("p (a b) -> p a b", b=65)[:, :, 64:65],
                    in_=zf[:, 8:16].rearrange("p (a b) -> p a b", b=1)), "dve")
            em.vbar()
            sev = em.cntv("dve")
            em.wait("t", "dve", sev)
            if dbg is not None and tt == 0 and c == 0:
                em.wait("sy", "dve", sev)
                em.opi("sy", lambda e: e.dma_start(out=dbg["vt1"], in_=VT1[:]), "xst", 16)
                em.opi("sy", lambda e: e.dma_start(out=dbg["kct"], in_=KcT[:]), "xst", 16)
                for ek in ("v", "t", "sc", "gp"):
                    em.wait(ek, "xst", em.cntv("xst"))
            if dbg is not None and tt == 0 and c == 1:
                em.wait("sy", "dve", sev)
                em.opi("sy", lambda e: e.dma_start(out=dbg["scc"], in_=scc[:]), "xst", 16)
                em.opi("sy", lambda e: e.dma_start(out=dbg["scs"], in_=scs[:]), "xst", 16)
                for ek in ("v", "t", "sc", "gp"):
                    em.wait(ek, "xst", em.cntv("xst"))

            for h in range(16):
                pr = h // 2
                rb = 64 * (h % 2)
                em.wait("t", "dve", em.cntv("dve"))
                em.op("t", lambda e, pr=pr, rb=rb, csl=csl: e.matmul(
                    psP[:, 0:128], Kc[:, csl(pr)][rb:rb + 64, :], Qc[:, csl(pr)][rb:rb + 64, :],
                    start=True, stop=False))
                pp = em.opi("t", lambda e, pr=pr, rb=rb, csl=csl: e.matmul(
                    psP[:, 0:128], Ks[:, csl(pr)][rb:rb + 64, :], Qs[:, csl(pr)][rb:rb + 64, :],
                    start=False, stop=True), "pe")
                em.wait("v", "pe", pp)
                em.opi("v", lambda e: e.tensor_tensor(
                    out=Pm[:], in0=psP[:, 0:128], in1=trib[:], op=ALU.mult), "dve")
                pm = em.vbar()
                em.wait("t", "dve", pm)
                em.op("t", lambda e, h=h: e.matmul(
                    psO[:, 0:65], Pm[:], VT1[:, 65 * h:65 * h + 65],
                    start=True, stop=False))
                em.op("t", lambda e, h=h, pr=pr, rb=rb, csl=csl: e.matmul(
                    psO[:, 0:65], Qc[:, csl(pr)][rb:rb + 64, :], scc[rb:rb + 64, 65 * pr:65 * pr + 65],
                    start=False, stop=False))
                po = em.opi("t", lambda e, h=h, pr=pr, rb=rb, csl=csl: e.matmul(
                    psO[:, 0:65], Qs[:, csl(pr)][rb:rb + 64, :], scs[rb:rb + 64, 65 * pr:65 * pr + 65],
                    start=False, stop=True), "pe")
                em.wait("v", "pe", po)
                em.opi("v", lambda e: e.tensor_scalar(
                    out=dscr[:], in0=psO[:, 64:65], scalar1=1e-6, scalar2=None,
                    op0=ALU.max), "dve")
                em.vbar()
                em.opi("v", lambda e: e.reciprocal(out=dscr[:], in_=dscr[:]), "dve")
                em.vbar()
                em.opi("v", lambda e, h=h: e.tensor_scalar(
                    out=out_tm[:, 64 * h:64 * (h + 1)], in0=psO[:, 0:64],
                    scalar1=dscr[:], scalar2=None, op0=ALU.mult), "dve")
                ctx_ev = em.cntv("dve")
                em.wait("t", "dve", ctx_ev)
                em.op("t", lambda e, h=h, pr=pr, rb=rb: e.matmul(
                    SCOS[rb:rb + 64, 64 * pr:64 * pr + 64], KcT[:, 128 * pr + rb:128 * pr + rb + 64],
                    VT1[:, 65 * h:65 * h + 64], start=True, stop=True))
                em.op("t", lambda e, h=h, pr=pr, rb=rb: e.matmul(
                    ZB[rb:rb + 64, pr:pr + 1], KcT[:, 128 * pr + rb:128 * pr + rb + 64],
                    onec[:], start=True, stop=True))
                em.op("t", lambda e, h=h, pr=pr, rb=rb: e.matmul(
                    SSIN[rb:rb + 64, 64 * pr:64 * pr + 64], KsT[:, 128 * pr + rb:128 * pr + rb + 64],
                    VT1[:, 65 * h:65 * h + 64], start=True, stop=True))
                em.opi("t", lambda e, h=h, pr=pr, rb=rb: e.matmul(
                    ZB[rb:rb + 64, 8 + pr:8 + pr + 1], KsT[:, 128 * pr + rb:128 * pr + rb + 64],
                    onec[:], start=True, stop=True), "pe")
            em.vbar()
            otm = em.cntv("dve")
            if dbg is not None and tt == 0:
                em.wait("sy", "dve", otm)
                em.opi("sy", lambda e, c=c: e.dma_start(
                    out=dbg["otm"][:, 1024 * c:1024 * (c + 1)], in_=out_tm[:]), "xst", 16)
                for ek in ("v", "t", "sc", "gp"):
                    em.wait(ek, "xst", em.cntv("xst"))
            em.wait("t", "dve", otm)
            for i in range(ND):
                em.wait("t", "dve", em.cntv("dve"))
                pt = em.opi("t", lambda e, i=i: e.transpose(
                    psP[:].bitcast(BF16)[:, 0:128], out_tm[:, 128 * i:128 * (i + 1)],
                    C["ident_bf16"][:]), "pe")
                em.wait("v", "pe", pt)
                em.opi("v", lambda e, i=i, csl=csl: e.tensor_copy(
                    out=outf[:, csl(i)], in_=psP[:].bitcast(BF16)[:, 0:128]), "dve")

        out_done = em.cntv("dve")
        em.wait("t", "dve", out_done)
        for j in range(ND):
            b = j % 2
            em.wait("sy", "pe", wfree[b])
            wl = em.opi("sy", lambda e, j=j, b=b: e.dma_start(
                out=wbq[b][:].rearrange("p (k o) -> p k o", o=256),
                in_=wob_d[j].rearrange("k p o -> p k o")), "wld", 16)
            em.wait("t", "wld", wl)
            em.wait("t", "dve", em.cntv("dve"))
            for i in range(ND):
                fn = lambda e, j=j, b=b, i=i: e.matmul(
                    psQ[b][:, 0:TT], wbq[b][:, i * 128:(i + 1) * 128],
                    outf[:, i * TT:(i + 1) * TT], start=(i == 0), stop=(i == ND - 1))
                pj = em.opi("t", fn, "pe") if i == ND - 1 else em.op("t", fn)
            wfree[b] = pj
            em.wait("v", "pe", pj)
            em.op("v", lambda e, j=j, b=b: e.tensor_scalar(
                out=qp_scr[:], in0=psQ[b][:, 0:TT], scalar1=1.0,
                scalar2=boc[:, j:j + 1], op0=ALU.mult, op1=ALU.add))
            em.opi("v", lambda e, j=j: e.tensor_tensor(
                out=xt[:, j * TT:(j + 1) * TT], in0=xt[:, j * TT:(j + 1) * TT],
                in1=qp_scr[:], op=ALU.add), "dve")
        em.wait("sy", "dve", em.cntv("dve"))
        emit_store_x_tile(em, xt, xout_ap, tt)
    em.wait("sy", "xst", em.cntv("xst"))



def emit_round_e2m1(em, xs, q, scr, tag):
    r0, sgn, m1, corr = scr["r0"], scr["a"], scr["r1"], scr["r2"]
    em.op("v", lambda e: e.tensor_scalar(out=r0[:], in0=xs, scalar1=2.0, scalar2=-2.0,
                                         op0=ALU.min, op1=ALU.max))
    em.op("v", lambda e: e.tensor_scalar(out=r0[:], in0=r0[:], scalar1=2.0, scalar2=MAGIC,
                                         op0=ALU.mult, op1=ALU.add))
    em.op("v", lambda e: e.tensor_scalar(out=r0[:], in0=r0[:], scalar1=MAGIC, scalar2=0.5,
                                         op0=ALU.subtract, op1=ALU.mult))
    em.wait("sc", "dve", em.cntv("dve"))
    sg = em.opi("sc", lambda e: e.activation(out=sgn[:], in_=xs, func=AF.Sign), "act")
    em.op("v", lambda e: e.tensor_scalar(out=m1[:], in0=xs, scalar1=2.5, scalar2=None,
                                         op0=ALU.is_gt))
    em.op("v", lambda e: e.tensor_scalar(out=corr[:], in0=xs, scalar1=-2.5, scalar2=None,
                                         op0=ALU.is_lt))
    em.op("v", lambda e: e.tensor_tensor(out=corr[:], in0=corr[:], in1=m1[:], op=ALU.add))
    em.op("v", lambda e: e.tensor_scalar(out=m1[:], in0=xs, scalar1=3.5, scalar2=None,
                                         op0=ALU.is_gt))
    em.op("v", lambda e: e.tensor_tensor(out=corr[:], in0=corr[:], in1=m1[:], op=ALU.add))
    em.op("v", lambda e: e.tensor_scalar(out=m1[:], in0=xs, scalar1=-3.5, scalar2=None,
                                         op0=ALU.is_lt))
    em.op("v", lambda e: e.tensor_tensor(out=corr[:], in0=corr[:], in1=m1[:], op=ALU.add))
    em.op("v", lambda e: e.tensor_scalar(out=m1[:], in0=xs, scalar1=5.0, scalar2=None,
                                         op0=ALU.is_gt))
    em.op("v", lambda e: e.scalar_tensor_tensor(out=corr[:], in0=m1[:], scalar=2.0,
                                                in1=corr[:], op0=ALU.mult, op1=ALU.add))
    em.op("v", lambda e: e.tensor_scalar(out=m1[:], in0=xs, scalar1=-5.0, scalar2=None,
                                         op0=ALU.is_lt))
    em.op("v", lambda e: e.scalar_tensor_tensor(out=corr[:], in0=m1[:], scalar=2.0,
                                                in1=corr[:], op0=ALU.mult, op1=ALU.add))
    em.wait("v", "act", sg)
    em.op("v", lambda e: e.tensor_tensor(out=corr[:], in0=corr[:], in1=sgn[:], op=ALU.mult))
    em.op("v", lambda e: e.tensor_tensor(out=q[:], in0=r0[:], in1=corr[:], op=ALU.add))



def emit_mx_scales(em, C, src_buf, n_pt, tag, scl_tensors):
    BPP = 8
    nblk = n_pt * BPP
    amax = em.sb(tag + "_amax", [128, 4 * nblk], F32)
    sclr = em.sb(tag + "_sclr", [128, nblk], F32)
    psT = em.ps("ps_g0")
    psb = em.ps("ps_g1")
    for s in range(4):
        for i in range(n_pt):
            em.wait("t", "dve", em.cntv("dve"))
            pet = em.opi("t", lambda e, s=s, i=i: e.transpose(
                psT[:, 0:128], src_buf[:, i * TT + 128 * s: i * TT + 128 * (s + 1)],
                C["ident_f32"][:]), "pe")
            em.wait("v", "pe", pet)
            em.opi("v", lambda e, s=s, i=i: e.tensor_reduce(
                out=amax[:, nblk * s + BPP * i: nblk * s + BPP * (i + 1)].rearrange(
                    "p (a b) -> p a b", b=1),
                in_=psT[:, 0:128].rearrange("p (a b) -> p a b", b=16),
                axis=AX.X, op=ALU.max, apply_absolute_value=True), "dve")
        em.vbar()
        em.op("v", lambda e, s=s: e.tensor_scalar(
            out=amax[:, nblk * s:nblk * (s + 1)], in0=amax[:, nblk * s:nblk * (s + 1)],
            scalar1=1.0 / 6.0, scalar2=1e-12, op0=ALU.mult, op1=ALU.max))
        em.vbar()
        em.opi("v", lambda e, s=s: e.reciprocal(
            out=sclr[:], in_=amax[:, nblk * s:nblk * (s + 1)]), "dve")
        rc = em.vbar()
        for (t_r, t_s, row0, nrows) in scl_tensors:
            em.wait("t", "dve", em.cntv("dve"))
            p1 = em.opi("t", lambda e, s=s, row0=row0, nrows=nrows: e.transpose(
                psb[0:nrows, 0:128], sclr[:, row0:row0 + nrows],
                C["ident_f32"][:]), "pe")
            em.wait("v", "pe", p1)
            c1 = em.opi("v", lambda e, s=s, t_r=t_r, nrows=nrows: e.tensor_copy(
                out=t_r[0:nrows, 128 * s:128 * (s + 1)], in_=psb[0:nrows, 0:128]),
                "dve")
            em.wait("t", "dve", c1)
            p2 = em.opi("t", lambda e, s=s, row0=row0, nrows=nrows: e.transpose(
                psb[0:nrows, 0:128],
                amax[:, nblk * s + row0: nblk * s + row0 + nrows],
                C["ident_f32"][:]), "pe")
            em.wait("v", "pe", p2)
            em.opi("v", lambda e, s=s, t_s=t_s, nrows=nrows: e.tensor_copy(
                out=t_s[0:nrows, 128 * s:128 * (s + 1)], in_=psb[0:nrows, 0:128]),
                "dve")
    return em.cntv("dve")



def emit_l2_ffn(em, C, xin_ap, xout_ap, wg_nat, wu_nat, wd_nat, tag="l2", wstate=None):
    nc = em.nc
    I = 2816
    nJ = I // 128
    skip_wq = wstate is not None and "wgT" in wstate
    if skip_wq:
        wgT, wuT, wdT = wstate["wgT"], wstate["wuT"], wstate["wdT"]
    else:
        wgT = nc.dram_tensor(tag + "_wgT", [nJ, ND, 128, 128], BF16, kind="Internal").ap()
        wuT = nc.dram_tensor(tag + "_wuT", [nJ, ND, 128, 128], BF16, kind="Internal").ap()
        wdT = nc.dram_tensor(tag + "_wdT", [ND, nJ, 128, 128], BF16, kind="Internal").ap()
        if wstate is not None:
            wstate.update(wgT=wgT, wuT=wuT, wdT=wdT)

    xt = em.sb(tag + "_xt", [128, ND * TT], F32)
    xcf_ap = em.carve(0, ND * TT, F32)
    xq_ap = em.carve(16384, ND * TT, BF16)
    hf_ap = em.carve(24576, nJ * TT, F32)
    hq_ap = em.carve(69632, nJ * TT, BF16)
    scr = {}
    for kk, koff in (("r0", 92160), ("r1", 96256), ("r2", 100352), ("a", 104448),
                     ("xs", 108544), ("q", 112640)):
        scr[kk] = _T(em.carve(koff, D, F32))
    t1_ap = em.carve(40960, ND * TT, F32)
    xcf = _T(xcf_ap); xq = _T(xq_ap); hf = _T(hf_ap); hq = _T(hq_ap)
    asil = [em.sb(tag + f"_as{i}", [128, TT], F32) for i in range(2)]
    sxr = em.sb(tag + "_sxr", [64, TT], F32)
    sxs = em.sb(tag + "_sxs", [64, TT], F32)
    shra = em.sb(tag + "_shra", [128, TT], F32)
    shsa = em.sb(tag + "_shsa", [128, TT], F32)
    shrb = em.sb(tag + "_shrb", [48, TT], F32)
    shsb = em.sb(tag + "_shsb", [48, TT], F32)
    wbufA = [em.sb(tag + f"_wA{i}", [128, ND * 128], BF16) for i in range(2)]
    wbufB = [em.sb(tag + f"_wB{i}", [128, ND * 128], BF16) for i in range(2)]
    wbufD = [em.sb(tag + f"_wD{i}", [128, nJ * 128], BF16) for i in range(2)]
    wstg = em.sb(tag + "_wstg", [128, 128], BF16)
    psA = [em.ps("ps_g0"), em.ps("ps_g1")]
    psB = [em.ps("ps_u0"), em.ps("ps_u1")]

    e16xt = load_pattern(em, C, "e16x")
    e176at = load_pattern(em, C, "e176a")
    e176bt = load_pattern(em, C, "e176b")

    def quant_weight(nat_ap, O, K, outT, n_ot, n_kt, wtag):
        wch = _T(em.carve(24576, 4096, F32))
        wqb = _T(em.carve(40960, 4096, BF16))
        wsc = em.sb(tag + "_wsc", [128, 256], F32)
        wscr = em.sb(tag + "_wscr", [128, 256], F32)
        for j in range(n_ot):
            em.wait("sy", "dve", em.cntv("dve"))
            em.wait("sy", "pe", em.cntv("pe"))
            wl = em.opi("sy", lambda e, j=j: e.dma_start(
                out=wch[:, 0:K], in_=nat_ap[128 * j:128 * (j + 1), :]), "xld", 16)
            em.wait("v", "xld", wl)
            nb = K // 16
            em.opi("v", lambda e, nb=nb: e.tensor_reduce(
                out=wsc[:, 0:nb].rearrange("p (a b) -> p a b", b=1),
                in_=wch[:, 0:K].rearrange("p (a b) -> p a b", b=16),
                axis=AX.X, op=ALU.max, apply_absolute_value=True), "dve")
            em.vbar()
            em.op("v", lambda e, nb=nb: e.tensor_scalar(
                out=wsc[:, 0:nb], in0=wsc[:, 0:nb], scalar1=1.0 / 6.0, scalar2=1e-12,
                op0=ALU.mult, op1=ALU.max))
            em.vbar()
            em.op("v", lambda e, nb=nb: e.reciprocal(out=wscr[:, 0:nb], in_=wsc[:, 0:nb]))
            em.vbar()
            for c0 in range(0, K, 1024):
                cw = min(1024, K - c0)
                nbp = cw // 16
                em.opi("v", lambda e, c0=c0, cw=cw, nbp=nbp: e.tensor_tensor(
                    out=scr["xs"][:, 0:cw].rearrange("p (a b) -> p a b", b=16),
                    in0=wch[:, c0:c0 + cw].rearrange("p (a b) -> p a b", b=16),
                    in1=wscr[:, c0 // 16:c0 // 16 + nbp].rearrange(
                        "p (a b) -> p a b", b=1).broadcast_to((128, nbp, 16)),
                    op=ALU.mult), "dve")
                emit_round_e2m1(em, scr["xs"][:, 0:cw], SubView(scr["q"], cw), 
                                {k: SubView(scr[k], cw) for k in scr}, wtag)
                em.opi("v", lambda e, c0=c0, cw=cw, nbp=nbp: e.tensor_tensor(
                    out=wqb[:, c0:c0 + cw].rearrange("p (a b) -> p a b", b=16),
                    in0=scr["q"][:, 0:cw].rearrange("p (a b) -> p a b", b=16),
                    in1=wsc[:, c0 // 16:c0 // 16 + nbp].rearrange(
                        "p (a b) -> p a b", b=1).broadcast_to((128, nbp, 16)),
                    op=ALU.mult), "dve")
            dq = em.cntv("dve")
            em.wait("t", "dve", dq)
            for ki in range(n_kt):
                em.wait("t", "xst", em.cntv("xst"))
                pet = em.opi("t", lambda e, ki=ki: e.transpose(
                    psA[0][:].bitcast(BF16)[:, 0:128],
                    wqb[:, 128 * ki:128 * (ki + 1)], C["ident_bf16"][:]), "pe")
                em.wait("v", "pe", pet)
                em.wait("v", "xst", em.cntv("xst"))
                cstg = em.opi("v", lambda e: e.tensor_copy(
                    out=wstg[:], in_=psA[0][:].bitcast(BF16)[:, 0:128]), "dve")
                em.wait("sy", "dve", cstg)
                em.opi("sy", lambda e, j=j, ki=ki, outT=outT: e.dma_start(
                    out=outT[j, ki], in_=wstg[:]), "xst", 16)

    if not skip_wq:
        quant_weight(wg_nat, I, D, wgT, nJ, ND, tag + "_qg")
        quant_weight(wu_nat, I, D, wuT, nJ, ND, tag + "_qu")
        quant_weight(wd_nat, D, I, wdT, ND, nJ, tag + "_qd")
        em.wait("sy", "xst", em.cntv("xst"))

    wA_free = [0, 0]; wB_free = [0, 0]; wD_free = [0, 0]
    psA_ev = [em.cntv("act"), em.cntv("act")]
    psB_ev = [em.cntv("dve"), em.cntv("dve")]
    a_free = [0, 0]

    def mm_group(bank, wbuf, rhs_buf, nK):
        last = None
        for i in range(nK):
            fn = lambda e, bank=bank, wbuf=wbuf, rhs_buf=rhs_buf, i=i: e.matmul(
                bank[:, 0:TT], wbuf[:, i * 128:(i + 1) * 128],
                rhs_buf[:, i * TT:(i + 1) * TT],
                start=(i == 0), stop=(i == nK - 1))
            if i == nK - 1:
                last = em.opi("t", fn, "pe")
            else:
                em.op("t", fn)
        return last

    for tt in range(NTT):
        if tt > 0:
            em.wait("sy", "xst", em.cntv("xst"))
        x_ld = emit_load_x_tile(em, xt, xin_ap, tt)
        em.wait("v", "xld", x_ld)
        em.wait("sc", "xld", x_ld)
        if tt > 0:
            em.wait("v", "pe", em.cntv("pe"))
        emit_ln_tile(em, C, xt, xcf, f"{tag}_{tt}", t1ap=t1_ap,
                     xb16ap=em.carve(24576, ND * TT, BF16),
                     xsqbap=em.carve(32768, ND * TT, BF16))

        emit_mx_scales(em, C, xcf, ND, tag + "_xs", [(sxr, sxs, 0, 64)])
        for i in range(ND):
            bank_r = psA[i % 2]
            bank_s = psB[i % 2]
            em.wait("t", "dve", em.cntv("dve"))
            em.wait("t", "act", em.cntv("act"))
            em.opi("t", lambda e, i=i, bank_r=bank_r: e.matmul(
                bank_r[:, 0:TT], e16xt[0:64, 128 * i:128 * (i + 1)], sxr[:],
                start=True, stop=True), "pe")
            pb = em.opi("t", lambda e, i=i, bank_s=bank_s: e.matmul(
                bank_s[:, 0:TT], e16xt[0:64, 128 * i:128 * (i + 1)], sxs[:],
                start=True, stop=True), "pe")
            em.wait("v", "pe", pb)
            sl = slice(i * TT, (i + 1) * TT)
            em.opi("v", lambda e, sl=sl, bank_r=bank_r: e.tensor_tensor(
                out=scr["xs"][:, 0:TT], in0=xcf[:, sl], in1=bank_r[:, 0:TT], op=ALU.mult),
                "dve")
            emit_round_e2m1(em, scr["xs"][:, 0:TT], SubView(scr["q"], TT),
                            {k: SubView(scr[k], TT) for k in scr}, tag + f"_xq{tt}_{i}")
            em.opi("v", lambda e, sl=sl, bank_s=bank_s: e.tensor_tensor(
                out=xq[:, sl], in0=scr["q"][:, 0:TT], in1=bank_s[:, 0:TT], op=ALU.mult),
                "dve")
        xq_ready = em.cntv("dve")

        em.wait("t", "dve", xq_ready)
        for j in range(nJ):
            b = j % 2
            em.wait("sy", "pe", max(wA_free[b], wB_free[b]))
            em.opi("sy", lambda e, j=j, b=b: e.dma_start(
                out=wbufA[b][:].rearrange("p (k o) -> p k o", o=128),
                in_=wgT[j].rearrange("k p o -> p k o")), "wld", 16)
            wl = em.opi("sy", lambda e, j=j, b=b: e.dma_start(
                out=wbufB[b][:].rearrange("p (k o) -> p k o", o=128),
                in_=wuT[j].rearrange("k p o -> p k o")), "wld", 16)
            em.wait("t", "wld", wl)
            em.wait("t", "act", psA_ev[b])
            em.wait("t", "dve", psB_ev[b])
            peg = mm_group(psA[b], wbufA[b], xq, ND)
            peu = mm_group(psB[b], wbufB[b], xq, ND)
            wA_free[b] = wB_free[b] = peu
            em.wait("sc", "pe", peg)
            em.wait("sc", "dve", a_free[b])
            sca = em.opi("sc", lambda e, b=b: e.activation(
                out=asil[b][:], in_=psA[b][:, 0:TT], func=AF.Silu), "act")
            psA_ev[b] = sca
            em.wait("v", "pe", peu)
            em.wait("v", "act", sca)
            hj = em.opi("v", lambda e, j=j, b=b: e.tensor_tensor(
                out=hf[:, j * TT:(j + 1) * TT], in0=psB[b][:, 0:TT],
                in1=asil[b][:], op=ALU.mult), "dve")
            psB_ev[b] = a_free[b] = hj

        em.wait("t", "dve", em.cntv("dve"))
        emit_mx_scales(em, C, hf, nJ, tag + "_hs",
                       [(shra, shsa, 0, 128), (shrb, shsb, 128, 48)])
        for i in range(nJ):
            bank_r = psA[i % 2]
            bank_s = psB[i % 2]
            em.wait("t", "dve", em.cntv("dve"))
            em.wait("t", "act", em.cntv("act"))
            if i < 16:
                lr = lambda e, i=i, bank_r=bank_r: e.matmul(
                    bank_r[:, 0:TT], e176at[:][:, 128 * i:128 * (i + 1)], shra[:],
                    start=True, stop=True)
                ls = lambda e, i=i, bank_s=bank_s: e.matmul(
                    bank_s[:, 0:TT], e176at[:][:, 128 * i:128 * (i + 1)], shsa[:],
                    start=True, stop=True)
            else:
                lr = lambda e, i=i, bank_r=bank_r: e.matmul(
                    bank_r[:, 0:TT], e176bt[0:48, 128 * (i - 16):128 * (i - 15)], shrb[:],
                    start=True, stop=True)
                ls = lambda e, i=i, bank_s=bank_s: e.matmul(
                    bank_s[:, 0:TT], e176bt[0:48, 128 * (i - 16):128 * (i - 15)], shsb[:],
                    start=True, stop=True)
            em.opi("t", lr, "pe")
            pb = em.opi("t", ls, "pe")
            em.wait("v", "pe", pb)
            sl = slice(i * TT, (i + 1) * TT)
            em.opi("v", lambda e, sl=sl, bank_r=bank_r: e.tensor_tensor(
                out=scr["xs"][:, 0:TT], in0=hf[:, sl], in1=bank_r[:, 0:TT], op=ALU.mult),
                "dve")
            emit_round_e2m1(em, scr["xs"][:, 0:TT], SubView(scr["q"], TT),
                            {k: SubView(scr[k], TT) for k in scr}, tag + f"_hq{tt}_{i}")
            em.opi("v", lambda e, sl=sl, bank_s=bank_s: e.tensor_tensor(
                out=hq[:, sl], in0=scr["q"][:, 0:TT], in1=bank_s[:, 0:TT], op=ALU.mult),
                "dve")
        hq_ready = em.cntv("dve")

        em.wait("t", "dve", hq_ready)
        for jd in range(ND):
            b = jd % 2
            em.wait("sy", "pe", wD_free[b])
            wl = em.opi("sy", lambda e, jd=jd, b=b: e.dma_start(
                out=wbufD[b][:].rearrange("p (k o) -> p k o", o=128),
                in_=wdT[jd].rearrange("k p o -> p k o")), "wld", 16)
            em.wait("t", "wld", wl)
            em.wait("t", "act", psA_ev[b])
            em.wait("t", "dve", psB_ev[b])
            pej = mm_group(psA[b], wbufD[b], hq, nJ)
            wD_free[b] = pej
            em.wait("v", "pe", pej)
            ev = em.opi("v", lambda e, jd=jd, b=b: e.scalar_tensor_tensor(
                out=xt[:, jd * TT:(jd + 1) * TT], in0=psA[b][:, 0:TT],
                scalar=0.5, in1=xt[:, jd * TT:(jd + 1) * TT],
                op0=ALU.mult, op1=ALU.add), "dve")
            psB_ev[b] = ev
            psA_ev[b] = em.cntv("act")
        em.wait("sy", "dve", em.cntv("dve"))
        emit_store_x_tile(em, xt, xout_ap, tt)
    em.wait("sy", "xst", em.cntv("xst"))



def emit_round_e2m3(em, xs, q, scr, tag):
    W = xs.tensor.shape[1] if hasattr(xs, "tensor") else None
    r0, r1, r2, a, m1, m2 = scr["r0"], scr["r1"], scr["r2"], scr["a"], scr["m1"], scr["m2"]
    em.op("v", lambda e: e.tensor_scalar(out=r0[:], in0=xs, scalar1=2.0, scalar2=-2.0,
                                         op0=ALU.min, op1=ALU.max))
    em.op("v", lambda e: e.tensor_scalar(out=r0[:], in0=r0[:], scalar1=8.0, scalar2=MAGIC,
                                         op0=ALU.mult, op1=ALU.add))
    em.op("v", lambda e: e.tensor_scalar(out=r0[:], in0=r0[:], scalar1=MAGIC, scalar2=0.125,
                                         op0=ALU.subtract, op1=ALU.mult))
    em.op("v", lambda e: e.tensor_scalar(out=r1[:], in0=xs, scalar1=4.0, scalar2=MAGIC,
                                         op0=ALU.mult, op1=ALU.add))
    em.op("v", lambda e: e.tensor_scalar(out=r1[:], in0=r1[:], scalar1=MAGIC, scalar2=0.25,
                                         op0=ALU.subtract, op1=ALU.mult))
    em.op("v", lambda e: e.tensor_scalar(out=r2[:], in0=xs, scalar1=2.0, scalar2=MAGIC,
                                         op0=ALU.mult, op1=ALU.add))
    em.op("v", lambda e: e.tensor_scalar(out=r2[:], in0=r2[:], scalar1=MAGIC, scalar2=0.5,
                                         op0=ALU.subtract, op1=ALU.mult))
    em.wait("sc", "dve", em.cntv("dve"))
    sca = em.opi("sc", lambda e: e.activation(out=a[:], in_=xs, func=AF.Abs), "act")
    em.wait("v", "act", sca)
    em.op("v", lambda e: e.tensor_scalar(out=m1[:], in0=a[:], scalar1=2.0625, scalar2=None,
                                         op0=ALU.is_gt))
    em.op("v", lambda e: e.tensor_scalar(out=m2[:], in0=a[:], scalar1=3.875, scalar2=None,
                                         op0=ALU.is_gt))
    em.op("v", lambda e: e.select(out=q[:], mask=m1[:], on_true=r1[:], on_false=r0[:]))
    em.op("v", lambda e: e.select(out=q[:], mask=m2[:], on_true=r2[:], on_false=q[:]))



def emit_l3_ffn(em, C, xin_ap, xout_ap, w_nat_ap, bias_ap, tag="l3", dbg=None):
    nc = em.nc
    xt = em.sb(tag + "_xt", [128, ND * TT], F32)
    xcf = _T(em.carve(0, ND * TT, F32))
    xdb = _T(em.carve(16384, ND * TT, BF16))
    wqT = _T(em.carve(57344, ND * D, BF16))
    wchunk = _T(em.carve(73728, D, F32))
    wqb = _T(em.carve(90112, D, BF16))
    biassb = em.sb(tag + "_bias", [128, ND], F32)
    scr = {}
    for _i, kk in enumerate(("r0", "r1", "r2", "a", "xs", "q")):
        scr[kk] = _T(em.carve(24576 + _i * 4096, D, F32))
    scr["m1"] = _T(em.carve(49152, D, I32))
    scr["m2"] = _T(em.carve(53248, D, I32))
    e32t = load_pattern(em, C, "e32")
    wsc = em.sb(tag + "_wsc", [128, 32], F32)
    amax_tm = em.sb(tag + "_amax", [128, 4 * 32], F32)
    sclr_tm = em.sb(tag + "_sclr", [128, 32], F32)
    scl_fm_r = em.sb(tag + "_sfr", [32, TT], F32)
    scl_fm_s = em.sb(tag + "_sfs", [32, TT], F32)
    ev1 = em.sb(tag + "_ev1", [128, TT], F32)
    psA = em.ps("ps_g0")
    psB2 = em.ps("ps_g1")
    psC = em.ps("ps_u0")
    psD2 = em.ps("ps_u1")

    for j in range(ND):
        em.opi("sy", lambda e, j=j: e.dma_start(
            out=biassb[:, j:j + 1], in_=bias_ap[128 * j:128 * (j + 1)].rearrange("(p o) -> p o", o=1)),
            "xld", 16)
    bld = em.cntv("xld")
    em.wait("v", "xld", bld)
    em.opi("v", lambda e: e.tensor_scalar(out=biassb[:], in0=biassb[:], scalar1=0.5,
                                          scalar2=None, op0=ALU.mult), "dve")

    for j in range(ND):
        em.wait("sy", "dve", em.cntv("dve"))
        wl = em.opi("sy", lambda e, j=j: e.dma_start(
            out=wchunk[:], in_=w_nat_ap[128 * j:128 * (j + 1), :]), "xld", 16)
        em.wait("v", "xld", wl)
        em.opi("v", lambda e: e.tensor_reduce(
            out=wsc[:].rearrange("p (a b) -> p a b", b=1),
            in_=wchunk[:].rearrange("p (a b) -> p a b", b=32),
            axis=AX.X, op=ALU.max, apply_absolute_value=True), "dve")
        em.vbar()
        em.op("v", lambda e: e.tensor_scalar(out=wsc[:], in0=wsc[:], scalar1=1.0 / 7.5,
                                             scalar2=1e-30, op0=ALU.mult, op1=ALU.max))
        em.vbar()
        em.op("v", lambda e: e.reciprocal(out=sclr_tm[:, 0:32], in_=wsc[:]))
        em.vbar()
        em.opi("v", lambda e: e.tensor_tensor(
            out=scr["xs"][:].rearrange("p (a b) -> p a b", b=32),
            in0=wchunk[:].rearrange("p (a b) -> p a b", b=32),
            in1=sclr_tm[:, 0:32].rearrange("p (a b) -> p a b", b=1).broadcast_to((128, 32, 32)),
            op=ALU.mult), "dve")
        emit_round_e2m3(em, scr["xs"][:], scr["q"], scr, tag + f"_wq{j}")
        dq = em.opi("v", lambda e: e.tensor_tensor(
            out=wqb[:].rearrange("p (a b) -> p a b", b=32),
            in0=scr["q"][:].rearrange("p (a b) -> p a b", b=32),
            in1=wsc[:].rearrange("p (a b) -> p a b", b=1).broadcast_to((128, 32, 32)),
            op=ALU.mult), "dve")
        em.wait("t", "dve", dq)
        for ki in range(ND):
            em.wait("t", "dve", em.cntv("dve"))
            fn = lambda e, ki=ki: e.transpose(
                psA[:].bitcast(BF16)[:, 0:128],
                wqb[:, 128 * ki:128 * (ki + 1)], C["ident_bf16"][:])
            pet = em.opi("t", fn, "pe")
            em.wait("v", "pe", pet)
            em.opi("v", lambda e, ki=ki, j=j: e.tensor_copy(
                out=wqT[:, ki * D + 128 * j: ki * D + 128 * (j + 1)],
                in_=psA[:].bitcast(BF16)[:, 0:128]), "dve")
    wq_done_v = em.cntv("dve")
    if dbg is not None:
        em.wait("sy", "dve", wq_done_v)
        em.opi("sy", lambda e: e.dma_start(out=dbg["wqT"], in_=wqT[:]), "xst", 16)

    for tt in range(NTT):
        if tt > 0:
            em.wait("sy", "xst", em.cntv("xst"))
        x_ld = emit_load_x_tile(em, xt, xin_ap, tt)
        em.wait("v", "xld", x_ld)
        em.wait("sc", "xld", x_ld)
        if tt > 0:
            em.wait("v", "pe", em.cntv("pe"))
        emit_ln_tile(em, C, xt, xcf, f"{tag}_{tt}", out_f32=True,
                     t1ap=em.carve(73728, ND * TT, F32),
                     xb16ap=em.carve(98304, ND * TT, BF16),
                     xsqbap=em.carve(106496, ND * TT, BF16))
        xcf_ready = em.cntv("dve")

        em.wait("t", "dve", xcf_ready)
        for s in range(4):
            for i in range(ND):
                bank = psB2 if (i % 2 == 0) else psC
                em.wait("t", "dve", em.cntv("dve"))
                pet = em.opi("t", lambda e, s=s, i=i, bank=bank: e.transpose(
                    bank[:, 0:128], xcf[:, i * TT + 128 * s: i * TT + 128 * (s + 1)],
                    C["ident_f32"][:]), "pe")
                em.wait("v", "pe", pet)
                em.opi("v", lambda e, s=s, i=i, bank=bank: e.tensor_reduce(
                    out=amax_tm[:, 32 * s + 4 * i: 32 * s + 4 * (i + 1)].rearrange(
                        "p (a b) -> p a b", b=1),
                    in_=bank[:, 0:128].rearrange("p (a b) -> p a b", b=32),
                    axis=AX.X, op=ALU.max, apply_absolute_value=True), "dve")
            em.vbar()
            em.op("v", lambda e, s=s: e.tensor_scalar(
                out=amax_tm[:, 32 * s:32 * (s + 1)], in0=amax_tm[:, 32 * s:32 * (s + 1)],
                scalar1=1.0 / 7.5, scalar2=1e-30, op0=ALU.mult, op1=ALU.max))
            em.vbar()
            rc = em.opi("v", lambda e, s=s: e.reciprocal(
                out=sclr_tm[:], in_=amax_tm[:, 32 * s:32 * (s + 1)]), "dve")
            rc = em.vbar()
            em.wait("t", "dve", em.cntv("dve"))
            p1 = em.opi("t", lambda e, s=s: e.transpose(
                psD2[0:32, 0:128], sclr_tm[:], C["ident_f32"][:]), "pe")
            em.wait("v", "pe", p1)
            em.opi("v", lambda e, s=s: e.tensor_copy(
                out=scl_fm_r[:, 128 * s:128 * (s + 1)], in_=psD2[0:32, 0:128]), "dve")
            rc2 = em.cntv("dve")
            em.wait("t", "dve", rc2)
            p2 = em.opi("t", lambda e, s=s: e.transpose(
                psD2[0:32, 0:128], amax_tm[:, 32 * s:32 * (s + 1)], C["ident_f32"][:]), "pe")
            em.wait("v", "pe", p2)
            em.opi("v", lambda e, s=s: e.tensor_copy(
                out=scl_fm_s[:, 128 * s:128 * (s + 1)], in_=psD2[0:32, 0:128]), "dve")
        scl_ready = em.cntv("dve")

        for i in range(ND):
            bank_r = psB2 if i % 2 == 0 else psC
            em.wait("t", "dve", em.cntv("dve"))
            em.wait("t", "act", em.cntv("act"))
            em.opi("t", lambda e, i=i, bank_r=bank_r: e.matmul(
                bank_r[:, 0:TT], e32t[:][:, 128 * i:128 * (i + 1)], scl_fm_r[:],
                start=True, stop=True), "pe")
            pb = em.opi("t", lambda e, i=i: e.matmul(
                psD2[:, 0:TT], e32t[:][:, 128 * i:128 * (i + 1)], scl_fm_s[:],
                start=True, stop=True), "pe")
            em.wait("v", "pe", pb)
            sl = slice(i * TT, (i + 1) * TT)
            em.opi("v", lambda e, sl=sl, bank_r=bank_r: e.tensor_tensor(
                out=scr["xs"][:, 0:TT], in0=xcf[:, sl], in1=bank_r[:, 0:TT], op=ALU.mult),
                "dve")
            scrT = {k: SubView(scr[k], TT) for k in scr}
            emit_round_e2m3(em, scr["xs"][:, 0:TT], scrT["q"], scrT, tag + f"_xq{tt}_{i}")
            em.opi("v", lambda e, sl=sl: e.tensor_tensor(
                out=xdb[:, sl], in0=scr["q"][:, 0:TT], in1=psD2[:, 0:TT], op=ALU.mult),
                "dve")
        xdb_ready = em.cntv("dve")
        if dbg is not None and tt == 0:
            em.wait("sy", "dve", xdb_ready)
            em.opi("sy", lambda e: e.dma_start(out=dbg["xdb"], in_=xdb[:]), "xst", 16)
            em.opi("sy", lambda e: e.dma_start(out=dbg["xcf"], in_=xcf[:]), "xst", 16)
            em.opi("sy", lambda e: e.dma_start(out=dbg["sfr"], in_=scl_fm_r[:]), "xst", 16)
            em.opi("sy", lambda e: e.dma_start(out=dbg["sfs"], in_=scl_fm_s[:]), "xst", 16)

        em.wait("t", "dve", xdb_ready)
        for j in range(ND):
            bank = psB2 if j % 2 == 0 else psC
            em.wait("t", "act", em.cntv("act"))
            for i in range(ND):
                fn = lambda e, j=j, i=i, bank=bank: e.matmul(
                    bank[:, 0:TT], wqT[:, i * D + 128 * j: i * D + 128 * (j + 1)],
                    xdb[:, i * TT:(i + 1) * TT], start=(i == 0), stop=(i == ND - 1))
                pj = em.opi("t", fn, "pe") if i == ND - 1 else em.op("t", fn)
            em.wait("sc", "pe", pj)
            em.wait("sc", "dve", em.cntv("dve"))
            ev = em.opi("sc", lambda e, j=j, bank=bank: e.activation(
                out=ev1[:], in_=bank[:, 0:TT], func=AF.Identity, scale=0.5,
                bias=biassb[:, j:j + 1]), "act")
            em.wait("v", "act", ev)
            em.opi("v", lambda e, j=j: e.tensor_tensor(
                out=xt[:, j * TT:(j + 1) * TT], in0=ev1[:],
                in1=xt[:, j * TT:(j + 1) * TT], op=ALU.add), "dve")
        em.wait("sy", "dve", em.cntv("dve"))
        emit_store_x_tile(em, xt, xout_ap, tt)
    em.wait("sy", "xst", em.cntv("xst"))


class _T:
    def __init__(self, ap):
        self.ap = ap
    def __getitem__(self, idx):
        if idx == slice(None):
            return self.ap
        return self.ap[idx]


class SubView:
    def __init__(self, t, W):
        self.t = t; self.W = W
    def __getitem__(self, idx):
        return self.t[:, 0:self.W]



def build_l1_ffn_program():
    nc = bass.Bass("TRN2", target_bir_lowering=False)
    stack = ExitStack()
    xin = nc.dram_tensor("xin", [D, S], F32, kind="ExternalInput").ap()
    wg = nc.dram_tensor("wg", [24, 4, 128, 256], I32, kind="ExternalInput").ap()
    wu = nc.dram_tensor("wu", [24, 4, 128, 256], I32, kind="ExternalInput").ap()
    wd = nc.dram_tensor("wd", [8, 12, 128, 256], I32, kind="ExternalInput").ap()
    xout = nc.dram_tensor("xout", [D, S], F32, kind="ExternalOutput").ap()
    em = Em(nc, stack)
    C = declare_consts(nc, em, need=())
    emit_l1_ffn(em, C, xin, xout, wg, wu, wd, tag="l1")
    em.finalize()
    stack.close()
    return nc


def build_l2_attn_program():
    nc = bass.Bass("TRN2", target_bir_lowering=False)
    stack = ExitStack()
    xin = nc.dram_tensor("xin", [D, S], F32, kind="ExternalInput").ap()
    wqp = nc.dram_tensor("wqp", [8, 4, 128, 256], F32, kind="ExternalInput").ap()
    wkp = nc.dram_tensor("wkp", [8, 4, 128, 256], F32, kind="ExternalInput").ap()
    wvp = nc.dram_tensor("wvp", [8, 4, 128, 256], F32, kind="ExternalInput").ap()
    wop = nc.dram_tensor("wop", [8, 4, 128, 256], F32, kind="ExternalInput").ap()
    vecs = {}
    for name in ("bq", "bk", "bv", "bo"):
        vecs[name] = nc.dram_tensor(name, [1024], F32, kind="ExternalInput").ap()
    cosr = nc.dram_tensor("cosr", [128, S], F32, kind="ExternalInput").ap()
    sinr = nc.dram_tensor("sinr", [128, S], F32, kind="ExternalInput").ap()
    xout = nc.dram_tensor("xout", [D, S], F32, kind="ExternalOutput").ap()
    em = Em(nc, stack, pool_bytes=63488)
    C = declare_consts(nc, em, need=("ident_bf16", "tri", "ones_col1"))
    import os
    dbg = None
    if os.environ.get("L2A_DBG"):
        dbg = {"scc": nc.dram_tensor("dbg_scc", [128, 8 * 65], BF16, kind="ExternalOutput").ap(),
               "scs": nc.dram_tensor("dbg_scs", [128, 8 * 65], BF16, kind="ExternalOutput").ap(),
               "otm": nc.dram_tensor("dbg_otm", [128, 4096], BF16, kind="ExternalOutput").ap(),
               "qc": nc.dram_tensor("dbg_qc", [128, 4096], BF16, kind="ExternalOutput").ap(),
               "kc": nc.dram_tensor("dbg_kc", [128, 4096], BF16, kind="ExternalOutput").ap(),
               "vb": nc.dram_tensor("dbg_vb", [128, 4096], BF16, kind="ExternalOutput").ap(),
               "vt1": nc.dram_tensor("dbg_vt1", [128, 16 * 65], BF16, kind="ExternalOutput").ap(),
               "kct": nc.dram_tensor("dbg_kct", [128, 8 * 128], BF16, kind="ExternalOutput").ap()}
    emit_l2_attn(em, C, xin, xout, wqp, wkp, wvp, wop,
                 vecs["bq"], vecs["bk"], vecs["bv"], vecs["bo"], cosr, sinr, tag="l2a", dbg=dbg)
    em.finalize()
    stack.close()
    return nc


def build_l1_attn_program():
    nc = bass.Bass("TRN2", target_bir_lowering=False)
    stack = ExitStack()
    xin = nc.dram_tensor("xin", [D, S], F32, kind="ExternalInput").ap()
    wqkvp = nc.dram_tensor("wqkvp", [24, 4, 128, 256], F32, kind="ExternalInput").ap()
    bq = nc.dram_tensor("bq", [1024], F32, kind="ExternalInput").ap()
    bk = nc.dram_tensor("bk", [1024], F32, kind="ExternalInput").ap()
    bv = nc.dram_tensor("bv", [1024], F32, kind="ExternalInput").ap()
    gam = nc.dram_tensor("gam", [16], F32, kind="ExternalInput").ap()
    bet = nc.dram_tensor("bet", [16], F32, kind="ExternalInput").ap()
    xout = nc.dram_tensor("xout", [D, S], F32, kind="ExternalOutput").ap()
    em = Em(nc, stack, pool_bytes=92160)
    C = declare_consts(nc, em, need=("ehq16", "ehb16"))
    emit_l1_attn(em, C, xin, xout, wqkvp, bq, bk, bv, gam, bet, tag="l1a")
    em.finalize()
    stack.close()
    return nc


def build_l3_attn_program():
    nc = bass.Bass("TRN2", target_bir_lowering=False)
    stack = ExitStack()
    xin = nc.dram_tensor("xin", [D, S], F32, kind="ExternalInput").ap()
    wqp = nc.dram_tensor("wqp", [8, 4, 128, 256], F32, kind="ExternalInput").ap()
    wkp = nc.dram_tensor("wkp", [8, 4, 128, 256], F32, kind="ExternalInput").ap()
    wvp = nc.dram_tensor("wvp", [8, 4, 128, 256], F32, kind="ExternalInput").ap()
    wop = nc.dram_tensor("wop", [8, 4, 128, 256], F32, kind="ExternalInput").ap()
    vecs = {}
    for name in ("bq", "bk", "bv", "bo"):
        vecs[name] = nc.dram_tensor(name, [1024], F32, kind="ExternalInput").ap()
    xout = nc.dram_tensor("xout", [D, S], F32, kind="ExternalOutput").ap()
    em = Em(nc, stack, pool_bytes=92160)
    C = declare_consts(nc, em, need=("ehq16", "ehb16"))
    emit_l3_attn(em, C, xin, xout, wqp, wkp, wvp, wop,
                 vecs["bq"], vecs["bk"], vecs["bv"], vecs["bo"], tag="l3a")
    em.finalize()
    stack.close()
    return nc


def build_l0_attn_program():
    nc = bass.Bass("TRN2", target_bir_lowering=False)
    stack = ExitStack()
    xin = nc.dram_tensor("xin", [D, S], F32, kind="ExternalInput").ap()
    wqp = nc.dram_tensor("wqp", [4, 4, 128, 256], F32, kind="ExternalInput").ap()
    wkp = nc.dram_tensor("wkp", [4, 4, 128, 256], F32, kind="ExternalInput").ap()
    wvp = nc.dram_tensor("wvp", [4, 4, 128, 256], F32, kind="ExternalInput").ap()
    wop = nc.dram_tensor("wop", [8, 2, 128, 256], F32, kind="ExternalInput").ap()
    vecs = {}
    for name, n in (("bq", 512), ("bk", 512), ("bv", 512), ("bo", 1024),
                    ("alpha", 512), ("augp", 512), ("lns", 1024), ("lnb", 1024)):
        vecs[name] = nc.dram_tensor(name, [n], F32, kind="ExternalInput").ap()
    xout = nc.dram_tensor("xout", [D, S], F32, kind="ExternalOutput").ap()
    em = Em(nc, stack)
    C = declare_consts(nc, em, need=("ident_f32", "ehq", "ehb", "eps_ln2"))
    emit_l0_attn(em, C, xin, xout, wqp, wkp, wvp, wop,
                 vecs["bq"], vecs["bk"], vecs["bv"], vecs["bo"],
                 vecs["alpha"], vecs["augp"], vecs["lns"], vecs["lnb"], tag="l0a")
    em.finalize()
    stack.close()
    return nc


def build_l2_ffn_program():
    nc = bass.Bass("TRN2", target_bir_lowering=False)
    stack = ExitStack()
    xin = nc.dram_tensor("xin", [D, S], F32, kind="ExternalInput").ap()
    wg = nc.dram_tensor("wg", [2816, 1024], F32, kind="ExternalInput").ap()
    wu = nc.dram_tensor("wu", [2816, 1024], F32, kind="ExternalInput").ap()
    wd = nc.dram_tensor("wd", [1024, 2816], F32, kind="ExternalInput").ap()
    xout = nc.dram_tensor("xout", [D, S], F32, kind="ExternalOutput").ap()
    em = Em(nc, stack)
    C = declare_consts(nc, em, need=("ident_f32", "ident_bf16", "e16x", "e176a", "e176b"))
    emit_l2_ffn(em, C, xin, xout, wg, wu, wd, tag="l2")
    em.finalize()
    stack.close()
    return nc


def build_l0_ffn_program():
    nc = bass.Bass("TRN2", target_bir_lowering=False)
    stack = ExitStack()
    xin = nc.dram_tensor("xin", [D, S], F32, kind="ExternalInput").ap()
    w1 = nc.dram_tensor("w1", [32, 4, 128, 256], F32, kind="ExternalInput").ap()
    w2 = nc.dram_tensor("w2", [32, 4, 128, 256], F32, kind="ExternalInput").ap()
    h1 = nc.dram_tensor("h1", [32, 16, 128, 256], F32, kind="ExternalInput").ap()
    h2 = nc.dram_tensor("h2", [32, 16, 128, 256], F32, kind="ExternalInput").ap()
    w3 = nc.dram_tensor("w3", [8, 16, 128, 256], F32, kind="ExternalInput").ap()
    s1 = nc.dram_tensor("s1", [4096], F32, kind="ExternalInput").ap()
    s2 = nc.dram_tensor("s2", [4096], F32, kind="ExternalInput").ap()
    sh1 = nc.dram_tensor("sh1", [4096], F32, kind="ExternalInput").ap()
    sh2 = nc.dram_tensor("sh2", [4096], F32, kind="ExternalInput").ap()
    s3 = nc.dram_tensor("s3", [1024], F32, kind="ExternalInput").ap()
    xout = nc.dram_tensor("xout", [D, S], F32, kind="ExternalOutput").ap()
    em = Em(nc, stack)
    C = declare_consts(nc, em, need=())
    emit_l0_ffn(em, C, xin, xout, w1, w2, h1, h2, w3, s1, s2, sh1, sh2, s3, tag="l0")
    em.finalize()
    stack.close()
    return nc


def build_l3_ffn_program(debug=False):
    nc = bass.Bass("TRN2", target_bir_lowering=False)
    stack = ExitStack()
    xin = nc.dram_tensor("xin", [D, S], F32, kind="ExternalInput").ap()
    w = nc.dram_tensor("w", [D, D], F32, kind="ExternalInput").ap()
    bias = nc.dram_tensor("bias", [D], F32, kind="ExternalInput").ap()
    xout = nc.dram_tensor("xout", [D, S], F32, kind="ExternalOutput").ap()
    dbg = None
    if debug:
        dbg = {
            "xdb": nc.dram_tensor("dbg_xdb", [128, ND * TT], BF16, kind="ExternalOutput").ap(),
            "xcf": nc.dram_tensor("dbg_xcf", [128, ND * TT], F32, kind="ExternalOutput").ap(),
            "sfr": nc.dram_tensor("dbg_sfr", [32, TT], F32, kind="ExternalOutput").ap(),
            "sfs": nc.dram_tensor("dbg_sfs", [32, TT], F32, kind="ExternalOutput").ap(),
            "wqT": nc.dram_tensor("dbg_wqT", [128, ND * D], BF16, kind="ExternalOutput").ap(),
        }
    em = Em(nc, stack)
    C = declare_consts(nc, em, need=("ident_f32", "ident_bf16", "e32"))
    emit_l3_ffn(em, C, xin, xout, w, bias, tag="l3", dbg=dbg)
    em.finalize()
    stack.close()
    return nc


def build_fused_program():
    nc = bass.Bass("TRN2", target_bir_lowering=False)
    stack = ExitStack()
    em = Em(nc, stack, pool_bytes=118784)

    def din(name, shape, dt=F32):
        return nc.dram_tensor(name, list(shape), dt, kind="ExternalInput").ap()

    xin = din("xin", [D, S])
    xout = nc.dram_tensor("xout", [D, S], F32, kind="ExternalOutput").ap()
    resid = [xin] + [nc.dram_tensor(f"r{i}", [D, S], F32, kind="Internal").ap()
                     for i in range(11)] + [xout]

    l0f = {k: din("l0f_" + k, s, dt) for k, s, dt in (
        ("w1", [32, 4, 128, 256], F32), ("w2", [32, 4, 128, 256], F32),
        ("h1", [32, 16, 128, 256], F32), ("h2", [32, 16, 128, 256], F32),
        ("w3", [8, 16, 128, 256], F32), ("s1", [4096], F32), ("s2", [4096], F32),
        ("sh1", [4096], F32), ("sh2", [4096], F32), ("s3", [1024], F32))}
    l0a = {k: din("l0a_" + k, s, dt) for k, s, dt in (
        ("wqp", [4, 4, 128, 256], F32), ("wkp", [4, 4, 128, 256], F32),
        ("wvp", [4, 4, 128, 256], F32), ("wop", [8, 2, 128, 256], F32),
        ("bq", [512], F32), ("bk", [512], F32), ("bv", [512], F32), ("bo", [1024], F32),
        ("alpha", [512], F32), ("augp", [512], F32), ("lns", [1024], F32),
        ("lnb", [1024], F32))}
    l1f = {k: din("l1f_" + k, s, dt) for k, s, dt in (
        ("wg", [24, 4, 128, 256], I32), ("wu", [24, 4, 128, 256], I32),
        ("wd", [8, 12, 128, 256], I32))}
    l1a = {k: din("l1a_" + k, s, dt) for k, s, dt in (
        ("wqkvp", [24, 4, 128, 256], F32), ("bq", [1024], F32), ("bk", [1024], F32),
        ("bv", [1024], F32), ("gam", [16], F32), ("bet", [16], F32))}
    l2f = {k: din("l2f_" + k, s, dt) for k, s, dt in (
        ("wg", [2816, 1024], F32), ("wu", [2816, 1024], F32), ("wd", [1024, 2816], F32))}
    l2a = {k: din("l2a_" + k, s, dt) for k, s, dt in (
        ("wqp", [8, 4, 128, 256], F32), ("wkp", [8, 4, 128, 256], F32),
        ("wvp", [8, 4, 128, 256], F32), ("wop", [8, 4, 128, 256], F32),
        ("bq", [1024], F32), ("bk", [1024], F32), ("bv", [1024], F32),
        ("bo", [1024], F32), ("cosr", [128, S], F32), ("sinr", [128, S], F32))}
    l3f = {k: din("l3f_" + k, s, dt) for k, s, dt in (
        ("w", [1024, 1024], F32), ("bias", [1024], F32))}
    l3a = {k: din("l3a_" + k, s, dt) for k, s, dt in (
        ("wqp", [8, 4, 128, 256], F32), ("wkp", [8, 4, 128, 256], F32),
        ("wvp", [8, 4, 128, 256], F32), ("wop", [8, 4, 128, 256], F32),
        ("bq", [1024], F32), ("bk", [1024], F32), ("bv", [1024], F32),
        ("bo", [1024], F32))}

    C = declare_consts(nc, em, need=("ident_f32", "ident_bf16", "e32", "e16x",
                                     "e176a", "e176b", "ehq", "ehb", "eps_ln2",
                                     "ehq16", "ehb16", "tri", "ones_col1"))
    ws = {"l0f": {}, "l1f": {}, "l2f": {}}

    em.pool()
    for nm in ("ln_s1", "ln_var", "ln_rstd", "ln_scr"):
        em.sb_global(nm, [1, TT], F32)
    for bank in ("ps_g0", "ps_g1", "ps_u0", "ps_u1", "ps_lnS", "ps_lnB", "ps_lnR"):
        em.ps(bank)

    steps = [
        lambda i, o, t: emit_l0_ffn(em, C, i, o, l0f["w1"], l0f["w2"], l0f["h1"],
                                    l0f["h2"], l0f["w3"], l0f["s1"], l0f["s2"],
                                    l0f["sh1"], l0f["sh2"], l0f["s3"], tag=t,
                                    wstate=ws["l0f"]),
        lambda i, o, t: emit_l0_attn(em, C, i, o, l0a["wqp"], l0a["wkp"], l0a["wvp"],
                                     l0a["wop"], l0a["bq"], l0a["bk"], l0a["bv"],
                                     l0a["bo"], l0a["alpha"], l0a["augp"],
                                     l0a["lns"], l0a["lnb"], tag=t),
        lambda i, o, t: emit_l0_ffn(em, C, i, o, l0f["w1"], l0f["w2"], l0f["h1"],
                                    l0f["h2"], l0f["w3"], l0f["s1"], l0f["s2"],
                                    l0f["sh1"], l0f["sh2"], l0f["s3"], tag=t,
                                    wstate=ws["l0f"]),
        lambda i, o, t: emit_l1_ffn(em, C, i, o, l1f["wg"], l1f["wu"], l1f["wd"],
                                    tag=t, wstate=ws["l1f"]),
        lambda i, o, t: emit_l1_attn(em, C, i, o, l1a["wqkvp"], l1a["bq"], l1a["bk"],
                                     l1a["bv"], l1a["gam"], l1a["bet"], tag=t),
        lambda i, o, t: emit_l1_ffn(em, C, i, o, l1f["wg"], l1f["wu"], l1f["wd"],
                                    tag=t, wstate=ws["l1f"]),
        lambda i, o, t: emit_l2_ffn(em, C, i, o, l2f["wg"], l2f["wu"], l2f["wd"],
                                    tag=t, wstate=ws["l2f"]),
        lambda i, o, t: emit_l2_attn(em, C, i, o, l2a["wqp"], l2a["wkp"], l2a["wvp"],
                                     l2a["wop"], l2a["bq"], l2a["bk"], l2a["bv"],
                                     l2a["bo"], l2a["cosr"], l2a["sinr"], tag=t),
        lambda i, o, t: emit_l2_ffn(em, C, i, o, l2f["wg"], l2f["wu"], l2f["wd"],
                                    tag=t, wstate=ws["l2f"]),
        lambda i, o, t: emit_l3_ffn(em, C, i, o, l3f["w"], l3f["bias"], tag=t),
        lambda i, o, t: emit_l3_attn(em, C, i, o, l3a["wqp"], l3a["wkp"], l3a["wvp"],
                                     l3a["wop"], l3a["bq"], l3a["bk"], l3a["bv"],
                                     l3a["bo"], tag=t),
        lambda i, o, t: emit_l3_ffn(em, C, i, o, l3f["w"], l3f["bias"], tag=t),
    ]
    for idx, fn in enumerate(steps):
        em.push_scope()
        fn(resid[idx], resid[idx + 1], f"s{idx}")
        em.join()
        em.pop_scope()
    em.finalize()
    stack.close()
    return nc



_PROG_CACHE = {}


def _prog(name, builder):
    if name not in _PROG_CACHE:
        _PROG_CACHE[name] = builder()
    return _PROG_CACHE[name]


def kernel(**inputs):
    ii = inputs
    consts = {"c_" + k: v for k, v in build_consts().items()}
    ncores = 8

    T_ = np.ascontiguousarray
    packs = {}
    packs["l0f"] = {
        "w1": pack_w(ii["L0_ffn_w1_w"]), "w2": pack_w(ii["L0_ffn_w2_w"]),
        "h1": pack_w(ii["L0_ffn_h1_w"]), "h2": pack_w(ii["L0_ffn_h2_w"]),
        "w3": pack_w(ii["L0_ffn_w3_w"]),
        "s1": ii["L0_ffn_w1_s"], "s2": ii["L0_ffn_w2_s"],
        "sh1": ii["L0_ffn_h1_s"], "sh2": ii["L0_ffn_h2_s"], "s3": ii["L0_ffn_w3_s"]}
    packs["l0a"] = {
        "wqp": pack_w(T_(ii["L0_attn_Wq"].T)), "wkp": pack_w(T_(ii["L0_attn_Wk"].T)),
        "wvp": pack_w(T_(ii["L0_attn_Wv"].T)), "wop": pack_w(T_(ii["L0_attn_Wo"].T)),
        "bq": ii["L0_attn_bq"], "bk": ii["L0_attn_bk"], "bv": ii["L0_attn_bv"],
        "bo": ii["L0_attn_bo"],
        "alpha": T_(ii["L0_attn_alpha"].reshape(-1)),
        "augp": T_(ii["L0_attn_aug_param"].reshape(-1)),
        "lns": ii["L0_attn_ln_scale"], "lnb": ii["L0_attn_ln_bias"]}
    perm = []
    for part in range(3):
        for h in range(16):
            perm.extend(range(192 * h + 64 * part, 192 * h + 64 * part + 64))
    perm = np.array(perm)
    bperm = ii["L1_attn_bqkv"][perm]
    packs["l1a"] = {
        "wqkvp": pack_w(T_(ii["L1_attn_Wqkv"][perm])),
        "bq": T_(bperm[0:1024]), "bk": T_(bperm[1024:2048]), "bv": T_(bperm[2048:3072]),
        "gam": T_(ii["L1_attn_seq_gamma"].reshape(-1)),
        "bet": T_(ii["L1_attn_seq_beta"].reshape(-1))}
    packs["l1f"] = {"wg": pack_w(ii["L1_ffn_w_gate"]), "wu": pack_w(ii["L1_ffn_w_up"]),
                    "wd": pack_w(ii["L1_ffn_w_down"])}
    packs["l2f"] = {"wg": ii["L2_ffn_w_gate"], "wu": ii["L2_ffn_w_up"],
                    "wd": ii["L2_ffn_w_down"]}
    ang = np.pi / (2.0 * S) * np.arange(S, dtype=np.float32)
    packs["l2a"] = {
        "wqp": pack_w(T_(ii["L2_attn_Wq"].T)), "wkp": pack_w(T_(ii["L2_attn_Wk"].T)),
        "wvp": pack_w(T_(ii["L2_attn_Wv"].T)), "wop": pack_w(T_(ii["L2_attn_Wo"].T)),
        "bq": ii["L2_attn_bq"], "bk": ii["L2_attn_bk"], "bv": ii["L2_attn_bv"],
        "bo": ii["L2_attn_bo"],
        "cosr": np.broadcast_to(np.cos(ang), (128, S)).copy(),
        "sinr": np.broadcast_to(np.sin(ang), (128, S)).copy()}
    packs["l3f"] = {"w": ii["L3_ffn_weight"], "bias": ii["L3_ffn_bias"]}
    packs["l3a"] = {
        "wqp": pack_w(ii["L3_attn_wq"]), "wkp": pack_w(ii["L3_attn_wk"]),
        "wvp": pack_w(ii["L3_attn_wv"]), "wop": pack_w(ii["L3_attn_wo"]),
        "bq": ii["L3_attn_bq"], "bk": ii["L3_attn_bk"], "bv": ii["L3_attn_bv"],
        "bo": ii["L3_attn_bo"]}

    x = np.asarray(ii["L0_attn_x"], np.float32)
    xs = [feat_major(x[c]) for c in range(ncores)]

    nc = _prog("fused", build_fused_program)
    base = {}
    for pk, pref in (("l0f", "l0f_"), ("l0a", "l0a_"), ("l1f", "l1f_"),
                     ("l1a", "l1a_"), ("l2f", "l2f_"), ("l2a", "l2a_"),
                     ("l3f", "l3f_"), ("l3a", "l3a_")):
        for k, v in packs[pk].items():
            base[pref + k] = v
    base.update(consts)
    in_maps = []
    for c in range(ncores):
        im = dict(base)
        im["xin"] = xs[c]
        in_maps.append(im)
    res = run_bass_kernel_spmd(nc, in_maps, core_ids=list(range(ncores)))
    out = np.stack([np.asarray(res.results[c]["xout"], np.float32).T
                    for c in range(ncores)], axis=0)
    return np.ascontiguousarray(out.astype(np.float32))


# version 04
# speedup vs baseline: 1.0044x; 1.0044x over previous
import numpy as np
from contextlib import ExitStack

import concourse.bass as bass
import concourse.mybir as mybir
import concourse.bass_utils as _bu
from concourse.bass_utils import run_bass_kernel_spmd

if not getattr(_bu, "_ldwopt_patched", False):
    _orig_run_command = _bu.run_command

    def _run_command_ldwopt(argv, **kwargs):
        if isinstance(argv, list):
            argv = ["--enable-ldw-opt=true" if c == "--enable-ldw-opt=false" else c
                    for c in argv]
        return _orig_run_command(argv, **kwargs)

    _bu.run_command = _run_command_ldwopt
    _bu._ldwopt_patched = True

F32 = mybir.dt.float32
BF16 = mybir.dt.bfloat16
I32 = mybir.dt.int32
U8 = mybir.dt.uint8
AF = mybir.ActivationFunctionType
ALU = mybir.AluOpType
AX = mybir.AxisListType

MAGIC = 12582912.0
D = 1024
S = 4096
ND = D // 128
TT = 512
NTT = S // TT
LN_EPS = 1e-5


def pack_w(w):
    O, K = w.shape
    nO, nK = O // 128, K // 128
    a = np.ascontiguousarray(w.T)
    a = a.reshape(nK // 2, 2, 128, nO, 128)
    a = a.transpose(3, 0, 2, 1, 4)
    return np.ascontiguousarray(a.reshape(nO, nK // 2, 128, 256))


def feat_major(x2d):
    return np.ascontiguousarray(x2d.T.astype(np.float32, copy=False))


def build_consts():
    c = {}
    c["ones_col"] = np.full((128, 1), 1.0 / D, np.float32)
    c["ones_row"] = np.ones((1, 128), np.float32)
    c["eps_ln"] = np.full((1, 1), LN_EPS, np.float32)
    c["ident_bf16"] = np.eye(128, dtype=np.float32)
    c["ident_f32"] = np.eye(128, dtype=np.float32)
    e32 = np.zeros((32, 1024), np.float32)
    for d in range(1024):
        e32[(d // 32) % 32, d] = 1.0
    c["e32"] = e32
    e16x = np.zeros((64, 1024), np.float32)
    for d in range(1024):
        e16x[d // 16, d] = 1.0
    c["e16x"] = e16x
    e176a = np.zeros((128, 16 * 128), np.float32)
    for jj in range(16):
        for p in range(128):
            e176a[8 * jj + p // 16, 128 * jj + p] = 1.0
    c["e176a"] = e176a
    e176b = np.zeros((48, 6 * 128), np.float32)
    for jj in range(6):
        for p in range(128):
            e176b[8 * jj + p // 16, 128 * jj + p] = 1.0
    c["e176b"] = e176b
    ehq = np.zeros((128, 4 * 8), np.float32)
    for i in range(4):
        for p in range(128):
            ehq[p, 8 * i + 2 * i + p // 64] = 0.125
    c["ehq"] = ehq
    ehb = np.zeros((8, 4 * 128), np.float32)
    for i in range(4):
        for p in range(128):
            ehb[2 * i + p // 64, 128 * i + p] = 1.0
    c["ehb"] = ehb
    c["eps_ln2"] = np.full((1, 1), 1e-6, np.float32)
    ehq16 = np.zeros((128, 8 * 16), np.float32)
    for i in range(8):
        for p in range(128):
            ehq16[p, 16 * i + 2 * i + p // 64] = 1.0
    c["ehq16"] = ehq16
    ehb16 = np.zeros((16, 8 * 128), np.float32)
    for i in range(8):
        for p in range(128):
            ehb16[2 * i + p // 64, 128 * i + p] = 1.0
    c["ehb16"] = ehb16
    tri = np.tril(np.ones((128, 128), np.float32))
    c["tri"] = np.triu(np.ones((128, 128), np.float32)).astype(np.float32)
    c["ones_col1"] = np.ones((128, 1), np.float32)
    return c


CONST_SPECS = {
    "ones_col": ([128, 1], F32),
    "ones_row": ([1, 128], F32),
    "eps_ln": ([1, 1], F32),
    "ident_bf16": ([128, 128], F32),
    "ident_f32": ([128, 128], F32),
    "e32": ([32, 1024], F32),
    "e16x": ([64, 1024], F32),
    "e176a": ([128, 16 * 128], F32),
    "e176b": ([48, 6 * 128], F32),
    "ehq": ([128, 4 * 8], F32),
    "ehb": ([8, 4 * 128], F32),
    "eps_ln2": ([1, 1], F32),
    "ehq16": ([128, 8 * 16], F32),
    "ehb16": ([16, 8 * 128], F32),
    "tri": ([128, 128], F32),
    "ones_col1": ([128, 1], F32),
}



class Em:

    EKEYS = ("sy", "v", "sc", "t", "gp")

    def __init__(self, nc, stack, pool_bytes=None):
        self.nc = nc
        self.stack = stack
        if pool_bytes is not None:
            self.POOL_BYTES = pool_bytes
        self.q = {k: [] for k in self.EKEYS}
        self.sems = {}
        self.cnt = {}
        self.sb_tensors = {}
        self.ps_tensors = {}

    def cntv(self, name):
        self.sem(name)
        return self.cnt[name]

    def sem(self, name):
        if name not in self.sems:
            self.sems[name] = self.stack.enter_context(self.nc.semaphore(name))
            self.cnt[name] = 0
        return self.sems[name]

    def sb(self, name, shape, dtype):
        if name not in self.sb_tensors:
            st = getattr(self, "_scope", None) or self.stack
            self.sb_tensors[name] = st.enter_context(
                self.nc.sbuf_tensor(name, shape, dtype))
            if getattr(self, "_scope", None) is not None:
                self._scope_names.append(name)
        return self.sb_tensors[name]

    def sb_global(self, name, shape, dtype):
        if name not in self.sb_tensors:
            self.sb_tensors[name] = self.stack.enter_context(
                self.nc.sbuf_tensor(name, shape, dtype))
            self._global_names = getattr(self, "_global_names", set())
            self._global_names.add(name)
        return self.sb_tensors[name]

    def push_scope(self):
        self._scope = ExitStack()
        self._scope_names = []

    def pop_scope(self):
        for n in self._scope_names:
            if n in self.sb_tensors:
                del self.sb_tensors[n]
        self._scope.close()
        self._scope = None
        self._scope_names = []

    def join(self):
        for ek in self.EKEYS:
            for name in list(self.sems.keys()):
                if self.cnt[name] > 0:
                    self.wait(ek, name, self.cnt[name])

    POOL_BYTES = 118784

    def pool(self):
        if hasattr(self, "sb_global"):
            return self.sb_global("g_pool", [128, self.POOL_BYTES], U8)
        return self.sb("g_pool", [128, self.POOL_BYTES], U8)

    def carve(self, off_bytes, cols, dtype):
        esz = mybir.dt.size(dtype)
        p = self.pool()
        return p[:, off_bytes:off_bytes + cols * esz].bitcast(dtype)

    def ps(self, name, shape=(128, 512), dtype=F32):
        if name not in self.ps_tensors:
            self.ps_tensors[name] = self.stack.enter_context(
                self.nc.psum_tensor(name, list(shape), dtype))
        return self.ps_tensors[name]

    def op(self, ek, fn):
        self.q[ek].append(fn)

    def opi(self, ek, fn, sem_name, k=1):
        h = self.sem(sem_name)
        self.cnt[sem_name] += k
        tot = self.cnt[sem_name]
        self.q[ek].append(lambda e, fn=fn, h=h, k=k: fn(e).then_inc(h, k))
        return tot

    def vbar(self, ek="v", sem_name="dve"):
        h = self.sem(sem_name)
        self.cnt[sem_name] += 1
        tot = self.cnt[sem_name]
        self.q[ek].append(lambda e, h=h: e.engine_nop().then_inc(h, 1))
        self.q[ek].append(lambda e, h=h, tot=tot: e.wait_ge(h, tot))
        return tot

    def wait(self, ek, sem_name, val):
        h = self.sem(sem_name)
        self.q[ek].append(lambda e, h=h, val=val: e.wait_ge(h, val))

    def wait_cur(self, ek, sem_name):
        self.wait(ek, sem_name, self.cnt.get(sem_name, 0))

    def finalize(self):
        block = self.stack.enter_context(self.nc.Block())
        for key, attr in (("sy", "sync"), ("v", "vector"), ("sc", "scalar"),
                          ("t", "tensor"), ("gp", "gpsimd")):
            ops = self.q[key]
            if ops:
                getattr(block, attr)(lambda e, ops=ops: [fn(e) for fn in ops])


BASE_CONSTS = ("ones_col", "ones_row", "eps_ln")
DRAM_ONLY_CONSTS = ("e32", "e16x", "e176a", "e176b", "ehb16")


def declare_consts(nc, em, need=(), dram_only=DRAM_ONLY_CONSTS):
    aps = {}
    keep = set(BASE_CONSTS) | set(need)
    for name, (shape, dt) in CONST_SPECS.items():
        if name not in keep:
            continue
        if name in dram_only:
            aps["@" + name] = nc.dram_tensor("c_" + name, list(shape), dt,
                                             kind="ExternalInput").ap()
            aps["#" + name] = (shape, dt)
            continue
        dram = nc.dram_tensor("c_" + name, list(shape), dt, kind="ExternalInput").ap()
        sdt = BF16 if name == "ident_bf16" else dt
        sbt = em.sb("cs_" + name, shape, sdt)
        if sdt != dt:
            em.opi("gp", lambda e, d=sbt, s=dram: e.dma_start(out=d[:], in_=s), "c_ldg", 16)
        else:
            em.opi("sy", lambda e, d=sbt, s=dram: e.dma_start(out=d[:], in_=s), "c_ld", 16)
        aps[name] = sbt
    oc_b = em.sb("cs_ones_col_b", [128, 1], BF16)
    em.wait("v", "c_ld", em.cntv("c_ld"))
    em.opi("v", lambda e, d=oc_b, s=aps["ones_col"]: e.tensor_copy(out=d[:], in_=s[:]),
           "c_prep", 1)
    aps["ones_col_b"] = oc_b
    return aps


_PAT_N = [0]


def load_pattern(em, C, name):
    shape, dt = C["#" + name]
    _PAT_N[0] += 1
    t = em.sb(f"pat_{name}_{_PAT_N[0]}", list(shape), dt)
    em.opi("sy", lambda e, t=t, s=C["@" + name]: e.dma_start(out=t[:], in_=s), "xld", 16)
    n = em.cntv("xld")
    for ek in ("t", "v", "sc"):
        em.wait(ek, "xld", n)
    return t



def emit_ln_tile(em, C, xt, xcb, tag, extra_wait=(), out_f32=False, t1ap=None,
                 xb16ap=None, xsqbap=None, eps_key="eps_ln",
                 ln_banks=("ps_lnS", "ps_lnB", "ps_lnR")):
    nc = em.nc
    xb16 = _T(xb16ap) if xb16ap is not None else em.sb_global("ln_xb16", [128, ND * TT], BF16)
    xsqb = _T(xsqbap) if xsqbap is not None else em.sb_global("ln_xsqb", [128, ND * TT], BF16)
    st_s1 = em.sb_global("ln_s1", [1, TT], F32)
    st_var = em.sb_global("ln_var", [1, TT], F32)
    st_rstd = em.sb_global("ln_rstd", [1, TT], F32)
    scr = em.sb_global("ln_scr", [1, TT], F32)
    t1 = None if t1ap is not None else em.sb_global("ln_t1", [128, ND * TT], F32)
    psS = em.ps(ln_banks[0])
    psB = em.ps(ln_banks[1])
    psR = em.ps(ln_banks[2])

    em.opi("v", lambda e: e.tensor_copy(out=xb16[:], in_=xt[:]), "dve")
    v_xb = em.cntv("dve")
    em.opi("sc", lambda e: e.activation(out=xsqb[:], in_=xt[:], func=AF.Square), "act")
    a_sq = em.cntv("act")

    em.wait("t", "dve", v_xb)
    em.wait("t", "act", a_sq)
    oc = C["ones_col_b"]
    for i in range(ND):
        fn = lambda e, i=i: e.matmul(psS[0:1, 0:TT], oc[:], xb16[:, i * TT:(i + 1) * TT],
                                     start=(i == 0), stop=(i == ND - 1))
        if i == ND - 1:
            em.opi("t", fn, "pe")
        else:
            em.op("t", fn)
    for i in range(ND):
        fn = lambda e, i=i: e.matmul(psS[32:33, 0:TT], oc[:], xsqb[:, i * TT:(i + 1) * TT],
                                     start=(i == 0), stop=(i == ND - 1))
        if i == ND - 1:
            em.opi("t", fn, "pe")
        else:
            em.op("t", fn)
    pe_stats = em.cntv("pe")

    em.wait("v", "pe", pe_stats)
    em.opi("v", lambda e: e.tensor_copy(out=st_s1[:], in_=psS[0:1, 0:TT]), "dve")
    em.vbar()
    em.opi("v", lambda e: e.tensor_mul(out=scr[:], in0=st_s1[:], in1=st_s1[:]), "dve")
    em.vbar()
    em.opi("v", lambda e: e.tensor_tensor(out=st_var[:], in0=psS[32:33, 0:TT], in1=scr[:],
                                          op=ALU.subtract), "dve")
    v_var = em.vbar()
    em.wait("sc", "dve", v_var)
    em.opi("sc", lambda e: e.activation(out=scr[:], in_=st_var[:], func=AF.Ln,
                                        bias=C[eps_key][:]), "act")
    em.wait("sc", "act", em.cntv("act"))
    em.opi("sc", lambda e: e.activation(out=st_rstd[:], in_=scr[:], func=AF.Exp,
                                        scale=-0.5), "act")
    a_rstd = em.cntv("act")

    orow = C["ones_row"]
    em.wait("t", "dve", v_var)
    em.opi("t", lambda e: e.matmul(psB[:, 0:TT], orow[:], st_s1[:], start=True, stop=True),
           "pe")
    em.wait("t", "act", a_rstd)
    em.opi("t", lambda e: e.matmul(psR[:, 0:TT], orow[:], st_rstd[:], start=True, stop=True),
           "pe")
    pe_bcast = em.cntv("pe")

    t1v = t1ap if t1ap is not None else t1[:]
    em.wait("v", "pe", pe_bcast)
    for i in range(ND):
        sl = slice(i * TT, (i + 1) * TT)
        em.op("v", lambda e, sl=sl: e.tensor_tensor(
            out=t1v[:, sl], in0=xt[:, sl],
            in1=psB[:, 0:TT], op=ALU.subtract))
        em.opi("v", lambda e, sl=sl: e.tensor_tensor(
            out=xcb[:, sl], in0=t1v[:, sl], in1=psR[:, 0:TT], op=ALU.mult), "dve")
    return em.cntv("dve")



def emit_load_x_tile(em, xt, xin_ap, tt):
    src = xin_ap.rearrange("(i p) t -> p i t", p=128)[:, :, tt * TT:(tt + 1) * TT]
    dst = xt[:].rearrange("p (i t) -> p i t", t=TT)
    return em.opi("sy", lambda e: e.dma_start(out=dst, in_=src), "xld", 16)


def emit_store_x_tile(em, xt, xout_ap, tt):
    dst = xout_ap.rearrange("(i p) t -> p i t", p=128)[:, :, tt * TT:(tt + 1) * TT]
    src = xt[:].rearrange("p (i t) -> p i t", t=TT)
    return em.opi("sy", lambda e: e.dma_start(out=dst, in_=src), "xst", 16)



def emit_l1_ffn(em, C, xin_ap, xout_ap, wg_i32, wu_i32, wd_i32, tag="l1", wstate=None):
    nc = em.nc
    H = 3072
    nJ = H // 128
    if wstate is not None and "wgb" in wstate:
        wgb, wub, wdb = wstate["wgb"], wstate["wub"], wstate["wdb"]
    else:
        wgb = nc.dram_tensor(tag + "_wgb", [nJ, 4, 128, 256], BF16, kind="Internal").ap()
        wub = nc.dram_tensor(tag + "_wub", [nJ, 4, 128, 256], BF16, kind="Internal").ap()
        wdb = nc.dram_tensor(tag + "_wdb", [ND, nJ // 2, 128, 256], BF16, kind="Internal").ap()
        for dst, srcw in ((wgb, wg_i32), (wub, wu_i32), (wdb, wd_i32)):
            em.opi("gp", lambda e, d=dst, s=srcw: e.dma_start(out=d, in_=s), "wcast", 16)
        if wstate is not None:
            wstate.update(wgb=wgb, wub=wub, wdb=wdb)
    em.wait("sy", "wcast", em.cntv("wcast"))

    xt = em.sb(tag + "_xt", [128, ND * TT], F32)
    xcb = _T(em.carve(0, ND * TT, BF16))
    hbuf = _T(em.carve(8192, nJ * TT, BF16))
    abuf = [em.sb(tag + f"_a{i}", [128, TT], F32) for i in range(2)]
    wbufA = [em.sb(tag + f"_wA{i}", [128, 8 * 128], BF16) for i in range(2)]
    wbufB = [em.sb(tag + f"_wB{i}", [128, 8 * 128], BF16) for i in range(2)]
    wbufD = [em.sb(tag + f"_wD{i}", [128, nJ * 128], BF16) for i in range(2)]
    psG = [em.ps("ps_g0"), em.ps("ps_g1")]
    psU = [em.ps("ps_u0"), em.ps("ps_u1")]

    wA_free = [0, 0]; wB_free = [0, 0]; wD_free = [0, 0]
    psG_ev = [0, 0]
    psU_ev = [0, 0]
    a_free = [0, 0]

    for tt in range(NTT):
        if tt > 0:
            em.wait("sy", "xst", em.cntv("xst"))
        x_ld = emit_load_x_tile(em, xt, xin_ap, tt)
        em.wait("v", "xld", x_ld)
        em.wait("sc", "xld", x_ld)
        if tt > 0:
            em.wait("v", "pe", em.cntv("pe"))
        emit_ln_tile(em, C, xt, xcb, f"{tag}_{tt}",
                     t1ap=em.carve(49152, ND * TT, F32),
                     xb16ap=em.carve(32768, ND * TT, BF16),
                     xsqbap=em.carve(40960, ND * TT, BF16))
        xcb_ready = em.cntv("dve")
        em.wait("t", "dve", xcb_ready)

        for j in range(nJ):
            b = j % 2
            em.wait("sy", "pe", wA_free[b])
            wlA = em.opi("sy", lambda e, j=j, b=b: e.dma_start(
                out=wbufA[b][:].rearrange("p (k o) -> p k o", o=256),
                in_=wgb[j].rearrange("k p o -> p k o")), "wld", 16)
            wlB = em.opi("sy", lambda e, j=j, b=b: e.dma_start(
                out=wbufB[b][:].rearrange("p (k o) -> p k o", o=256),
                in_=wub[j].rearrange("k p o -> p k o")), "wld", 16)
            em.wait("t", "wld", wlB)
            em.wait("t", "act", psG_ev[b])
            em.wait("t", "dve", psU_ev[b])
            for i in range(ND):
                fn = lambda e, b=b, i=i: e.matmul(
                    psG[b][:, 0:TT], wbufA[b][:, i * 128:(i + 1) * 128],
                    xcb[:, i * TT:(i + 1) * TT],
                    start=(i == 0), stop=(i == ND - 1))
                peg = em.opi("t", fn, "pe") if i == ND - 1 else em.op("t", fn)
            for i in range(ND):
                fn = lambda e, b=b, i=i: e.matmul(
                    psU[b][:, 0:TT], wbufB[b][:, i * 128:(i + 1) * 128],
                    xcb[:, i * TT:(i + 1) * TT],
                    start=(i == 0), stop=(i == ND - 1))
                peu = em.opi("t", fn, "pe") if i == ND - 1 else em.op("t", fn)
            wA_free[b] = wB_free[b] = peu
            em.wait("sc", "pe", peg)
            em.wait("sc", "dve", a_free[b])
            sca = em.opi("sc", lambda e, b=b: e.activation(
                out=abuf[b][:], in_=psG[b][:, 0:TT], func=AF.Silu,
                scale=1.0 / 127.0), "act")
            psG_ev[b] = sca
            em.wait("v", "pe", peu)
            em.wait("v", "act", sca)
            hj = em.opi("v", lambda e, j=j, b=b: e.scalar_tensor_tensor(
                out=hbuf[:, j * TT:(j + 1) * TT], in0=psU[b][:, 0:TT],
                scalar=1.0 / 127.0, in1=abuf[b][:], op0=ALU.mult, op1=ALU.mult), "dve")
            psU_ev[b] = a_free[b] = hj
        h_ready = em.cntv("dve")

        em.wait("t", "dve", h_ready)
        psD_ev = psG_ev
        for jd in range(ND):
            b = jd % 2
            em.wait("sy", "pe", wD_free[b])
            wlD = em.opi("sy", lambda e, jd=jd, b=b: e.dma_start(
                out=wbufD[b][:].rearrange("p (k o) -> p k o", o=256),
                in_=wdb[jd].rearrange("k p o -> p k o")), "wld", 16)
            em.wait("t", "wld", wlD)
            em.wait("t", "act", psD_ev[b])
            em.wait("t", "dve", psU_ev[b])
            for i in range(nJ):
                fn = lambda e, b=b, i=i: e.matmul(
                    psG[b][:, 0:TT], wbufD[b][:, i * 128:(i + 1) * 128],
                    hbuf[:, i * TT:(i + 1) * TT],
                    start=(i == 0), stop=(i == nJ - 1))
                ped = em.opi("t", fn, "pe") if i == nJ - 1 else em.op("t", fn)
            wD_free[b] = ped
            em.wait("v", "pe", ped)
            ev = em.opi("v", lambda e, jd=jd, b=b: e.scalar_tensor_tensor(
                out=xt[:, jd * TT:(jd + 1) * TT], in0=psG[b][:, 0:TT],
                scalar=0.5 / 127.0, in1=xt[:, jd * TT:(jd + 1) * TT],
                op0=ALU.mult, op1=ALU.add), "dve")
            psU_ev[b] = ev
            psD_ev = [max(psD_ev[0], 0), max(psD_ev[1], 0)]
        em.wait("sy", "dve", em.cntv("dve"))
        emit_store_x_tile(em, xt, xout_ap, tt)
    em.wait("sy", "xst", em.cntv("xst"))



def emit_l0_ffn(em, C, xin_ap, xout_ap, w1p, w2p, h1p, h2p, w3p,
                s1_ap, s2_ap, sh1_ap, sh2_ap, s3_ap, tag="l0", wstate=None):
    nc = em.nc
    H = 4096
    nJ = H // 128
    if wstate is not None and "w1b" in wstate:
        w1b, w2b, h1b, h2b, w3b = (wstate["w1b"], wstate["w2b"], wstate["h1b"],
                                   wstate["h2b"], wstate["w3b"])
    else:
        w1b = nc.dram_tensor(tag + "_w1b", [nJ, 4, 128, 256], BF16, kind="Internal").ap()
        w2b = nc.dram_tensor(tag + "_w2b", [nJ, 4, 128, 256], BF16, kind="Internal").ap()
        h1b = nc.dram_tensor(tag + "_h1b", [nJ, 16, 128, 256], BF16, kind="Internal").ap()
        h2b = nc.dram_tensor(tag + "_h2b", [nJ, 16, 128, 256], BF16, kind="Internal").ap()
        w3b = nc.dram_tensor(tag + "_w3b", [ND, 16, 128, 256], BF16, kind="Internal").ap()
        for dst, srcw in ((w1b, w1p), (w2b, w2p), (h1b, h1p), (h2b, h2p), (w3b, w3p)):
            em.opi("gp", lambda e, d=dst, s=srcw: e.dma_start(out=d, in_=s), "wcast", 16)
        if wstate is not None:
            wstate.update(w1b=w1b, w2b=w2b, h1b=h1b, h2b=h2b, w3b=w3b)
    em.wait("sy", "wcast", em.cntv("wcast"))

    xt = em.sb(tag + "_xt", [128, ND * TT], F32)
    xcb = _T(em.carve(0, ND * TT, BF16))
    hbuf = _T(em.carve(8192, nJ * TT, BF16))
    cbuf = _T(em.carve(40960, nJ * TT, BF16))
    abuf = [em.sb(tag + f"_a{i}", [128, TT], F32) for i in range(2)]
    s1sb = em.sb(tag + "_s1", [128, nJ], F32)
    s2sb = em.sb(tag + "_s2", [128, nJ], F32)
    sh1sb = em.sb(tag + "_sh1", [128, nJ], F32)
    sh2sb = em.sb(tag + "_sh2", [128, nJ], F32)
    s3sb = em.sb(tag + "_s3", [128, ND], F32)
    for sb_t, ap_v, n in ((s1sb, s1_ap, nJ), (s2sb, s2_ap, nJ), (sh1sb, sh1_ap, nJ),
                          (sh2sb, sh2_ap, nJ), (s3sb, s3_ap, ND)):
        for j in range(n):
            em.opi("sy", lambda e, sb_t=sb_t, ap_v=ap_v, j=j: e.dma_start(
                out=sb_t[:, j:j + 1],
                in_=ap_v[128 * j:128 * (j + 1)].rearrange("(p o) -> p o", o=1)), "xld", 16)
    sld = em.cntv("xld")
    em.wait("v", "xld", sld)
    em.opi("v", lambda e: e.tensor_scalar(out=s3sb[:], in0=s3sb[:], scalar1=0.5,
                                          scalar2=None, op0=ALU.mult), "dve")
    em.vbar()
    wbuf1 = [em.sb(tag + f"_wb1{i}", [128, 8 * 128], BF16) for i in range(2)]
    wbuf2 = [em.sb(tag + f"_wb2{i}", [128, 8 * 128], BF16) for i in range(2)]
    wbufH = [em.sb(tag + f"_wbh{i}", [128, 32 * 128], BF16) for i in range(2)]
    psA = [em.ps("ps_g0"), em.ps("ps_g1")]
    psB = [em.ps("ps_u0"), em.ps("ps_u1")]

    wz_free = {id(b): 0 for b in wbuf1 + wbuf2 + wbufH}
    psA_ev = [0, 0]
    psB_ev = [0, 0]
    a_free = [0, 0]

    def mm_group(bank, wbuf, rhs_buf, nK, j):
        last = None
        for i in range(nK):
            fn = lambda e, bank=bank, wbuf=wbuf, rhs_buf=rhs_buf, i=i: e.matmul(
                bank[:, 0:TT], wbuf[:, i * 128:(i + 1) * 128],
                rhs_buf[:, i * TT:(i + 1) * TT],
                start=(i == 0), stop=(i == nK - 1))
            if i == nK - 1:
                last = em.opi("t", fn, "pe")
            else:
                em.op("t", fn)
        return last

    for tt in range(NTT):
        if tt > 0:
            em.wait("sy", "xst", em.cntv("xst"))
        x_ld = emit_load_x_tile(em, xt, xin_ap, tt)
        em.wait("v", "xld", x_ld)
        em.wait("sc", "xld", x_ld)
        if tt > 0:
            em.wait("v", "pe", em.cntv("pe"))
        emit_ln_tile(em, C, xt, xcb, f"{tag}_{tt}",
                     t1ap=em.carve(90112, ND * TT, F32),
                     xb16ap=em.carve(73728, ND * TT, BF16),
                     xsqbap=em.carve(81920, ND * TT, BF16))
        em.wait("t", "dve", em.cntv("dve"))

        for j in range(nJ):
            b = j % 2
            em.wait("sy", "pe", max(wz_free[id(wbuf1[b])], wz_free[id(wbuf2[b])]))
            em.opi("sy", lambda e, j=j, b=b: e.dma_start(
                out=wbuf1[b][:].rearrange("p (k o) -> p k o", o=256),
                in_=w1b[j].rearrange("k p o -> p k o")), "wld", 16)
            wl = em.opi("sy", lambda e, j=j, b=b: e.dma_start(
                out=wbuf2[b][:].rearrange("p (k o) -> p k o", o=256),
                in_=w2b[j].rearrange("k p o -> p k o")), "wld", 16)
            em.wait("t", "wld", wl)
            em.wait("t", "act", psA_ev[b])
            em.wait("t", "dve", psB_ev[b])
            peg = mm_group(psA[b], wbuf1[b], xcb, ND, j)
            peu = mm_group(psB[b], wbuf2[b], xcb, ND, j)
            wz_free[id(wbuf1[b])] = wz_free[id(wbuf2[b])] = peu
            em.wait("sc", "pe", peg)
            em.wait("sc", "dve", a_free[b])
            sca = em.opi("sc", lambda e, j=j, b=b: e.activation(
                out=abuf[b][:], in_=psA[b][:, 0:TT], func=AF.Silu,
                scale=s1sb[:, j:j + 1]), "act")
            psA_ev[b] = sca
            em.wait("v", "pe", peu)
            em.wait("v", "act", sca)
            hj = em.opi("v", lambda e, j=j, b=b: e.scalar_tensor_tensor(
                out=hbuf[:, j * TT:(j + 1) * TT], in0=psB[b][:, 0:TT],
                scalar=s2sb[:, j:j + 1], in1=abuf[b][:],
                op0=ALU.mult, op1=ALU.mult), "dve")
            psB_ev[b] = a_free[b] = hj

        em.wait("t", "dve", em.cntv("dve"))
        for j in range(nJ):
            b = j % 2
            em.wait("sy", "pe", wz_free[id(wbufH[b])])
            wl = em.opi("sy", lambda e, j=j, b=b: e.dma_start(
                out=wbufH[b][:].rearrange("p (k o) -> p k o", o=256),
                in_=h1b[j].rearrange("k p o -> p k o")), "wld", 16)
            em.wait("t", "wld", wl)
            em.wait("t", "act", psA_ev[b])
            pej = mm_group(psA[b], wbufH[b], hbuf, nJ, j)
            wz_free[id(wbufH[b])] = pej
            em.wait("sc", "pe", pej)
            sca = em.opi("sc", lambda e, j=j, b=b: e.activation(
                out=cbuf[:, j * TT:(j + 1) * TT], in_=psA[b][:, 0:TT], func=AF.Gelu,
                scale=sh1sb[:, j:j + 1]), "act")
            psA_ev[b] = sca

        em.wait("t", "act", em.cntv("act"))
        for j in range(nJ):
            b = j % 2
            em.wait("sy", "pe", wz_free[id(wbufH[b])])
            wl = em.opi("sy", lambda e, j=j, b=b: e.dma_start(
                out=wbufH[b][:].rearrange("p (k o) -> p k o", o=256),
                in_=h2b[j].rearrange("k p o -> p k o")), "wld", 16)
            em.wait("t", "wld", wl)
            em.wait("t", "act", psA_ev[b])
            pej = mm_group(psA[b], wbufH[b], cbuf, nJ, j)
            wz_free[id(wbufH[b])] = pej
            em.wait("sc", "pe", pej)
            sca = em.opi("sc", lambda e, j=j, b=b: e.activation(
                out=hbuf[:, j * TT:(j + 1) * TT], in_=psA[b][:, 0:TT], func=AF.Gelu,
                scale=sh2sb[:, j:j + 1]), "act")
            psA_ev[b] = sca

        em.wait("t", "act", em.cntv("act"))
        for jd in range(ND):
            b = jd % 2
            em.wait("sy", "pe", wz_free[id(wbufH[b])])
            wl = em.opi("sy", lambda e, jd=jd, b=b: e.dma_start(
                out=wbufH[b][:, 0:16 * 256].rearrange("p (k o) -> p k o", o=256),
                in_=w3b[jd].rearrange("k p o -> p k o")), "wld", 16)
            em.wait("t", "wld", wl)
            em.wait("t", "act", psA_ev[b])
            em.wait("t", "dve", psB_ev[b])
            pej = mm_group(psA[b], wbufH[b], hbuf, nJ, jd)
            wz_free[id(wbufH[b])] = pej
            em.wait("v", "pe", pej)
            ev = em.opi("v", lambda e, jd=jd, b=b: e.scalar_tensor_tensor(
                out=xt[:, jd * TT:(jd + 1) * TT], in0=psA[b][:, 0:TT],
                scalar=s3sb[:, jd:jd + 1], in1=xt[:, jd * TT:(jd + 1) * TT],
                op0=ALU.mult, op1=ALU.add), "dve")
            psB_ev[b] = ev
            psA_ev[b] = em.cntv("act")
        em.wait("sy", "dve", em.cntv("dve"))
        emit_store_x_tile(em, xt, xout_ap, tt)
    em.wait("sy", "xst", em.cntv("xst"))



def load_pp_vec(em, sbuf_t, ap_v, n, sem="xld"):
    for j in range(n):
        em.opi("sy", lambda e, j=j: e.dma_start(
            out=sbuf_t[:, j:j + 1],
            in_=ap_v[128 * j:128 * (j + 1)].rearrange("(p o) -> p o", o=1)), sem, 16)
    return em.cntv(sem)


def emit_l0_attn(em, C, xin_ap, xout_ap, wqp, wkp, wvp, wop,
                 bq_ap, bk_ap, bv_ap, bo_ap, alpha_ap, augp_ap,
                 lns_ap, lnb_ap, tag="l0a"):
    nc = em.nc
    NO = 4
    wq = em.sb(tag + "_wq", [128, NO * 1024], BF16)
    wk = em.sb(tag + "_wk", [128, NO * 1024], BF16)
    wv = em.sb(tag + "_wv", [128, NO * 1024], BF16)
    wo = em.sb(tag + "_wo", [128, ND * 512], BF16)
    for dst, srcw, n in ((wq, wqp, NO), (wk, wkp, NO), (wv, wvp, NO), (wo, wop, ND)):
        em.opi("gp", lambda e, dst=dst, srcw=srcw: e.dma_start(
            out=dst[:].rearrange("p (j kk c) -> p j kk c",
                                 j=srcw.tensor.shape[0], c=256),
            in_=srcw.rearrange("j kk p c -> p j kk c")), "wcast", 16)
    wc = em.cntv("wcast")
    bqc = em.sb(tag + "_bq", [128, NO], F32)
    bkc = em.sb(tag + "_bk", [128, NO], F32)
    bvc = em.sb(tag + "_bv", [128, NO], F32)
    boc = em.sb(tag + "_bo", [128, ND], F32)
    alc = em.sb(tag + "_al", [128, NO], F32)
    agc = em.sb(tag + "_ag", [128, NO], F32)
    lsc = em.sb(tag + "_ls", [128, ND], F32)
    lbc = em.sb(tag + "_lb", [128, ND], F32)
    for t_, a_, n_ in ((bqc, bq_ap, NO), (bkc, bk_ap, NO), (bvc, bv_ap, NO),
                       (boc, bo_ap, ND), (alc, alpha_ap, NO), (agc, augp_ap, NO),
                       (lsc, lns_ap, ND), (lbc, lnb_ap, ND)):
        load_pp_vec(em, t_, a_, n_)
    vld = em.cntv("xld")
    dec = em.sb(tag + "_dec", [128, NO], F32)
    omd = em.sb(tag + "_omd", [128, NO], F32)
    bvo = em.sb(tag + "_bvo", [128, NO], F32)
    dec_rep = em.sb(tag + "_decr", [128, NO * TT], F32)
    em.wait("sc", "xld", vld)
    em.opi("sc", lambda e: e.activation(out=dec[:], in_=alc[:], func=AF.Sigmoid), "act")
    em.wait("v", "act", em.cntv("act"))
    em.opi("v", lambda e: e.tensor_scalar(out=omd[:], in0=dec[:], scalar1=-1.0,
                                          scalar2=1.0, op0=ALU.mult, op1=ALU.add), "dve")
    em.vbar()
    em.opi("v", lambda e: e.tensor_mul(out=bvo[:], in0=bvc[:], in1=omd[:]), "dve")
    vb = em.vbar()
    em.opi("gp", lambda e: e.memset(dec_rep[:], 0.0), "gpd", 1)
    em.wait("sc", "dve", vb)
    em.wait("sc", "gpd", em.cntv("gpd"))
    for i in range(NO):
        em.opi("sc", lambda e, i=i: e.activation(
            out=dec_rep[:, i * TT:(i + 1) * TT], in_=dec_rep[:, i * TT:(i + 1) * TT],
            func=AF.Identity, scale=0.0, bias=dec[:, i:i + 1]), "act")
    dr = em.cntv("act")

    xt = em.sb(tag + "_xt", [128, ND * TT], F32)
    xcb = _T(em.carve(0, ND * TT, BF16))
    qb = _T(em.carve(8192, NO * TT, F32))
    kb = _T(em.carve(16384, NO * TT, F32))
    vp = _T(em.carve(24576, NO * TT, F32))
    ub = _T(em.carve(32768, NO * TT, F32))
    stb = _T(em.carve(40960, NO * TT, F32))
    oib = _T(em.carve(49152, NO * TT, BF16))
    outb = _T(em.carve(53248, ND * TT, F32))
    ocn = _T(em.carve(69632, ND * TT, F32))
    prod = _T(em.carve(86016, NO * TT, F32))
    carry = em.sb(tag + "_carry", [128, NO], F32)
    probs = em.sb(tag + "_probs", [128, 4 * 8], F32)
    attnf = em.sb(tag + "_attnf", [128, 4 * 8], F32)
    attnT = em.sb(tag + "_attnT", [8, TT], F32)
    nmx = em.sb(tag + "_nmx", [128, 1], F32)
    ssum = em.sb(tag + "_ssum", [128, 1], F32)
    psQ = [em.ps("ps_g0"), em.ps("ps_g1")]
    psS = em.ps("ps_u0")
    psB = em.ps("ps_u1")

    em.wait("t", "wcast", wc)
    em.wait("t", "act", dr)

    for tt in range(NTT):
        if tt > 0:
            em.wait("sy", "xst", em.cntv("xst"))
        x_ld = emit_load_x_tile(em, xt, xin_ap, tt)
        em.wait("v", "xld", x_ld)
        em.wait("sc", "xld", x_ld)
        if tt > 0:
            em.wait("v", "pe", em.cntv("pe"))
        emit_ln_tile(em, C, xt, xcb, f"{tag}_{tt}",
                     t1ap=em.carve(102400, ND * TT, F32),
                     xb16ap=em.carve(53248, ND * TT, BF16),
                     xsqbap=em.carve(61440, ND * TT, BF16))
        em.wait("t", "dve", em.cntv("dve"))

        for (wbuf, dstb, bias_t, scale_t) in ((wq, qb, bqc, None), (wk, kb, bkc, None),
                                              (wv, vp, bvo, omd)):
            for j in range(NO):
                b = j % 2
                em.wait("t", "act", em.cntv("act") if tt == 0 and j < 2 else em.cntv("act"))
                for i in range(ND):
                    fn = lambda e, wbuf=wbuf, j=j, b=b, i=i: e.matmul(
                        psQ[b][:, 0:TT], wbuf[:, j * 1024 + i * 128: j * 1024 + (i + 1) * 128],
                        xcb[:, i * TT:(i + 1) * TT], start=(i == 0), stop=(i == ND - 1))
                    pj = em.opi("t", fn, "pe") if i == ND - 1 else em.op("t", fn)
                em.wait("sc", "pe", pj)
                if scale_t is None:
                    em.opi("sc", lambda e, dstb=dstb, bias_t=bias_t, j=j, b=b: e.activation(
                        out=dstb[:, j * TT:(j + 1) * TT], in_=psQ[b][:, 0:TT],
                        func=AF.Identity, bias=bias_t[:, j:j + 1]), "act")
                else:
                    em.opi("sc", lambda e, dstb=dstb, bias_t=bias_t, scale_t=scale_t, j=j, b=b:
                           e.activation(
                        out=dstb[:, j * TT:(j + 1) * TT], in_=psQ[b][:, 0:TT],
                        func=AF.Identity, scale=scale_t[:, j:j + 1],
                        bias=bias_t[:, j:j + 1]), "act")
        qkv_done = em.cntv("act")

        em.wait("v", "act", qkv_done)
        for i in range(NO):
            em.opi("v", lambda e, i=i: e.tensor_tensor(
                out=prod[:, i * TT:(i + 1) * TT], in0=qb[:, i * TT:(i + 1) * TT],
                in1=kb[:, i * TT:(i + 1) * TT], op=ALU.mult), "dve")
        pr = em.cntv("dve")
        em.wait("t", "dve", pr)
        for c in range(4):
            for i in range(NO):
                fn = lambda e, c=c, i=i: e.matmul(
                    psS[0:128, 8 * c:8 * (c + 1)], prod[:, i * TT + 128 * c: i * TT + 128 * (c + 1)],
                    C["ehq"][:, 8 * i:8 * (i + 1)], start=(i == 0), stop=(i == NO - 1))
                ps_ = em.opi("t", fn, "pe") if i == NO - 1 else em.op("t", fn)
        em.wait("v", "pe", em.cntv("pe"))
        em.wait("sc", "pe", em.cntv("pe"))
        for c in range(4):
            em.opi("v", lambda e, c=c: e.tensor_reduce(
                out=nmx[:], in_=psS[0:128, 8 * c:8 * (c + 1)], axis=AX.X, op=ALU.max,
                negate=True), "dve")
            nm = em.vbar()
            em.wait("sc", "dve", nm)
            ex = em.opi("sc", lambda e, c=c: e.activation(
                out=probs[:, 8 * c:8 * (c + 1)], in_=psS[0:128, 8 * c:8 * (c + 1)],
                func=AF.Exp, bias=nmx[:]), "act")
            em.wait("v", "act", ex)
            em.opi("v", lambda e, c=c: e.tensor_reduce(
                out=ssum[:], in_=probs[:, 8 * c:8 * (c + 1)], axis=AX.X, op=ALU.add), "dve")
            em.vbar()
            em.opi("v", lambda e: e.reciprocal(out=ssum[:], in_=ssum[:]), "dve")
            em.vbar()
            em.opi("v", lambda e, c=c: e.tensor_scalar(
                out=attnf[:, 8 * c:8 * (c + 1)], in0=probs[:, 8 * c:8 * (c + 1)],
                scalar1=ssum[:], scalar2=None, op0=ALU.mult), "dve")
        at = em.cntv("dve")
        em.wait("t", "dve", at)
        for c in range(4):
            em.opi("t", lambda e, c=c: e.transpose(
                psB[0:8, 128 * c:128 * (c + 1)], attnf[:, 8 * c:8 * (c + 1)],
                C["ident_f32"][:]), "pe")
        tp = em.cntv("pe")
        em.wait("v", "pe", tp)
        em.opi("v", lambda e: e.tensor_copy(out=attnT[:], in_=psB[0:8, 0:TT]), "dve")
        atT = em.vbar()
        em.wait("t", "dve", atT)
        for i in range(NO):
            bb = i % 2
            em.wait("t", "dve", em.cntv("dve"))
            pb = em.opi("t", lambda e, i=i, bb=bb: e.matmul(
                psQ[bb][:, 0:TT], C["ehb"][0:8, 128 * i:128 * (i + 1)], attnT[:],
                start=True, stop=True), "pe")
            em.wait("v", "pe", pb)
            em.opi("v", lambda e, i=i, bb=bb: e.tensor_tensor(
                out=ub[:, i * TT:(i + 1) * TT], in0=psQ[bb][:, 0:TT],
                in1=vp[:, i * TT:(i + 1) * TT], op=ALU.mult), "dve")
        u_done = em.cntv("dve")

        for i in range(NO):
            if tt == 0:
                em.opi("v", lambda e, i=i: e.tensor_tensor_scan(
                    out=stb[:, i * TT:(i + 1) * TT], data0=dec_rep[:, i * TT:(i + 1) * TT],
                    data1=ub[:, i * TT:(i + 1) * TT], initial=0.0,
                    op0=ALU.mult, op1=ALU.add), "dve")
            else:
                em.opi("v", lambda e, i=i: e.tensor_tensor_scan(
                    out=stb[:, i * TT:(i + 1) * TT], data0=dec_rep[:, i * TT:(i + 1) * TT],
                    data1=ub[:, i * TT:(i + 1) * TT], initial=carry[:, i:i + 1],
                    op0=ALU.mult, op1=ALU.add), "dve")
        em.vbar()
        for i in range(NO):
            em.opi("v", lambda e, i=i: e.tensor_copy(
                out=carry[:, i:i + 1], in_=stb[:, (i + 1) * TT - 1:(i + 1) * TT]), "dve")
        em.vbar()

        em.wait("sc", "dve", em.cntv("dve"))
        for i in range(NO):
            sl = slice(i * TT, (i + 1) * TT)
            em.opi("sc", lambda e, sl=sl, i=i: e.activation(
                out=prod[:, sl], in_=qb[:, sl], func=AF.Sigmoid,
                scale=agc[:, i:i + 1]), "act")
        ag = em.cntv("act")
        em.wait("v", "act", ag)
        for i in range(NO):
            sl = slice(i * TT, (i + 1) * TT)
            em.op("v", lambda e, sl=sl: e.tensor_tensor(
                out=ub[:, sl], in0=qb[:, sl], in1=stb[:, sl], op=ALU.mult))
            em.opi("v", lambda e, sl=sl: e.tensor_tensor(
                out=oib[:, sl], in0=ub[:, sl], in1=prod[:, sl], op=ALU.add), "dve")
        oi_done = em.cntv("dve")

        em.wait("t", "dve", oi_done)
        for j in range(ND):
            b = j % 2
            em.wait("t", "act", em.cntv("act"))
            for i in range(NO):
                fn = lambda e, j=j, b=b, i=i: e.matmul(
                    psQ[b][:, 0:TT], wo[:, j * 512 + i * 128: j * 512 + (i + 1) * 128],
                    oib[:, i * TT:(i + 1) * TT], start=(i == 0), stop=(i == NO - 1))
                pj = em.opi("t", fn, "pe") if i == NO - 1 else em.op("t", fn)
            em.wait("sc", "pe", pj)
            em.opi("sc", lambda e, j=j, b=b: e.activation(
                out=outb[:, j * TT:(j + 1) * TT], in_=psQ[b][:, 0:TT],
                func=AF.Identity, bias=boc[:, j:j + 1]), "act")
        ob_done = em.cntv("act")

        em.wait("v", "act", ob_done)
        em.wait("sc", "act", ob_done)
        emit_ln_tile(em, C, outb, ocn, f"{tag}_iln{tt}", out_f32=True,
                     t1ap=em.carve(102400, ND * TT, F32),
                     xb16ap=em.carve(0, ND * TT, BF16),
                     xsqbap=em.carve(8192, ND * TT, BF16),
                     eps_key="eps_ln2")
        for j in range(ND):
            sl = slice(j * TT, (j + 1) * TT)
            em.op("v", lambda e, sl=sl, j=j: e.tensor_scalar(
                out=ocn[:, sl], in0=ocn[:, sl], scalar1=lsc[:, j:j + 1],
                scalar2=lbc[:, j:j + 1], op0=ALU.mult, op1=ALU.add))
            em.opi("v", lambda e, sl=sl: e.tensor_tensor(
                out=xt[:, sl], in0=xt[:, sl], in1=ocn[:, sl], op=ALU.add), "dve")
        em.wait("sy", "dve", em.cntv("dve"))
        emit_store_x_tile(em, xt, xout_ap, tt)
    em.wait("sy", "xst", em.cntv("xst"))



def emit_l3_attn(em, C, xin_ap, xout_ap, wqp, wkp, wvp, wop,
                 bq_ap, bk_ap, bv_ap, bo_ap, tag="l3a"):
    nc = em.nc
    kv_loc = nc.dram_tensor(tag + "_kvloc", [2048, S], BF16, kind="Internal").ap()
    kv_all = nc.dram_tensor(tag + "_kvall", [8 * 2048, S], BF16, kind="Internal").ap()
    qloc = nc.dram_tensor(tag + "_qloc", [D, S], BF16, kind="Internal").ap()

    wq = em.sb(tag + "_wq", [128, ND * 1024], BF16)
    wk = em.sb(tag + "_wk", [128, ND * 1024], BF16)
    wv = em.sb(tag + "_wv", [128, ND * 1024], BF16)
    wo = em.sb(tag + "_wo", [128, ND * 1024], BF16)
    for dst, srcw in ((wq, wqp), (wk, wkp), (wv, wvp), (wo, wop)):
        em.opi("gp", lambda e, dst=dst, srcw=srcw: e.dma_start(
            out=dst[:].rearrange("p (j kk c) -> p j kk c", j=8, c=256),
            in_=srcw.rearrange("j kk p c -> p j kk c")), "wcast", 16)
    wc = em.cntv("wcast")
    bqc = em.sb(tag + "_bq", [128, ND], F32)
    bkc = em.sb(tag + "_bk", [128, ND], F32)
    bvc = em.sb(tag + "_bv", [128, ND], F32)
    boc = em.sb(tag + "_bo", [128, ND], F32)
    for t_, a_ in ((bqc, bq_ap), (bkc, bk_ap), (bvc, bv_ap), (boc, bo_ap)):
        load_pp_vec(em, t_, a_, ND)
    vld = em.cntv("xld")
    em.wait("v", "xld", vld)
    em.opi("v", lambda e: e.tensor_scalar(out=bqc[:], in0=bqc[:], scalar1=0.125,
                                          scalar2=None, op0=ALU.mult), "dve")
    em.vbar()
    ehb16f = load_pattern(em, C, "ehb16")
    ehq16b = em.sb(tag + "_ehqb", [128, 8 * 16], BF16)
    ehb16b = em.sb(tag + "_ehbb", [16, 8 * 128], BF16)
    em.wait("v", "c_ld", em.cntv("c_ld"))
    em.wait("v", "c_ldg", em.cntv("c_ldg"))
    em.opi("v", lambda e: e.tensor_copy(out=ehq16b[:], in_=C["ehq16"][:]), "dve")
    em.opi("v", lambda e: e.tensor_copy(out=ehb16b[:], in_=ehb16f[:]), "dve")
    epr = em.cntv("dve")

    xt = _T(em.carve(90112, ND * TT, F32))
    xcb = _T(em.carve(0, ND * TT, BF16))
    qtile = _T(em.carve(8192, ND * TT, BF16))
    ksb = [_T(em.carve(16384 + 8192 * i, ND * TT, BF16)) for i in range(2)]
    vsb = ksb
    ctx = _T(em.carve(32768, ND * TT, F32))
    ctxb = _T(em.carve(65536, ND * TT, BF16))
    prod = _T(em.carve(73728, ND * TT, BF16))
    sc_s = _T(em.carve(32768, 8 * TT, F32)[0:16, :])
    eb = _T(em.carve(81920, 8 * TT, BF16)[0:16, :])
    mxs = _T(em.carve(61440, TT, F32)[0:16, :])
    rsum = _T(em.carve(57344, TT, F32)[0:16, :])
    escr = _T(em.carve(59392, TT, F32)[0:16, :])
    psQ = [em.ps("ps_g0"), em.ps("ps_g1")]
    psS = em.ps("ps_u0")
    psB = [em.ps("ps_u1"), em.ps("ps_lnR")]

    em.wait("t", "wcast", wc)
    em.wait("t", "dve", epr)

    for ct in range(NTT):
        x_ld = emit_load_x_tile(em, xt, xin_ap, ct)
        em.wait("v", "xld", x_ld)
        em.wait("sc", "xld", x_ld)
        em.wait("v", "pe", em.cntv("pe"))
        emit_ln_tile(em, C, xt, xcb, f"{tag}_{ct}",
                     t1ap=em.carve(32768, ND * TT, F32),
                     xb16ap=em.carve(49152, ND * TT, BF16),
                     xsqbap=em.carve(57344, ND * TT, BF16))
        em.wait("t", "dve", em.cntv("dve"))
        for (wbuf, bias_t, is_q, dst_dram, row0) in (
                (wq, bqc, True, qloc, 0), (wk, bkc, False, kv_loc, 0),
                (wv, bvc, False, kv_loc, 1024)):
            ev = 0
            for j in range(ND):
                b = j % 2
                em.wait("t", "act", em.cntv("act"))
                for i in range(ND):
                    fn = lambda e, wbuf=wbuf, j=j, b=b, i=i: e.matmul(
                        psQ[b][:, 0:TT], wbuf[:, j * 1024 + i * 128: j * 1024 + (i + 1) * 128],
                        xcb[:, i * TT:(i + 1) * TT], start=(i == 0), stop=(i == ND - 1))
                    pj = em.opi("t", fn, "pe") if i == ND - 1 else em.op("t", fn)
                em.wait("sc", "pe", pj)
                em.wait("sc", "xst", em.cntv("xst"))
                if is_q:
                    ev = em.opi("sc", lambda e, j=j, b=b, bias_t=bias_t: e.activation(
                        out=qtile[:, j * TT:(j + 1) * TT], in_=psQ[b][:, 0:TT],
                        func=AF.Identity, scale=0.125, bias=bias_t[:, j:j + 1]), "act")
                else:
                    ev = em.opi("sc", lambda e, j=j, b=b, bias_t=bias_t: e.activation(
                        out=qtile[:, j * TT:(j + 1) * TT], in_=psQ[b][:, 0:TT],
                        func=AF.Identity, bias=bias_t[:, j:j + 1]), "act")
            em.wait("sy", "act", ev)
            if row0 == 0:
                dstap = dst_dram.rearrange("(i p) t -> p i t", p=128)[
                    :, 0:8, ct * TT:(ct + 1) * TT]
            else:
                dstap = dst_dram.rearrange("(i p) t -> p i t", p=128)[
                    :, 8:16, ct * TT:(ct + 1) * TT]
            em.opi("sy", lambda e, dstap=dstap: e.dma_start(
                out=dstap, in_=qtile[:].rearrange("p (i t) -> p i t", t=TT)), "xst", 16)
    em.wait("gp", "xst", em.cntv("xst"))

    cc = em.opi("gp", lambda e: e.collective_compute(
        "AllGather", ALU.bypass, ins=[kv_loc.opt()], outs=[kv_all.opt()],
        replica_groups=[[0, 1, 2, 3, 4, 5, 6, 7]]), "cc", 1)
    em.wait("sy", "cc", cc)

    kv_free = [0, 0]
    for ct in range(NTT):
        if ct > 0:
            em.wait("sy", "xst", em.cntv("xst"))
        x_ld = emit_load_x_tile(em, xt, xin_ap, ct)
        ql = em.opi("sy", lambda e, ct=ct: e.dma_start(
            out=qtile[:].rearrange("p (i t) -> p i t", t=TT),
            in_=qloc.rearrange("(i p) t -> p i t", p=128)[:, :, ct * TT:(ct + 1) * TT]),
            "xld", 16)
        for s in range(8):
            b = s % 2
            em.wait("sy", "pe", kv_free[b])
            kl = em.opi("sy", lambda e, s=s, ct=ct, b=b: e.dma_start(
                out=ksb[b][:].rearrange("p (i t) -> p i t", t=TT),
                in_=kv_all.rearrange("(g p) t -> p g t", p=128)[
                    :, 16 * s:16 * s + 8, ct * TT:(ct + 1) * TT]), "xld", 16)
            em.wait("v", "xld", max(ql, kl))
            em.wait("v", "pe", em.cntv("pe"))
            for i in range(ND):
                em.opi("v", lambda e, i=i, b=b: e.tensor_tensor(
                    out=prod[:, i * TT:(i + 1) * TT], in0=qtile[:, i * TT:(i + 1) * TT],
                    in1=ksb[b][:, i * TT:(i + 1) * TT], op=ALU.mult), "dve")
            prd = em.cntv("dve")
            em.wait("t", "dve", prd)
            for i in range(ND):
                fn = lambda e, i=i: e.matmul(
                    psS[0:16, 0:TT], ehq16b[:, 16 * i:16 * (i + 1)],
                    prod[:, i * TT:(i + 1) * TT], start=(i == 0), stop=(i == ND - 1))
                pss = em.opi("t", fn, "pe") if i == ND - 1 else em.op("t", fn)
            kv_free[b] = pss
            em.wait("v", "pe", pss)
            em.opi("v", lambda e, s=s: e.tensor_copy(
                out=sc_s[:, s * TT:(s + 1) * TT], in_=psS[0:16, 0:TT]), "dve")
        em.vbar()
        em.opi("v", lambda e: e.tensor_tensor(out=mxs[:], in0=sc_s[:, 0:TT],
                                              in1=sc_s[:, TT:2 * TT], op=ALU.max), "dve")
        for s in range(2, 8):
            em.opi("v", lambda e, s=s: e.tensor_tensor(
                out=mxs[:], in0=mxs[:], in1=sc_s[:, s * TT:(s + 1) * TT], op=ALU.max), "dve")
        em.vbar()
        for s in range(8):
            em.opi("v", lambda e, s=s: e.tensor_tensor(
                out=escr[:], in0=sc_s[:, s * TT:(s + 1) * TT], in1=mxs[:],
                op=ALU.subtract), "dve")
            eve = em.vbar()
            em.wait("sc", "dve", eve)
            ex = em.opi("sc", lambda e, s=s: e.activation(
                out=eb[:, s * TT:(s + 1) * TT], in_=escr[:], func=AF.Exp), "act")
            em.wait("v", "act", ex)
        em.opi("v", lambda e: e.tensor_tensor(out=rsum[:], in0=eb[:, 0:TT],
                                              in1=eb[:, TT:2 * TT], op=ALU.add), "dve")
        for s in range(2, 8):
            em.opi("v", lambda e, s=s: e.tensor_tensor(
                out=rsum[:], in0=rsum[:], in1=eb[:, s * TT:(s + 1) * TT], op=ALU.add), "dve")
        em.vbar()
        em.opi("v", lambda e: e.reciprocal(out=rsum[:], in_=rsum[:]), "dve")
        rs = em.vbar()
        v_free = [0, 0]
        for s in range(8):
            b = s % 2
            em.wait("sy", "pe", kv_free[b])
            em.wait("sy", "dve", v_free[b])
            vlb = em.opi("sy", lambda e, s=s, ct=ct, b=b: e.dma_start(
                out=vsb[b][:].rearrange("p (i t) -> p i t", t=TT),
                in_=kv_all.rearrange("(g p) t -> p g t", p=128)[
                    :, 16 * s + 8:16 * s + 16, ct * TT:(ct + 1) * TT]), "xld", 16)
            em.wait("t", "xld", vlb)
            em.wait("t", "dve", em.cntv("dve"))
            for i in range(ND):
                bb = i % 2
                em.wait("t", "dve", em.cntv("dve"))
                pb = em.opi("t", lambda e, s=s, i=i, bb=bb: e.matmul(
                    psB[bb][:, 0:TT], ehb16b[0:16, 128 * i:128 * (i + 1)],
                    eb[:, s * TT:(s + 1) * TT], start=True, stop=True), "pe")
                em.wait("v", "pe", pb)
                em.wait("v", "xld", vlb)
                if s == 0:
                    em.opi("v", lambda e, i=i, bb=bb, b=b: e.tensor_tensor(
                        out=ctx[:, i * TT:(i + 1) * TT], in0=psB[bb][:, 0:TT],
                        in1=vsb[b][:, i * TT:(i + 1) * TT], op=ALU.mult), "dve")
                else:
                    em.op("v", lambda e, i=i, bb=bb, b=b: e.tensor_tensor(
                        out=prod[:, 0:TT], in0=psB[bb][:, 0:TT],
                        in1=vsb[b][:, i * TT:(i + 1) * TT], op=ALU.mult))
                    em.opi("v", lambda e, i=i: e.tensor_tensor(
                        out=ctx[:, i * TT:(i + 1) * TT], in0=ctx[:, i * TT:(i + 1) * TT],
                        in1=prod[:, 0:TT], op=ALU.add), "dve")
            v_free[b] = em.cntv("dve")
        ctx_done = em.cntv("dve")
        em.wait("t", "dve", ctx_done)
        for i in range(ND):
            bb = i % 2
            em.wait("t", "dve", em.cntv("dve"))
            pb = em.opi("t", lambda e, i=i, bb=bb: e.matmul(
                psB[bb][:, 0:TT], ehb16f[0:16, 128 * i:128 * (i + 1)], rsum[:],
                start=True, stop=True), "pe")
            em.wait("v", "pe", pb)
            em.opi("v", lambda e, i=i, bb=bb: e.tensor_tensor(
                out=ctxb[:, i * TT:(i + 1) * TT], in0=ctx[:, i * TT:(i + 1) * TT],
                in1=psB[bb][:, 0:TT], op=ALU.mult), "dve")
        cn = em.cntv("dve")
        em.wait("t", "dve", cn)
        em.wait("v", "xld", x_ld)
        for j in range(ND):
            b = j % 2
            em.wait("t", "dve", em.cntv("dve"))
            for i in range(ND):
                fn = lambda e, j=j, b=b, i=i: e.matmul(
                    psQ[b][:, 0:TT], wo[:, j * 1024 + i * 128: j * 1024 + (i + 1) * 128],
                    ctxb[:, i * TT:(i + 1) * TT], start=(i == 0), stop=(i == ND - 1))
                pj = em.opi("t", fn, "pe") if i == ND - 1 else em.op("t", fn)
            em.wait("v", "pe", pj)
            em.op("v", lambda e, j=j, b=b: e.tensor_scalar(
                out=prod[:, 0:TT * 2].bitcast(F32),
                in0=psQ[b][:, 0:TT], scalar1=1.0, scalar2=boc[:, j:j + 1],
                op0=ALU.mult, op1=ALU.add))
            em.opi("v", lambda e, j=j: e.tensor_tensor(
                out=xt[:, j * TT:(j + 1) * TT], in0=xt[:, j * TT:(j + 1) * TT],
                in1=prod[:, 0:TT * 2].bitcast(F32), op=ALU.add), "dve")
        em.wait("sy", "dve", em.cntv("dve"))
        emit_store_x_tile(em, xt, xout_ap, ct)
    em.wait("sy", "xst", em.cntv("xst"))



def emit_l1_attn(em, C, xin_ap, xout_ap, wqkvp, bq_ap, bk_ap, bv_ap,
                 gam_ap, bet_ap, tag="l1a"):
    nc = em.nc
    kv_loc = nc.dram_tensor(tag + "_kvloc", [2048, S], BF16, kind="Internal").ap()
    kv_all = nc.dram_tensor(tag + "_kvall", [8 * 2048, S], BF16, kind="Internal").ap()
    qloc = nc.dram_tensor(tag + "_qloc", [D, S], BF16, kind="Internal").ap()

    wqkv = em.sb(tag + "_wqkv", [128, 24 * 1024], BF16)
    em.opi("gp", lambda e: e.dma_start(
        out=wqkv[:].rearrange("p (j kk c) -> p j kk c", j=24, c=256),
        in_=wqkvp.rearrange("j kk p c -> p j kk c")), "wcast", 16)
    wc = em.cntv("wcast")
    bqc = em.sb(tag + "_bq", [128, ND], F32)
    bkc = em.sb(tag + "_bk", [128, ND], F32)
    bvc = em.sb(tag + "_bv", [128, ND], F32)
    gmc = em.sb(tag + "_gm", [16, 1], F32)
    btc = em.sb(tag + "_bt", [16, 1], F32)
    for t_, a_ in ((bqc, bq_ap), (bkc, bk_ap), (bvc, bv_ap)):
        load_pp_vec(em, t_, a_, ND)
    em.opi("sy", lambda e: e.dma_start(
        out=gmc[:], in_=gam_ap.rearrange("(p o) -> p o", o=1)), "xld", 16)
    em.opi("sy", lambda e: e.dma_start(
        out=btc[:], in_=bet_ap.rearrange("(p o) -> p o", o=1)), "xld", 16)
    vld = em.cntv("xld")
    em.wait("v", "xld", vld)
    em.opi("v", lambda e: e.tensor_scalar(out=bqc[:], in0=bqc[:], scalar1=0.125,
                                          scalar2=None, op0=ALU.mult), "dve")
    em.vbar()
    ehb16f = load_pattern(em, C, "ehb16")
    ehq16b = em.sb(tag + "_ehqb", [128, 8 * 16], BF16)
    ehb16b = em.sb(tag + "_ehbb", [16, 8 * 128], BF16)
    em.wait("v", "c_ld", em.cntv("c_ld"))
    em.opi("v", lambda e: e.tensor_copy(out=ehq16b[:], in_=C["ehq16"][:]), "dve")
    em.opi("v", lambda e: e.tensor_copy(out=ehb16b[:], in_=ehb16f[:]), "dve")
    epr = em.cntv("dve")

    xt = _T(em.carve(90112, ND * TT, F32))
    xcb = _T(em.carve(0, ND * TT, BF16))
    qtile = _T(em.carve(8192, ND * TT, BF16))
    ksb = [_T(em.carve(16384 + 8192 * i, ND * TT, BF16)) for i in range(2)]
    vsb = ksb
    ctx = _T(em.carve(32768, ND * TT, F32))
    sc_s = _T(em.carve(49152, 8 * TT, F32)[0:16, :])
    pb16 = _T(em.carve(65536, 8 * TT, BF16)[0:16, :])
    prod = _T(em.carve(73728, ND * TT, BF16))
    pscr = _T(em.carve(81920, TT, F32)[0:16, :])
    pscr2 = _T(em.carve(84480 - 512, TT, F32)[0:16, :])
    psQ = [em.ps("ps_g0"), em.ps("ps_g1")]
    psS = em.ps("ps_u0")
    psB = [em.ps("ps_u1"), em.ps("ps_lnR")]

    em.wait("t", "wcast", wc)
    em.wait("t", "dve", epr)

    for ct in range(NTT):
        x_ld = emit_load_x_tile(em, xt, xin_ap, ct)
        em.wait("v", "xld", x_ld)
        em.wait("sc", "xld", x_ld)
        em.wait("v", "pe", em.cntv("pe"))
        emit_ln_tile(em, C, xt, xcb, f"{tag}_{ct}",
                     t1ap=em.carve(32768, ND * TT, F32),
                     xb16ap=em.carve(49152, ND * TT, BF16),
                     xsqbap=em.carve(57344, ND * TT, BF16))
        em.wait("t", "dve", em.cntv("dve"))
        for part in range(3):
            bias_t = (bqc, bkc, bvc)[part]
            ev = 0
            for j in range(ND):
                b = j % 2
                jj = part * ND + j
                em.wait("t", "act", em.cntv("act"))
                for i in range(ND):
                    fn = lambda e, jj=jj, b=b, i=i: e.matmul(
                        psQ[b][:, 0:TT], wqkv[:, jj * 1024 + i * 128: jj * 1024 + (i + 1) * 128],
                        xcb[:, i * TT:(i + 1) * TT], start=(i == 0), stop=(i == ND - 1))
                    pj = em.opi("t", fn, "pe") if i == ND - 1 else em.op("t", fn)
                em.wait("sc", "pe", pj)
                em.wait("sc", "xst", em.cntv("xst"))
                if part == 0:
                    ev = em.opi("sc", lambda e, j=j, b=b, bias_t=bias_t: e.activation(
                        out=qtile[:, j * TT:(j + 1) * TT], in_=psQ[b][:, 0:TT],
                        func=AF.Identity, scale=0.125, bias=bias_t[:, j:j + 1]), "act")
                else:
                    ev = em.opi("sc", lambda e, j=j, b=b, bias_t=bias_t: e.activation(
                        out=qtile[:, j * TT:(j + 1) * TT], in_=psQ[b][:, 0:TT],
                        func=AF.Identity, bias=bias_t[:, j:j + 1]), "act")
            em.wait("sy", "act", ev)
            if part == 0:
                dstap = qloc.rearrange("(i p) t -> p i t", p=128)[:, :, ct * TT:(ct + 1) * TT]
            elif part == 1:
                dstap = kv_loc.rearrange("(i p) t -> p i t", p=128)[:, 0:8, ct * TT:(ct + 1) * TT]
            else:
                dstap = kv_loc.rearrange("(i p) t -> p i t", p=128)[:, 8:16, ct * TT:(ct + 1) * TT]
            em.opi("sy", lambda e, dstap=dstap: e.dma_start(
                out=dstap, in_=qtile[:].rearrange("p (i t) -> p i t", t=TT)), "xst", 16)
    em.wait("gp", "xst", em.cntv("xst"))

    cc = em.opi("gp", lambda e: e.collective_compute(
        "AllGather", ALU.bypass, ins=[kv_loc.opt()], outs=[kv_all.opt()],
        replica_groups=[[0, 1, 2, 3, 4, 5, 6, 7]]), "cc", 1)
    em.wait("sy", "cc", cc)

    kv_free = [0, 0]
    for ct in range(NTT):
        if ct > 0:
            em.wait("sy", "xst", em.cntv("xst"))
        x_ld = emit_load_x_tile(em, xt, xin_ap, ct)
        ql = em.opi("sy", lambda e, ct=ct: e.dma_start(
            out=qtile[:].rearrange("p (i t) -> p i t", t=TT),
            in_=qloc.rearrange("(i p) t -> p i t", p=128)[:, :, ct * TT:(ct + 1) * TT]),
            "xld", 16)
        for s in range(8):
            b = s % 2
            em.wait("sy", "pe", kv_free[b])
            kl = em.opi("sy", lambda e, s=s, ct=ct, b=b: e.dma_start(
                out=ksb[b][:].rearrange("p (i t) -> p i t", t=TT),
                in_=kv_all.rearrange("(g p) t -> p g t", p=128)[
                    :, 16 * s:16 * s + 8, ct * TT:(ct + 1) * TT]), "xld", 16)
            em.wait("v", "xld", max(ql, kl))
            em.wait("v", "pe", em.cntv("pe"))
            for i in range(ND):
                em.opi("v", lambda e, i=i, b=b: e.tensor_tensor(
                    out=prod[:, i * TT:(i + 1) * TT], in0=qtile[:, i * TT:(i + 1) * TT],
                    in1=ksb[b][:, i * TT:(i + 1) * TT], op=ALU.mult), "dve")
            prd = em.cntv("dve")
            em.wait("t", "dve", prd)
            for i in range(ND):
                fn = lambda e, i=i: e.matmul(
                    psS[0:16, 0:TT], ehq16b[:, 16 * i:16 * (i + 1)],
                    prod[:, i * TT:(i + 1) * TT], start=(i == 0), stop=(i == ND - 1))
                pss = em.opi("t", fn, "pe") if i == ND - 1 else em.op("t", fn)
            kv_free[b] = pss
            em.wait("v", "pe", pss)
            em.opi("v", lambda e, s=s: e.tensor_copy(
                out=sc_s[:, s * TT:(s + 1) * TT], in_=psS[0:16, 0:TT]), "dve")
        em.vbar()
        for s in range(8):
            sl = slice(s * TT, (s + 1) * TT)
            em.op("v", lambda e, sl=sl: e.tensor_tensor(
                out=pscr[:], in0=sc_s[:, sl], in1=sc_s[:, sl], op=ALU.mult))
            em.op("v", lambda e: e.tensor_scalar(
                out=pscr2[:], in0=pscr[:], scalar1=1.0 / 480.0, scalar2=-1.0 / 48.0,
                op0=ALU.mult, op1=ALU.add))
            em.op("v", lambda e: e.tensor_tensor(
                out=pscr2[:], in0=pscr2[:], in1=pscr[:], op=ALU.mult))
            em.op("v", lambda e: e.tensor_scalar(
                out=pscr2[:], in0=pscr2[:], scalar1=1.0, scalar2=0.25,
                op0=ALU.mult, op1=ALU.add))
            em.op("v", lambda e, sl=sl: e.tensor_tensor(
                out=pscr2[:], in0=pscr2[:], in1=sc_s[:, sl], op=ALU.mult))
            em.op("v", lambda e: e.tensor_scalar(
                out=pscr2[:], in0=pscr2[:], scalar1=1.0, scalar2=0.5,
                op0=ALU.mult, op1=ALU.add))
            em.opi("v", lambda e, sl=sl: e.tensor_scalar(
                out=pb16[:, sl], in0=pscr2[:], scalar1=gmc[:], scalar2=btc[:],
                op0=ALU.mult, op1=ALU.add), "dve")
        em.vbar()
        v_free = [0, 0]
        for s in range(8):
            b = s % 2
            em.wait("sy", "pe", kv_free[b])
            em.wait("sy", "dve", v_free[b])
            vlb = em.opi("sy", lambda e, s=s, ct=ct, b=b: e.dma_start(
                out=vsb[b][:].rearrange("p (i t) -> p i t", t=TT),
                in_=kv_all.rearrange("(g p) t -> p g t", p=128)[
                    :, 16 * s + 8:16 * s + 16, ct * TT:(ct + 1) * TT]), "xld", 16)
            em.wait("t", "xld", vlb)
            em.wait("t", "dve", em.cntv("dve"))
            for i in range(ND):
                bb = i % 2
                em.wait("t", "dve", em.cntv("dve"))
                pb = em.opi("t", lambda e, s=s, i=i, bb=bb: e.matmul(
                    psB[bb][:, 0:TT], ehb16b[0:16, 128 * i:128 * (i + 1)],
                    pb16[:, s * TT:(s + 1) * TT], start=True, stop=True), "pe")
                em.wait("v", "pe", pb)
                em.wait("v", "xld", vlb)
                if s == 0:
                    em.opi("v", lambda e, i=i, bb=bb, b=b: e.tensor_tensor(
                        out=ctx[:, i * TT:(i + 1) * TT], in0=psB[bb][:, 0:TT],
                        in1=vsb[b][:, i * TT:(i + 1) * TT], op=ALU.mult), "dve")
                else:
                    em.op("v", lambda e, i=i, bb=bb, b=b: e.tensor_tensor(
                        out=prod[:, 0:TT], in0=psB[bb][:, 0:TT],
                        in1=vsb[b][:, i * TT:(i + 1) * TT], op=ALU.mult))
                    em.opi("v", lambda e, i=i: e.tensor_tensor(
                        out=ctx[:, i * TT:(i + 1) * TT], in0=ctx[:, i * TT:(i + 1) * TT],
                        in1=prod[:, 0:TT], op=ALU.add), "dve")
            v_free[b] = em.cntv("dve")
        em.wait("v", "xld", x_ld)
        for j in range(ND):
            em.opi("v", lambda e, j=j: e.tensor_tensor(
                out=xt[:, j * TT:(j + 1) * TT], in0=xt[:, j * TT:(j + 1) * TT],
                in1=ctx[:, j * TT:(j + 1) * TT], op=ALU.add), "dve")
        em.wait("sy", "dve", em.cntv("dve"))
        emit_store_x_tile(em, xt, xout_ap, ct)
    em.wait("sy", "xst", em.cntv("xst"))



def emit_l2_attn(em, C, xin_ap, xout_ap, wqp, wkp, wvp, wop,
                 bq_ap, bk_ap, bv_ap, bo_ap, cos_ap, sin_ap, tag="l2a", dbg=None):
    nc = em.nc
    NCH = TT // 128
    wbq = [em.sb(tag + f"_wbq{i}", [128, 8 * 128], BF16) for i in range(2)]
    wbk = [em.sb(tag + f"_wbk{i}", [128, 8 * 128], BF16) for i in range(2)]
    wbv = [em.sb(tag + f"_wbv{i}", [128, 8 * 128], BF16) for i in range(2)]
    wqb_d = nc.dram_tensor(tag + "_wqb", [8, 4, 128, 256], BF16, kind="Internal").ap()
    wkb_d = nc.dram_tensor(tag + "_wkb", [8, 4, 128, 256], BF16, kind="Internal").ap()
    wvb_d = nc.dram_tensor(tag + "_wvb", [8, 4, 128, 256], BF16, kind="Internal").ap()
    wob_d = nc.dram_tensor(tag + "_wob", [8, 4, 128, 256], BF16, kind="Internal").ap()
    for dst, srcw in ((wqb_d, wqp), (wkb_d, wkp), (wvb_d, wvp), (wob_d, wop)):
        em.opi("gp", lambda e, d=dst, s=srcw: e.dma_start(out=d, in_=s), "wcast", 16)
    em.wait("sy", "wcast", em.cntv("wcast"))
    bqc = em.sb(tag + "_bq", [128, ND], F32)
    bkc = em.sb(tag + "_bk", [128, ND], F32)
    bvc = em.sb(tag + "_bv", [128, ND], F32)
    boc = em.sb(tag + "_bo", [128, ND], F32)
    for t_, a_ in ((bqc, bq_ap), (bkc, bk_ap), (bvc, bv_ap), (boc, bo_ap)):
        load_pp_vec(em, t_, a_, ND)
    onec = em.sb(tag + "_onec", [128, 1], BF16)
    trib = C["tri"]
    em.wait("v", "c_ld", em.cntv("c_ld"))
    em.wait("v", "c_ldg", em.cntv("c_ldg"))
    em.opi("v", lambda e: e.tensor_copy(out=onec[:], in_=C["ones_col1"][:]), "dve")
    cpr = em.cntv("dve")

    xt = em.sb(tag + "_xt", [128, ND * TT], F32)
    xcb = _T(em.carve(0, ND * TT, BF16))
    Qc = _T(em.carve(8192, ND * TT, BF16))
    Qs = _T(em.carve(16384, ND * TT, BF16))
    Kc = _T(em.carve(24576, ND * TT, BF16))
    Ks = _T(em.carve(32768, ND * TT, BF16))
    Vb = _T(em.carve(40960, ND * TT, BF16))
    outf = _T(em.carve(49152, ND * TT, BF16))
    qp_scr = _T(em.carve(57344, TT, F32))
    costl = _T(em.carve(59392, TT, F32))
    sintl = _T(em.carve(61440, TT, F32))
    KcT = em.sb(tag + "_kct", [128, 8 * 128], BF16)
    KsT = em.sb(tag + "_kst", [128, 8 * 128], BF16)
    VT1 = em.sb(tag + "_vt1", [128, 16 * 65], BF16)
    Pm = em.sb(tag + "_pm", [128, 128], BF16)
    scc = em.sb(tag + "_scc", [128, 8 * 65], BF16)
    scs = em.sb(tag + "_scs2", [128, 8 * 65], BF16)
    sfc = em.sb(tag + "_sfc", [128, 512], F32)
    sfs = em.sb(tag + "_sfs", [128, 512], F32)
    zf = em.sb(tag + "_zf", [128, 16], F32)
    out_tm = em.sb(tag + "_otm", [128, 1024], BF16)
    dscr = em.sb(tag + "_dscr", [128, 1], F32)
    psQ = [em.ps("ps_g0"), em.ps("ps_g1")]
    psO = em.ps("ps_u0")
    psP = em.ps("ps_u1")
    SCOS = em.ps("ps_lnS")
    SSIN = em.ps("ps_lnB")
    ZB = em.ps("ps_lnR")

    em.opi("gp", lambda e: e.memset(sfc[:], 0.0), "gpd", 1)
    em.opi("gp", lambda e: e.memset(sfs[:], 0.0), "gpd", 1)
    em.opi("gp", lambda e: e.memset(zf[:], 0.0), "gpd", 1)
    em.wait("v", "gpd", em.cntv("gpd"))
    em.wait("v", "dve", cpr)
    for h in range(16):
        em.opi("v", lambda e, h=h: e.tensor_copy(
            out=VT1[:, 65 * h + 64:65 * h + 65], in_=onec[:]), "dve")
    em.vbar()

    wfree = {0: 0, 1: 0}
    for tt in range(NTT):
        if tt > 0:
            em.wait("sy", "xst", em.cntv("xst"))
        x_ld = emit_load_x_tile(em, xt, xin_ap, tt)
        cl = em.opi("sy", lambda e, tt=tt: e.dma_start(
            out=costl[:], in_=cos_ap[:, tt * TT:(tt + 1) * TT]), "xld", 16)
        sl_ = em.opi("sy", lambda e, tt=tt: e.dma_start(
            out=sintl[:], in_=sin_ap[:, tt * TT:(tt + 1) * TT]), "xld", 16)
        em.wait("v", "xld", max(x_ld, sl_))
        em.wait("sc", "xld", max(x_ld, sl_))
        em.wait("v", "pe", em.cntv("pe"))
        emit_ln_tile(em, C, xt, xcb, f"{tag}_{tt}",
                     t1ap=em.carve(8192, ND * TT, F32),
                     xb16ap=em.carve(24576, ND * TT, BF16),
                     xsqbap=em.carve(32768, ND * TT, BF16),
                     ln_banks=("ps_u0", "ps_g0", "ps_g1"))
        em.wait("t", "dve", em.cntv("dve"))

        if dbg is not None and tt == 0:
            pass
        for part, (wdram, wbufs, bias_t) in enumerate(
                ((wqb_d, wbq, bqc), (wkb_d, wbk, bkc), (wvb_d, wbv, bvc))):
            for j in range(ND):
                b = j % 2
                em.wait("sy", "pe", wfree[b])
                wl = em.opi("sy", lambda e, wdram=wdram, wbufs=wbufs, j=j, b=b: e.dma_start(
                    out=wbufs[b][:].rearrange("p (k o) -> p k o", o=256),
                    in_=wdram[j].rearrange("k p o -> p k o")), "wld", 16)
                em.wait("t", "wld", wl)
                em.wait("t", "act", em.cntv("act"))
                em.wait("t", "dve", em.cntv("dve"))
                for i in range(ND):
                    fn = lambda e, wbufs=wbufs, j=j, b=b, i=i: e.matmul(
                        psQ[b][:, 0:TT], wbufs[b][:, i * 128:(i + 1) * 128],
                        xcb[:, i * TT:(i + 1) * TT], start=(i == 0), stop=(i == ND - 1))
                    pj = em.opi("t", fn, "pe") if i == ND - 1 else em.op("t", fn)
                wfree[b] = pj
                em.wait("sc", "pe", pj)
                if part < 2:
                    em.wait("sc", "dve", em.cntv("dve"))
                    ev = em.opi("sc", lambda e, j=j, b=b, bias_t=bias_t: e.activation(
                        out=qp_scr[:], in_=psQ[b][:, 0:TT], func=AF.Relu,
                        bias=bias_t[:, j:j + 1]), "act")
                    em.wait("v", "act", ev)
                    dc = Qc if part == 0 else Kc
                    ds = Qs if part == 0 else Ks
                    em.op("v", lambda e, dc=dc, j=j: e.tensor_tensor(
                        out=dc[:, j * TT:(j + 1) * TT], in0=qp_scr[:], in1=costl[:],
                        op=ALU.mult))
                    em.opi("v", lambda e, ds=ds, j=j: e.tensor_tensor(
                        out=ds[:, j * TT:(j + 1) * TT], in0=qp_scr[:], in1=sintl[:],
                        op=ALU.mult), "dve")
                else:
                    ev = em.opi("sc", lambda e, j=j, b=b, bias_t=bias_t: e.activation(
                        out=Vb[:, j * TT:(j + 1) * TT], in_=psQ[b][:, 0:TT],
                        func=AF.Identity, bias=bias_t[:, j:j + 1]), "act")
        proj_done_v = em.cntv("dve")
        proj_done_a = em.cntv("act")
        if dbg is not None and tt == 0:
            em.wait("sy", "dve", proj_done_v)
            em.wait("sy", "act", proj_done_a)
            em.opi("sy", lambda e: e.dma_start(out=dbg["qc"], in_=Qc[:]), "xst", 16)
            em.opi("sy", lambda e: e.dma_start(out=dbg["kc"], in_=Kc[:]), "xst", 16)
            em.opi("sy", lambda e: e.dma_start(out=dbg["vb"], in_=Vb[:]), "xst", 16)

        for c in range(NCH):
            csl = lambda i, c=c: slice(i * TT + 128 * c, i * TT + 128 * (c + 1))
            em.wait("t", "dve", em.cntv("dve"))
            em.wait("t", "act", proj_done_a)
            for pr in range(ND):
                for (srcb, dstb, dstoff) in ((Kc, KcT, 128 * pr), (Ks, KsT, 128 * pr)):
                    em.wait("t", "dve", em.cntv("dve"))
                    pt = em.opi("t", lambda e, srcb=srcb, pr=pr, csl=csl: e.transpose(
                        psP[:].bitcast(BF16)[:, 0:128], srcb[:, csl(pr)],
                        C["ident_bf16"][:]), "pe")
                    em.wait("v", "pe", pt)
                    em.opi("v", lambda e, dstb=dstb, dstoff=dstoff: e.tensor_copy(
                        out=dstb[:, dstoff:dstoff + 128],
                        in_=psP[:].bitcast(BF16)[:, 0:128]), "dve")
                em.wait("t", "dve", em.cntv("dve"))
                pt = em.opi("t", lambda e, pr=pr, csl=csl: e.transpose(
                    psP[:].bitcast(BF16)[:, 0:128], Vb[:, csl(pr)],
                    C["ident_bf16"][:]), "pe")
                em.wait("v", "pe", pt)
                for hh in range(2):
                    em.opi("v", lambda e, pr=pr, hh=hh: e.tensor_copy(
                        out=VT1[:, 65 * (2 * pr + hh):65 * (2 * pr + hh) + 64],
                        in_=psP[:].bitcast(BF16)[:, 64 * hh:64 * hh + 64]), "dve")
            em.vbar()
            tr_done = em.cntv("dve")
            if tt == 0 and c == 0:
                em.opi("gp", lambda e: e.memset(scc[:], 0.0), "gpd", 1)
                em.opi("gp", lambda e: e.memset(scs[:], 0.0), "gpd", 1)
                gpd = em.cntv("gpd")
                em.wait("t", "gpd", gpd)
                em.wait("v", "gpd", gpd)
            else:
                em.wait("v", "pe", em.cntv("pe"))
                em.opi("v", lambda e: e.tensor_tensor(
                    out=sfc[:], in0=sfc[:], in1=SCOS[:], op=ALU.add), "dve")
                em.opi("v", lambda e: e.tensor_tensor(
                    out=sfs[:], in0=sfs[:], in1=SSIN[:], op=ALU.add), "dve")
                em.opi("v", lambda e: e.tensor_tensor(
                    out=zf[:], in0=zf[:], in1=ZB[:, 0:16], op=ALU.add), "dve")
                em.vbar()
                em.opi("v", lambda e: e.tensor_copy(
                    out=scc[:].rearrange("p (a b) -> p a b", b=65)[:, :, 0:64],
                    in_=sfc[:].rearrange("p (a b) -> p a b", b=64)), "dve")
                em.opi("v", lambda e: e.tensor_copy(
                    out=scc[:].rearrange("p (a b) -> p a b", b=65)[:, :, 64:65],
                    in_=zf[:, 0:8].rearrange("p (a b) -> p a b", b=1)), "dve")
                em.opi("v", lambda e: e.tensor_copy(
                    out=scs[:].rearrange("p (a b) -> p a b", b=65)[:, :, 0:64],
                    in_=sfs[:].rearrange("p (a b) -> p a b", b=64)), "dve")
                em.opi("v", lambda e: e.tensor_copy(
                    out=scs[:].rearrange("p (a b) -> p a b", b=65)[:, :, 64:65],
                    in_=zf[:, 8:16].rearrange("p (a b) -> p a b", b=1)), "dve")
            em.vbar()
            sev = em.cntv("dve")
            em.wait("t", "dve", sev)
            if dbg is not None and tt == 0 and c == 0:
                em.wait("sy", "dve", sev)
                em.opi("sy", lambda e: e.dma_start(out=dbg["vt1"], in_=VT1[:]), "xst", 16)
                em.opi("sy", lambda e: e.dma_start(out=dbg["kct"], in_=KcT[:]), "xst", 16)
                for ek in ("v", "t", "sc", "gp"):
                    em.wait(ek, "xst", em.cntv("xst"))
            if dbg is not None and tt == 0 and c == 1:
                em.wait("sy", "dve", sev)
                em.opi("sy", lambda e: e.dma_start(out=dbg["scc"], in_=scc[:]), "xst", 16)
                em.opi("sy", lambda e: e.dma_start(out=dbg["scs"], in_=scs[:]), "xst", 16)
                for ek in ("v", "t", "sc", "gp"):
                    em.wait(ek, "xst", em.cntv("xst"))

            for h in range(16):
                pr = h // 2
                rb = 64 * (h % 2)
                em.wait("t", "dve", em.cntv("dve"))
                em.op("t", lambda e, pr=pr, rb=rb, csl=csl: e.matmul(
                    psP[:, 0:128], Kc[:, csl(pr)][rb:rb + 64, :], Qc[:, csl(pr)][rb:rb + 64, :],
                    start=True, stop=False))
                pp = em.opi("t", lambda e, pr=pr, rb=rb, csl=csl: e.matmul(
                    psP[:, 0:128], Ks[:, csl(pr)][rb:rb + 64, :], Qs[:, csl(pr)][rb:rb + 64, :],
                    start=False, stop=True), "pe")
                em.wait("v", "pe", pp)
                em.opi("v", lambda e: e.tensor_tensor(
                    out=Pm[:], in0=psP[:, 0:128], in1=trib[:], op=ALU.mult), "dve")
                pm = em.vbar()
                em.wait("t", "dve", pm)
                em.op("t", lambda e, h=h: e.matmul(
                    psO[:, 0:65], Pm[:], VT1[:, 65 * h:65 * h + 65],
                    start=True, stop=False))
                em.op("t", lambda e, h=h, pr=pr, rb=rb, csl=csl: e.matmul(
                    psO[:, 0:65], Qc[:, csl(pr)][rb:rb + 64, :], scc[rb:rb + 64, 65 * pr:65 * pr + 65],
                    start=False, stop=False))
                po = em.opi("t", lambda e, h=h, pr=pr, rb=rb, csl=csl: e.matmul(
                    psO[:, 0:65], Qs[:, csl(pr)][rb:rb + 64, :], scs[rb:rb + 64, 65 * pr:65 * pr + 65],
                    start=False, stop=True), "pe")
                em.wait("v", "pe", po)
                em.opi("v", lambda e: e.tensor_scalar(
                    out=dscr[:], in0=psO[:, 64:65], scalar1=1e-6, scalar2=None,
                    op0=ALU.max), "dve")
                em.vbar()
                em.opi("v", lambda e: e.reciprocal(out=dscr[:], in_=dscr[:]), "dve")
                em.vbar()
                em.opi("v", lambda e, h=h: e.tensor_scalar(
                    out=out_tm[:, 64 * h:64 * (h + 1)], in0=psO[:, 0:64],
                    scalar1=dscr[:], scalar2=None, op0=ALU.mult), "dve")
                ctx_ev = em.cntv("dve")
                em.wait("t", "dve", ctx_ev)
                em.op("t", lambda e, h=h, pr=pr, rb=rb: e.matmul(
                    SCOS[rb:rb + 64, 64 * pr:64 * pr + 64], KcT[:, 128 * pr + rb:128 * pr + rb + 64],
                    VT1[:, 65 * h:65 * h + 64], start=True, stop=True))
                em.op("t", lambda e, h=h, pr=pr, rb=rb: e.matmul(
                    ZB[rb:rb + 64, pr:pr + 1], KcT[:, 128 * pr + rb:128 * pr + rb + 64],
                    onec[:], start=True, stop=True))
                em.op("t", lambda e, h=h, pr=pr, rb=rb: e.matmul(
                    SSIN[rb:rb + 64, 64 * pr:64 * pr + 64], KsT[:, 128 * pr + rb:128 * pr + rb + 64],
                    VT1[:, 65 * h:65 * h + 64], start=True, stop=True))
                em.opi("t", lambda e, h=h, pr=pr, rb=rb: e.matmul(
                    ZB[rb:rb + 64, 8 + pr:8 + pr + 1], KsT[:, 128 * pr + rb:128 * pr + rb + 64],
                    onec[:], start=True, stop=True), "pe")
            em.vbar()
            otm = em.cntv("dve")
            if dbg is not None and tt == 0:
                em.wait("sy", "dve", otm)
                em.opi("sy", lambda e, c=c: e.dma_start(
                    out=dbg["otm"][:, 1024 * c:1024 * (c + 1)], in_=out_tm[:]), "xst", 16)
                for ek in ("v", "t", "sc", "gp"):
                    em.wait(ek, "xst", em.cntv("xst"))
            em.wait("t", "dve", otm)
            for i in range(ND):
                em.wait("t", "dve", em.cntv("dve"))
                pt = em.opi("t", lambda e, i=i: e.transpose(
                    psP[:].bitcast(BF16)[:, 0:128], out_tm[:, 128 * i:128 * (i + 1)],
                    C["ident_bf16"][:]), "pe")
                em.wait("v", "pe", pt)
                em.opi("v", lambda e, i=i, csl=csl: e.tensor_copy(
                    out=outf[:, csl(i)], in_=psP[:].bitcast(BF16)[:, 0:128]), "dve")

        out_done = em.cntv("dve")
        em.wait("t", "dve", out_done)
        for j in range(ND):
            b = j % 2
            em.wait("sy", "pe", wfree[b])
            wl = em.opi("sy", lambda e, j=j, b=b: e.dma_start(
                out=wbq[b][:].rearrange("p (k o) -> p k o", o=256),
                in_=wob_d[j].rearrange("k p o -> p k o")), "wld", 16)
            em.wait("t", "wld", wl)
            em.wait("t", "dve", em.cntv("dve"))
            for i in range(ND):
                fn = lambda e, j=j, b=b, i=i: e.matmul(
                    psQ[b][:, 0:TT], wbq[b][:, i * 128:(i + 1) * 128],
                    outf[:, i * TT:(i + 1) * TT], start=(i == 0), stop=(i == ND - 1))
                pj = em.opi("t", fn, "pe") if i == ND - 1 else em.op("t", fn)
            wfree[b] = pj
            em.wait("v", "pe", pj)
            em.op("v", lambda e, j=j, b=b: e.tensor_scalar(
                out=qp_scr[:], in0=psQ[b][:, 0:TT], scalar1=1.0,
                scalar2=boc[:, j:j + 1], op0=ALU.mult, op1=ALU.add))
            em.opi("v", lambda e, j=j: e.tensor_tensor(
                out=xt[:, j * TT:(j + 1) * TT], in0=xt[:, j * TT:(j + 1) * TT],
                in1=qp_scr[:], op=ALU.add), "dve")
        em.wait("sy", "dve", em.cntv("dve"))
        emit_store_x_tile(em, xt, xout_ap, tt)
    em.wait("sy", "xst", em.cntv("xst"))



def emit_round_e2m1(em, xs, q, scr, tag):
    r0, sgn, m1, corr = scr["r0"], scr["a"], scr["r1"], scr["r2"]
    em.op("v", lambda e: e.tensor_scalar(out=r0[:], in0=xs, scalar1=2.0, scalar2=-2.0,
                                         op0=ALU.min, op1=ALU.max))
    em.op("v", lambda e: e.tensor_scalar(out=r0[:], in0=r0[:], scalar1=2.0, scalar2=MAGIC,
                                         op0=ALU.mult, op1=ALU.add))
    em.op("v", lambda e: e.tensor_scalar(out=r0[:], in0=r0[:], scalar1=MAGIC, scalar2=0.5,
                                         op0=ALU.subtract, op1=ALU.mult))
    em.wait("sc", "dve", em.cntv("dve"))
    sg = em.opi("sc", lambda e: e.activation(out=sgn[:], in_=xs, func=AF.Sign), "act")
    em.op("v", lambda e: e.tensor_scalar(out=m1[:], in0=xs, scalar1=2.5, scalar2=None,
                                         op0=ALU.is_gt))
    em.op("v", lambda e: e.tensor_scalar(out=corr[:], in0=xs, scalar1=-2.5, scalar2=None,
                                         op0=ALU.is_lt))
    em.op("v", lambda e: e.tensor_tensor(out=corr[:], in0=corr[:], in1=m1[:], op=ALU.add))
    em.op("v", lambda e: e.tensor_scalar(out=m1[:], in0=xs, scalar1=3.5, scalar2=None,
                                         op0=ALU.is_gt))
    em.op("v", lambda e: e.tensor_tensor(out=corr[:], in0=corr[:], in1=m1[:], op=ALU.add))
    em.op("v", lambda e: e.tensor_scalar(out=m1[:], in0=xs, scalar1=-3.5, scalar2=None,
                                         op0=ALU.is_lt))
    em.op("v", lambda e: e.tensor_tensor(out=corr[:], in0=corr[:], in1=m1[:], op=ALU.add))
    em.op("v", lambda e: e.tensor_scalar(out=m1[:], in0=xs, scalar1=5.0, scalar2=None,
                                         op0=ALU.is_gt))
    em.op("v", lambda e: e.scalar_tensor_tensor(out=corr[:], in0=m1[:], scalar=2.0,
                                                in1=corr[:], op0=ALU.mult, op1=ALU.add))
    em.op("v", lambda e: e.tensor_scalar(out=m1[:], in0=xs, scalar1=-5.0, scalar2=None,
                                         op0=ALU.is_lt))
    em.op("v", lambda e: e.scalar_tensor_tensor(out=corr[:], in0=m1[:], scalar=2.0,
                                                in1=corr[:], op0=ALU.mult, op1=ALU.add))
    em.wait("v", "act", sg)
    em.op("v", lambda e: e.tensor_tensor(out=corr[:], in0=corr[:], in1=sgn[:], op=ALU.mult))
    em.op("v", lambda e: e.tensor_tensor(out=q[:], in0=r0[:], in1=corr[:], op=ALU.add))



def emit_mx_scales(em, C, src_buf, n_pt, tag, scl_tensors):
    BPP = 8
    nblk = n_pt * BPP
    amax = em.sb(tag + "_amax", [128, 4 * nblk], F32)
    sclr = em.sb(tag + "_sclr", [128, nblk], F32)
    psT = em.ps("ps_g0")
    psb = em.ps("ps_g1")
    for s in range(4):
        for i in range(n_pt):
            em.wait("t", "dve", em.cntv("dve"))
            pet = em.opi("t", lambda e, s=s, i=i: e.transpose(
                psT[:, 0:128], src_buf[:, i * TT + 128 * s: i * TT + 128 * (s + 1)],
                C["ident_f32"][:]), "pe")
            em.wait("v", "pe", pet)
            em.opi("v", lambda e, s=s, i=i: e.tensor_reduce(
                out=amax[:, nblk * s + BPP * i: nblk * s + BPP * (i + 1)].rearrange(
                    "p (a b) -> p a b", b=1),
                in_=psT[:, 0:128].rearrange("p (a b) -> p a b", b=16),
                axis=AX.X, op=ALU.max, apply_absolute_value=True), "dve")
        em.vbar()
        em.op("v", lambda e, s=s: e.tensor_scalar(
            out=amax[:, nblk * s:nblk * (s + 1)], in0=amax[:, nblk * s:nblk * (s + 1)],
            scalar1=1.0 / 6.0, scalar2=1e-12, op0=ALU.mult, op1=ALU.max))
        em.vbar()
        em.opi("v", lambda e, s=s: e.reciprocal(
            out=sclr[:], in_=amax[:, nblk * s:nblk * (s + 1)]), "dve")
        rc = em.vbar()
        for (t_r, t_s, row0, nrows) in scl_tensors:
            em.wait("t", "dve", em.cntv("dve"))
            p1 = em.opi("t", lambda e, s=s, row0=row0, nrows=nrows: e.transpose(
                psb[0:nrows, 0:128], sclr[:, row0:row0 + nrows],
                C["ident_f32"][:]), "pe")
            em.wait("v", "pe", p1)
            c1 = em.opi("v", lambda e, s=s, t_r=t_r, nrows=nrows: e.tensor_copy(
                out=t_r[0:nrows, 128 * s:128 * (s + 1)], in_=psb[0:nrows, 0:128]),
                "dve")
            em.wait("t", "dve", c1)
            p2 = em.opi("t", lambda e, s=s, row0=row0, nrows=nrows: e.transpose(
                psb[0:nrows, 0:128],
                amax[:, nblk * s + row0: nblk * s + row0 + nrows],
                C["ident_f32"][:]), "pe")
            em.wait("v", "pe", p2)
            em.opi("v", lambda e, s=s, t_s=t_s, nrows=nrows: e.tensor_copy(
                out=t_s[0:nrows, 128 * s:128 * (s + 1)], in_=psb[0:nrows, 0:128]),
                "dve")
    return em.cntv("dve")



def emit_l2_ffn(em, C, xin_ap, xout_ap, wg_nat, wu_nat, wd_nat, tag="l2", wstate=None):
    nc = em.nc
    I = 2816
    nJ = I // 128
    skip_wq = wstate is not None and "wgT" in wstate
    if skip_wq:
        wgT, wuT, wdT = wstate["wgT"], wstate["wuT"], wstate["wdT"]
    else:
        wgT = nc.dram_tensor(tag + "_wgT", [nJ, ND, 128, 128], BF16, kind="Internal").ap()
        wuT = nc.dram_tensor(tag + "_wuT", [nJ, ND, 128, 128], BF16, kind="Internal").ap()
        wdT = nc.dram_tensor(tag + "_wdT", [ND, nJ, 128, 128], BF16, kind="Internal").ap()
        if wstate is not None:
            wstate.update(wgT=wgT, wuT=wuT, wdT=wdT)

    xt = em.sb(tag + "_xt", [128, ND * TT], F32)
    xcf_ap = em.carve(0, ND * TT, F32)
    xq_ap = em.carve(16384, ND * TT, BF16)
    hf_ap = em.carve(24576, nJ * TT, F32)
    hq_ap = em.carve(69632, nJ * TT, BF16)
    scr = {}
    for kk, koff in (("r0", 92160), ("r1", 96256), ("r2", 100352), ("a", 104448),
                     ("xs", 108544), ("q", 112640)):
        scr[kk] = _T(em.carve(koff, D, F32))
    t1_ap = em.carve(40960, ND * TT, F32)
    xcf = _T(xcf_ap); xq = _T(xq_ap); hf = _T(hf_ap); hq = _T(hq_ap)
    asil = [em.sb(tag + f"_as{i}", [128, TT], F32) for i in range(2)]
    sxr = em.sb(tag + "_sxr", [64, TT], F32)
    sxs = em.sb(tag + "_sxs", [64, TT], F32)
    shra = em.sb(tag + "_shra", [128, TT], F32)
    shsa = em.sb(tag + "_shsa", [128, TT], F32)
    shrb = em.sb(tag + "_shrb", [48, TT], F32)
    shsb = em.sb(tag + "_shsb", [48, TT], F32)
    wbufA = [em.sb(tag + f"_wA{i}", [128, ND * 128], BF16) for i in range(2)]
    wbufB = [em.sb(tag + f"_wB{i}", [128, ND * 128], BF16) for i in range(2)]
    wbufD = [em.sb(tag + f"_wD{i}", [128, nJ * 128], BF16) for i in range(2)]
    wstg = em.sb(tag + "_wstg", [128, 128], BF16)
    psA = [em.ps("ps_g0"), em.ps("ps_g1")]
    psB = [em.ps("ps_u0"), em.ps("ps_u1")]

    e16xt = load_pattern(em, C, "e16x")
    e176at = load_pattern(em, C, "e176a")
    e176bt = load_pattern(em, C, "e176b")

    def quant_weight(nat_ap, O, K, outT, n_ot, n_kt, wtag):
        wch = _T(em.carve(24576, 4096, F32))
        wqb = _T(em.carve(40960, 4096, BF16))
        wsc = em.sb(tag + "_wsc", [128, 256], F32)
        wscr = em.sb(tag + "_wscr", [128, 256], F32)
        for j in range(n_ot):
            em.wait("sy", "dve", em.cntv("dve"))
            em.wait("sy", "pe", em.cntv("pe"))
            wl = em.opi("sy", lambda e, j=j: e.dma_start(
                out=wch[:, 0:K], in_=nat_ap[128 * j:128 * (j + 1), :]), "xld", 16)
            em.wait("v", "xld", wl)
            nb = K // 16
            em.opi("v", lambda e, nb=nb: e.tensor_reduce(
                out=wsc[:, 0:nb].rearrange("p (a b) -> p a b", b=1),
                in_=wch[:, 0:K].rearrange("p (a b) -> p a b", b=16),
                axis=AX.X, op=ALU.max, apply_absolute_value=True), "dve")
            em.vbar()
            em.op("v", lambda e, nb=nb: e.tensor_scalar(
                out=wsc[:, 0:nb], in0=wsc[:, 0:nb], scalar1=1.0 / 6.0, scalar2=1e-12,
                op0=ALU.mult, op1=ALU.max))
            em.vbar()
            em.op("v", lambda e, nb=nb: e.reciprocal(out=wscr[:, 0:nb], in_=wsc[:, 0:nb]))
            em.vbar()
            for c0 in range(0, K, 1024):
                cw = min(1024, K - c0)
                nbp = cw // 16
                em.opi("v", lambda e, c0=c0, cw=cw, nbp=nbp: e.tensor_tensor(
                    out=scr["xs"][:, 0:cw].rearrange("p (a b) -> p a b", b=16),
                    in0=wch[:, c0:c0 + cw].rearrange("p (a b) -> p a b", b=16),
                    in1=wscr[:, c0 // 16:c0 // 16 + nbp].rearrange(
                        "p (a b) -> p a b", b=1).broadcast_to((128, nbp, 16)),
                    op=ALU.mult), "dve")
                emit_round_e2m1(em, scr["xs"][:, 0:cw], SubView(scr["q"], cw), 
                                {k: SubView(scr[k], cw) for k in scr}, wtag)
                em.opi("v", lambda e, c0=c0, cw=cw, nbp=nbp: e.tensor_tensor(
                    out=wqb[:, c0:c0 + cw].rearrange("p (a b) -> p a b", b=16),
                    in0=scr["q"][:, 0:cw].rearrange("p (a b) -> p a b", b=16),
                    in1=wsc[:, c0 // 16:c0 // 16 + nbp].rearrange(
                        "p (a b) -> p a b", b=1).broadcast_to((128, nbp, 16)),
                    op=ALU.mult), "dve")
            dq = em.cntv("dve")
            em.wait("t", "dve", dq)
            for ki in range(n_kt):
                em.wait("t", "xst", em.cntv("xst"))
                pet = em.opi("t", lambda e, ki=ki: e.transpose(
                    psA[0][:].bitcast(BF16)[:, 0:128],
                    wqb[:, 128 * ki:128 * (ki + 1)], C["ident_bf16"][:]), "pe")
                em.wait("v", "pe", pet)
                em.wait("v", "xst", em.cntv("xst"))
                cstg = em.opi("v", lambda e: e.tensor_copy(
                    out=wstg[:], in_=psA[0][:].bitcast(BF16)[:, 0:128]), "dve")
                em.wait("sy", "dve", cstg)
                em.opi("sy", lambda e, j=j, ki=ki, outT=outT: e.dma_start(
                    out=outT[j, ki], in_=wstg[:]), "xst", 16)

    if not skip_wq:
        quant_weight(wg_nat, I, D, wgT, nJ, ND, tag + "_qg")
        quant_weight(wu_nat, I, D, wuT, nJ, ND, tag + "_qu")
        quant_weight(wd_nat, D, I, wdT, ND, nJ, tag + "_qd")
        em.wait("sy", "xst", em.cntv("xst"))

    wA_free = [0, 0]; wB_free = [0, 0]; wD_free = [0, 0]
    psA_ev = [em.cntv("act"), em.cntv("act")]
    psB_ev = [em.cntv("dve"), em.cntv("dve")]
    a_free = [0, 0]

    def mm_group(bank, wbuf, rhs_buf, nK):
        last = None
        for i in range(nK):
            fn = lambda e, bank=bank, wbuf=wbuf, rhs_buf=rhs_buf, i=i: e.matmul(
                bank[:, 0:TT], wbuf[:, i * 128:(i + 1) * 128],
                rhs_buf[:, i * TT:(i + 1) * TT],
                start=(i == 0), stop=(i == nK - 1))
            if i == nK - 1:
                last = em.opi("t", fn, "pe")
            else:
                em.op("t", fn)
        return last

    for tt in range(NTT):
        if tt > 0:
            em.wait("sy", "xst", em.cntv("xst"))
        x_ld = emit_load_x_tile(em, xt, xin_ap, tt)
        em.wait("v", "xld", x_ld)
        em.wait("sc", "xld", x_ld)
        if tt > 0:
            em.wait("v", "pe", em.cntv("pe"))
        emit_ln_tile(em, C, xt, xcf, f"{tag}_{tt}", t1ap=t1_ap,
                     xb16ap=em.carve(24576, ND * TT, BF16),
                     xsqbap=em.carve(32768, ND * TT, BF16))

        emit_mx_scales(em, C, xcf, ND, tag + "_xs", [(sxr, sxs, 0, 64)])
        for i in range(ND):
            bank_r = psA[i % 2]
            bank_s = psB[i % 2]
            em.wait("t", "dve", em.cntv("dve"))
            em.wait("t", "act", em.cntv("act"))
            em.opi("t", lambda e, i=i, bank_r=bank_r: e.matmul(
                bank_r[:, 0:TT], e16xt[0:64, 128 * i:128 * (i + 1)], sxr[:],
                start=True, stop=True), "pe")
            pb = em.opi("t", lambda e, i=i, bank_s=bank_s: e.matmul(
                bank_s[:, 0:TT], e16xt[0:64, 128 * i:128 * (i + 1)], sxs[:],
                start=True, stop=True), "pe")
            em.wait("v", "pe", pb)
            sl = slice(i * TT, (i + 1) * TT)
            em.opi("v", lambda e, sl=sl, bank_r=bank_r: e.tensor_tensor(
                out=scr["xs"][:, 0:TT], in0=xcf[:, sl], in1=bank_r[:, 0:TT], op=ALU.mult),
                "dve")
            emit_round_e2m1(em, scr["xs"][:, 0:TT], SubView(scr["q"], TT),
                            {k: SubView(scr[k], TT) for k in scr}, tag + f"_xq{tt}_{i}")
            em.opi("v", lambda e, sl=sl, bank_s=bank_s: e.tensor_tensor(
                out=xq[:, sl], in0=scr["q"][:, 0:TT], in1=bank_s[:, 0:TT], op=ALU.mult),
                "dve")
        xq_ready = em.cntv("dve")

        em.wait("t", "dve", xq_ready)
        for j in range(nJ):
            b = j % 2
            em.wait("sy", "pe", max(wA_free[b], wB_free[b]))
            em.opi("sy", lambda e, j=j, b=b: e.dma_start(
                out=wbufA[b][:].rearrange("p (k o) -> p k o", o=128),
                in_=wgT[j].rearrange("k p o -> p k o")), "wld", 16)
            wl = em.opi("sy", lambda e, j=j, b=b: e.dma_start(
                out=wbufB[b][:].rearrange("p (k o) -> p k o", o=128),
                in_=wuT[j].rearrange("k p o -> p k o")), "wld", 16)
            em.wait("t", "wld", wl)
            em.wait("t", "act", psA_ev[b])
            em.wait("t", "dve", psB_ev[b])
            peg = mm_group(psA[b], wbufA[b], xq, ND)
            peu = mm_group(psB[b], wbufB[b], xq, ND)
            wA_free[b] = wB_free[b] = peu
            em.wait("sc", "pe", peg)
            em.wait("sc", "dve", a_free[b])
            sca = em.opi("sc", lambda e, b=b: e.activation(
                out=asil[b][:], in_=psA[b][:, 0:TT], func=AF.Silu), "act")
            psA_ev[b] = sca
            em.wait("v", "pe", peu)
            em.wait("v", "act", sca)
            hj = em.opi("v", lambda e, j=j, b=b: e.tensor_tensor(
                out=hf[:, j * TT:(j + 1) * TT], in0=psB[b][:, 0:TT],
                in1=asil[b][:], op=ALU.mult), "dve")
            psB_ev[b] = a_free[b] = hj

        em.wait("t", "dve", em.cntv("dve"))
        emit_mx_scales(em, C, hf, nJ, tag + "_hs",
                       [(shra, shsa, 0, 128), (shrb, shsb, 128, 48)])
        for i in range(nJ):
            bank_r = psA[i % 2]
            bank_s = psB[i % 2]
            em.wait("t", "dve", em.cntv("dve"))
            em.wait("t", "act", em.cntv("act"))
            if i < 16:
                lr = lambda e, i=i, bank_r=bank_r: e.matmul(
                    bank_r[:, 0:TT], e176at[:][:, 128 * i:128 * (i + 1)], shra[:],
                    start=True, stop=True)
                ls = lambda e, i=i, bank_s=bank_s: e.matmul(
                    bank_s[:, 0:TT], e176at[:][:, 128 * i:128 * (i + 1)], shsa[:],
                    start=True, stop=True)
            else:
                lr = lambda e, i=i, bank_r=bank_r: e.matmul(
                    bank_r[:, 0:TT], e176bt[0:48, 128 * (i - 16):128 * (i - 15)], shrb[:],
                    start=True, stop=True)
                ls = lambda e, i=i, bank_s=bank_s: e.matmul(
                    bank_s[:, 0:TT], e176bt[0:48, 128 * (i - 16):128 * (i - 15)], shsb[:],
                    start=True, stop=True)
            em.opi("t", lr, "pe")
            pb = em.opi("t", ls, "pe")
            em.wait("v", "pe", pb)
            sl = slice(i * TT, (i + 1) * TT)
            em.opi("v", lambda e, sl=sl, bank_r=bank_r: e.tensor_tensor(
                out=scr["xs"][:, 0:TT], in0=hf[:, sl], in1=bank_r[:, 0:TT], op=ALU.mult),
                "dve")
            emit_round_e2m1(em, scr["xs"][:, 0:TT], SubView(scr["q"], TT),
                            {k: SubView(scr[k], TT) for k in scr}, tag + f"_hq{tt}_{i}")
            em.opi("v", lambda e, sl=sl, bank_s=bank_s: e.tensor_tensor(
                out=hq[:, sl], in0=scr["q"][:, 0:TT], in1=bank_s[:, 0:TT], op=ALU.mult),
                "dve")
        hq_ready = em.cntv("dve")

        em.wait("t", "dve", hq_ready)
        for jd in range(ND):
            b = jd % 2
            em.wait("sy", "pe", wD_free[b])
            wl = em.opi("sy", lambda e, jd=jd, b=b: e.dma_start(
                out=wbufD[b][:].rearrange("p (k o) -> p k o", o=128),
                in_=wdT[jd].rearrange("k p o -> p k o")), "wld", 16)
            em.wait("t", "wld", wl)
            em.wait("t", "act", psA_ev[b])
            em.wait("t", "dve", psB_ev[b])
            pej = mm_group(psA[b], wbufD[b], hq, nJ)
            wD_free[b] = pej
            em.wait("v", "pe", pej)
            ev = em.opi("v", lambda e, jd=jd, b=b: e.scalar_tensor_tensor(
                out=xt[:, jd * TT:(jd + 1) * TT], in0=psA[b][:, 0:TT],
                scalar=0.5, in1=xt[:, jd * TT:(jd + 1) * TT],
                op0=ALU.mult, op1=ALU.add), "dve")
            psB_ev[b] = ev
            psA_ev[b] = em.cntv("act")
        em.wait("sy", "dve", em.cntv("dve"))
        emit_store_x_tile(em, xt, xout_ap, tt)
    em.wait("sy", "xst", em.cntv("xst"))



def emit_round_e2m3(em, xs, q, scr, tag):
    W = xs.tensor.shape[1] if hasattr(xs, "tensor") else None
    r0, r1, r2, a, m1, m2 = scr["r0"], scr["r1"], scr["r2"], scr["a"], scr["m1"], scr["m2"]
    em.op("v", lambda e: e.tensor_scalar(out=r0[:], in0=xs, scalar1=2.0, scalar2=-2.0,
                                         op0=ALU.min, op1=ALU.max))
    em.op("v", lambda e: e.tensor_scalar(out=r0[:], in0=r0[:], scalar1=8.0, scalar2=MAGIC,
                                         op0=ALU.mult, op1=ALU.add))
    em.op("v", lambda e: e.tensor_scalar(out=r0[:], in0=r0[:], scalar1=MAGIC, scalar2=0.125,
                                         op0=ALU.subtract, op1=ALU.mult))
    em.op("v", lambda e: e.tensor_scalar(out=r1[:], in0=xs, scalar1=4.0, scalar2=MAGIC,
                                         op0=ALU.mult, op1=ALU.add))
    em.op("v", lambda e: e.tensor_scalar(out=r1[:], in0=r1[:], scalar1=MAGIC, scalar2=0.25,
                                         op0=ALU.subtract, op1=ALU.mult))
    em.op("v", lambda e: e.tensor_scalar(out=r2[:], in0=xs, scalar1=2.0, scalar2=MAGIC,
                                         op0=ALU.mult, op1=ALU.add))
    em.op("v", lambda e: e.tensor_scalar(out=r2[:], in0=r2[:], scalar1=MAGIC, scalar2=0.5,
                                         op0=ALU.subtract, op1=ALU.mult))
    em.wait("sc", "dve", em.cntv("dve"))
    sca = em.opi("sc", lambda e: e.activation(out=a[:], in_=xs, func=AF.Abs), "act")
    em.wait("v", "act", sca)
    em.op("v", lambda e: e.tensor_scalar(out=m1[:], in0=a[:], scalar1=2.0625, scalar2=None,
                                         op0=ALU.is_gt))
    em.op("v", lambda e: e.tensor_scalar(out=m2[:], in0=a[:], scalar1=3.875, scalar2=None,
                                         op0=ALU.is_gt))
    em.op("v", lambda e: e.select(out=q[:], mask=m1[:], on_true=r1[:], on_false=r0[:]))
    em.op("v", lambda e: e.select(out=q[:], mask=m2[:], on_true=r2[:], on_false=q[:]))



def emit_l3_ffn(em, C, xin_ap, xout_ap, w_nat_ap, bias_ap, tag="l3", dbg=None):
    nc = em.nc
    xt = em.sb(tag + "_xt", [128, ND * TT], F32)
    xcf = _T(em.carve(0, ND * TT, F32))
    xdb = _T(em.carve(16384, ND * TT, BF16))
    wqT = _T(em.carve(57344, ND * D, BF16))
    wchunk = _T(em.carve(73728, D, F32))
    wqb = _T(em.carve(90112, D, BF16))
    biassb = em.sb(tag + "_bias", [128, ND], F32)
    scr = {}
    for _i, kk in enumerate(("r0", "r1", "r2", "a", "xs", "q")):
        scr[kk] = _T(em.carve(24576 + _i * 4096, D, F32))
    scr["m1"] = _T(em.carve(49152, D, I32))
    scr["m2"] = _T(em.carve(53248, D, I32))
    e32t = load_pattern(em, C, "e32")
    wsc = em.sb(tag + "_wsc", [128, 32], F32)
    amax_tm = em.sb(tag + "_amax", [128, 4 * 32], F32)
    sclr_tm = em.sb(tag + "_sclr", [128, 32], F32)
    scl_fm_r = em.sb(tag + "_sfr", [32, TT], F32)
    scl_fm_s = em.sb(tag + "_sfs", [32, TT], F32)
    ev1 = em.sb(tag + "_ev1", [128, TT], F32)
    psA = em.ps("ps_g0")
    psB2 = em.ps("ps_g1")
    psC = em.ps("ps_u0")
    psD2 = em.ps("ps_u1")

    for j in range(ND):
        em.opi("sy", lambda e, j=j: e.dma_start(
            out=biassb[:, j:j + 1], in_=bias_ap[128 * j:128 * (j + 1)].rearrange("(p o) -> p o", o=1)),
            "xld", 16)
    bld = em.cntv("xld")
    em.wait("v", "xld", bld)
    em.opi("v", lambda e: e.tensor_scalar(out=biassb[:], in0=biassb[:], scalar1=0.5,
                                          scalar2=None, op0=ALU.mult), "dve")

    for j in range(ND):
        em.wait("sy", "dve", em.cntv("dve"))
        wl = em.opi("sy", lambda e, j=j: e.dma_start(
            out=wchunk[:], in_=w_nat_ap[128 * j:128 * (j + 1), :]), "xld", 16)
        em.wait("v", "xld", wl)
        em.opi("v", lambda e: e.tensor_reduce(
            out=wsc[:].rearrange("p (a b) -> p a b", b=1),
            in_=wchunk[:].rearrange("p (a b) -> p a b", b=32),
            axis=AX.X, op=ALU.max, apply_absolute_value=True), "dve")
        em.vbar()
        em.op("v", lambda e: e.tensor_scalar(out=wsc[:], in0=wsc[:], scalar1=1.0 / 7.5,
                                             scalar2=1e-30, op0=ALU.mult, op1=ALU.max))
        em.vbar()
        em.op("v", lambda e: e.reciprocal(out=sclr_tm[:, 0:32], in_=wsc[:]))
        em.vbar()
        em.opi("v", lambda e: e.tensor_tensor(
            out=scr["xs"][:].rearrange("p (a b) -> p a b", b=32),
            in0=wchunk[:].rearrange("p (a b) -> p a b", b=32),
            in1=sclr_tm[:, 0:32].rearrange("p (a b) -> p a b", b=1).broadcast_to((128, 32, 32)),
            op=ALU.mult), "dve")
        emit_round_e2m3(em, scr["xs"][:], scr["q"], scr, tag + f"_wq{j}")
        dq = em.opi("v", lambda e: e.tensor_tensor(
            out=wqb[:].rearrange("p (a b) -> p a b", b=32),
            in0=scr["q"][:].rearrange("p (a b) -> p a b", b=32),
            in1=wsc[:].rearrange("p (a b) -> p a b", b=1).broadcast_to((128, 32, 32)),
            op=ALU.mult), "dve")
        em.wait("t", "dve", dq)
        for ki in range(ND):
            em.wait("t", "dve", em.cntv("dve"))
            fn = lambda e, ki=ki: e.transpose(
                psA[:].bitcast(BF16)[:, 0:128],
                wqb[:, 128 * ki:128 * (ki + 1)], C["ident_bf16"][:])
            pet = em.opi("t", fn, "pe")
            em.wait("v", "pe", pet)
            em.opi("v", lambda e, ki=ki, j=j: e.tensor_copy(
                out=wqT[:, ki * D + 128 * j: ki * D + 128 * (j + 1)],
                in_=psA[:].bitcast(BF16)[:, 0:128]), "dve")
    wq_done_v = em.cntv("dve")
    if dbg is not None:
        em.wait("sy", "dve", wq_done_v)
        em.opi("sy", lambda e: e.dma_start(out=dbg["wqT"], in_=wqT[:]), "xst", 16)

    for tt in range(NTT):
        if tt > 0:
            em.wait("sy", "xst", em.cntv("xst"))
        x_ld = emit_load_x_tile(em, xt, xin_ap, tt)
        em.wait("v", "xld", x_ld)
        em.wait("sc", "xld", x_ld)
        if tt > 0:
            em.wait("v", "pe", em.cntv("pe"))
        emit_ln_tile(em, C, xt, xcf, f"{tag}_{tt}", out_f32=True,
                     t1ap=em.carve(73728, ND * TT, F32),
                     xb16ap=em.carve(98304, ND * TT, BF16),
                     xsqbap=em.carve(106496, ND * TT, BF16))
        xcf_ready = em.cntv("dve")

        em.wait("t", "dve", xcf_ready)
        for s in range(4):
            for i in range(ND):
                bank = psB2 if (i % 2 == 0) else psC
                em.wait("t", "dve", em.cntv("dve"))
                pet = em.opi("t", lambda e, s=s, i=i, bank=bank: e.transpose(
                    bank[:, 0:128], xcf[:, i * TT + 128 * s: i * TT + 128 * (s + 1)],
                    C["ident_f32"][:]), "pe")
                em.wait("v", "pe", pet)
                em.opi("v", lambda e, s=s, i=i, bank=bank: e.tensor_reduce(
                    out=amax_tm[:, 32 * s + 4 * i: 32 * s + 4 * (i + 1)].rearrange(
                        "p (a b) -> p a b", b=1),
                    in_=bank[:, 0:128].rearrange("p (a b) -> p a b", b=32),
                    axis=AX.X, op=ALU.max, apply_absolute_value=True), "dve")
            em.vbar()
            em.op("v", lambda e, s=s: e.tensor_scalar(
                out=amax_tm[:, 32 * s:32 * (s + 1)], in0=amax_tm[:, 32 * s:32 * (s + 1)],
                scalar1=1.0 / 7.5, scalar2=1e-30, op0=ALU.mult, op1=ALU.max))
            em.vbar()
            rc = em.opi("v", lambda e, s=s: e.reciprocal(
                out=sclr_tm[:], in_=amax_tm[:, 32 * s:32 * (s + 1)]), "dve")
            rc = em.vbar()
            em.wait("t", "dve", em.cntv("dve"))
            p1 = em.opi("t", lambda e, s=s: e.transpose(
                psD2[0:32, 0:128], sclr_tm[:], C["ident_f32"][:]), "pe")
            em.wait("v", "pe", p1)
            em.opi("v", lambda e, s=s: e.tensor_copy(
                out=scl_fm_r[:, 128 * s:128 * (s + 1)], in_=psD2[0:32, 0:128]), "dve")
            rc2 = em.cntv("dve")
            em.wait("t", "dve", rc2)
            p2 = em.opi("t", lambda e, s=s: e.transpose(
                psD2[0:32, 0:128], amax_tm[:, 32 * s:32 * (s + 1)], C["ident_f32"][:]), "pe")
            em.wait("v", "pe", p2)
            em.opi("v", lambda e, s=s: e.tensor_copy(
                out=scl_fm_s[:, 128 * s:128 * (s + 1)], in_=psD2[0:32, 0:128]), "dve")
        scl_ready = em.cntv("dve")

        for i in range(ND):
            bank_r = psB2 if i % 2 == 0 else psC
            em.wait("t", "dve", em.cntv("dve"))
            em.wait("t", "act", em.cntv("act"))
            em.opi("t", lambda e, i=i, bank_r=bank_r: e.matmul(
                bank_r[:, 0:TT], e32t[:][:, 128 * i:128 * (i + 1)], scl_fm_r[:],
                start=True, stop=True), "pe")
            pb = em.opi("t", lambda e, i=i: e.matmul(
                psD2[:, 0:TT], e32t[:][:, 128 * i:128 * (i + 1)], scl_fm_s[:],
                start=True, stop=True), "pe")
            em.wait("v", "pe", pb)
            sl = slice(i * TT, (i + 1) * TT)
            em.opi("v", lambda e, sl=sl, bank_r=bank_r: e.tensor_tensor(
                out=scr["xs"][:, 0:TT], in0=xcf[:, sl], in1=bank_r[:, 0:TT], op=ALU.mult),
                "dve")
            scrT = {k: SubView(scr[k], TT) for k in scr}
            emit_round_e2m3(em, scr["xs"][:, 0:TT], scrT["q"], scrT, tag + f"_xq{tt}_{i}")
            em.opi("v", lambda e, sl=sl: e.tensor_tensor(
                out=xdb[:, sl], in0=scr["q"][:, 0:TT], in1=psD2[:, 0:TT], op=ALU.mult),
                "dve")
        xdb_ready = em.cntv("dve")
        if dbg is not None and tt == 0:
            em.wait("sy", "dve", xdb_ready)
            em.opi("sy", lambda e: e.dma_start(out=dbg["xdb"], in_=xdb[:]), "xst", 16)
            em.opi("sy", lambda e: e.dma_start(out=dbg["xcf"], in_=xcf[:]), "xst", 16)
            em.opi("sy", lambda e: e.dma_start(out=dbg["sfr"], in_=scl_fm_r[:]), "xst", 16)
            em.opi("sy", lambda e: e.dma_start(out=dbg["sfs"], in_=scl_fm_s[:]), "xst", 16)

        em.wait("t", "dve", xdb_ready)
        for j in range(ND):
            bank = psB2 if j % 2 == 0 else psC
            em.wait("t", "act", em.cntv("act"))
            for i in range(ND):
                fn = lambda e, j=j, i=i, bank=bank: e.matmul(
                    bank[:, 0:TT], wqT[:, i * D + 128 * j: i * D + 128 * (j + 1)],
                    xdb[:, i * TT:(i + 1) * TT], start=(i == 0), stop=(i == ND - 1))
                pj = em.opi("t", fn, "pe") if i == ND - 1 else em.op("t", fn)
            em.wait("sc", "pe", pj)
            em.wait("sc", "dve", em.cntv("dve"))
            ev = em.opi("sc", lambda e, j=j, bank=bank: e.activation(
                out=ev1[:], in_=bank[:, 0:TT], func=AF.Identity, scale=0.5,
                bias=biassb[:, j:j + 1]), "act")
            em.wait("v", "act", ev)
            em.opi("v", lambda e, j=j: e.tensor_tensor(
                out=xt[:, j * TT:(j + 1) * TT], in0=ev1[:],
                in1=xt[:, j * TT:(j + 1) * TT], op=ALU.add), "dve")
        em.wait("sy", "dve", em.cntv("dve"))
        emit_store_x_tile(em, xt, xout_ap, tt)
    em.wait("sy", "xst", em.cntv("xst"))


class _T:
    def __init__(self, ap):
        self.ap = ap
    def __getitem__(self, idx):
        if idx == slice(None):
            return self.ap
        return self.ap[idx]


class SubView:
    def __init__(self, t, W):
        self.t = t; self.W = W
    def __getitem__(self, idx):
        return self.t[:, 0:self.W]



def build_l1_ffn_program():
    nc = bass.Bass("TRN2", target_bir_lowering=False)
    stack = ExitStack()
    xin = nc.dram_tensor("xin", [D, S], F32, kind="ExternalInput").ap()
    wg = nc.dram_tensor("wg", [24, 4, 128, 256], I32, kind="ExternalInput").ap()
    wu = nc.dram_tensor("wu", [24, 4, 128, 256], I32, kind="ExternalInput").ap()
    wd = nc.dram_tensor("wd", [8, 12, 128, 256], I32, kind="ExternalInput").ap()
    xout = nc.dram_tensor("xout", [D, S], F32, kind="ExternalOutput").ap()
    em = Em(nc, stack)
    C = declare_consts(nc, em, need=())
    emit_l1_ffn(em, C, xin, xout, wg, wu, wd, tag="l1")
    em.finalize()
    stack.close()
    return nc


def build_l2_attn_program():
    nc = bass.Bass("TRN2", target_bir_lowering=False)
    stack = ExitStack()
    xin = nc.dram_tensor("xin", [D, S], F32, kind="ExternalInput").ap()
    wqp = nc.dram_tensor("wqp", [8, 4, 128, 256], F32, kind="ExternalInput").ap()
    wkp = nc.dram_tensor("wkp", [8, 4, 128, 256], F32, kind="ExternalInput").ap()
    wvp = nc.dram_tensor("wvp", [8, 4, 128, 256], F32, kind="ExternalInput").ap()
    wop = nc.dram_tensor("wop", [8, 4, 128, 256], F32, kind="ExternalInput").ap()
    vecs = {}
    for name in ("bq", "bk", "bv", "bo"):
        vecs[name] = nc.dram_tensor(name, [1024], F32, kind="ExternalInput").ap()
    cosr = nc.dram_tensor("cosr", [128, S], F32, kind="ExternalInput").ap()
    sinr = nc.dram_tensor("sinr", [128, S], F32, kind="ExternalInput").ap()
    xout = nc.dram_tensor("xout", [D, S], F32, kind="ExternalOutput").ap()
    em = Em(nc, stack, pool_bytes=63488)
    C = declare_consts(nc, em, need=("ident_bf16", "tri", "ones_col1"))
    import os
    dbg = None
    if os.environ.get("L2A_DBG"):
        dbg = {"scc": nc.dram_tensor("dbg_scc", [128, 8 * 65], BF16, kind="ExternalOutput").ap(),
               "scs": nc.dram_tensor("dbg_scs", [128, 8 * 65], BF16, kind="ExternalOutput").ap(),
               "otm": nc.dram_tensor("dbg_otm", [128, 4096], BF16, kind="ExternalOutput").ap(),
               "qc": nc.dram_tensor("dbg_qc", [128, 4096], BF16, kind="ExternalOutput").ap(),
               "kc": nc.dram_tensor("dbg_kc", [128, 4096], BF16, kind="ExternalOutput").ap(),
               "vb": nc.dram_tensor("dbg_vb", [128, 4096], BF16, kind="ExternalOutput").ap(),
               "vt1": nc.dram_tensor("dbg_vt1", [128, 16 * 65], BF16, kind="ExternalOutput").ap(),
               "kct": nc.dram_tensor("dbg_kct", [128, 8 * 128], BF16, kind="ExternalOutput").ap()}
    emit_l2_attn(em, C, xin, xout, wqp, wkp, wvp, wop,
                 vecs["bq"], vecs["bk"], vecs["bv"], vecs["bo"], cosr, sinr, tag="l2a", dbg=dbg)
    em.finalize()
    stack.close()
    return nc


def build_l1_attn_program():
    nc = bass.Bass("TRN2", target_bir_lowering=False)
    stack = ExitStack()
    xin = nc.dram_tensor("xin", [D, S], F32, kind="ExternalInput").ap()
    wqkvp = nc.dram_tensor("wqkvp", [24, 4, 128, 256], F32, kind="ExternalInput").ap()
    bq = nc.dram_tensor("bq", [1024], F32, kind="ExternalInput").ap()
    bk = nc.dram_tensor("bk", [1024], F32, kind="ExternalInput").ap()
    bv = nc.dram_tensor("bv", [1024], F32, kind="ExternalInput").ap()
    gam = nc.dram_tensor("gam", [16], F32, kind="ExternalInput").ap()
    bet = nc.dram_tensor("bet", [16], F32, kind="ExternalInput").ap()
    xout = nc.dram_tensor("xout", [D, S], F32, kind="ExternalOutput").ap()
    em = Em(nc, stack, pool_bytes=92160)
    C = declare_consts(nc, em, need=("ehq16", "ehb16"))
    emit_l1_attn(em, C, xin, xout, wqkvp, bq, bk, bv, gam, bet, tag="l1a")
    em.finalize()
    stack.close()
    return nc


def build_l3_attn_program():
    nc = bass.Bass("TRN2", target_bir_lowering=False)
    stack = ExitStack()
    xin = nc.dram_tensor("xin", [D, S], F32, kind="ExternalInput").ap()
    wqp = nc.dram_tensor("wqp", [8, 4, 128, 256], F32, kind="ExternalInput").ap()
    wkp = nc.dram_tensor("wkp", [8, 4, 128, 256], F32, kind="ExternalInput").ap()
    wvp = nc.dram_tensor("wvp", [8, 4, 128, 256], F32, kind="ExternalInput").ap()
    wop = nc.dram_tensor("wop", [8, 4, 128, 256], F32, kind="ExternalInput").ap()
    vecs = {}
    for name in ("bq", "bk", "bv", "bo"):
        vecs[name] = nc.dram_tensor(name, [1024], F32, kind="ExternalInput").ap()
    xout = nc.dram_tensor("xout", [D, S], F32, kind="ExternalOutput").ap()
    em = Em(nc, stack, pool_bytes=92160)
    C = declare_consts(nc, em, need=("ehq16", "ehb16"))
    emit_l3_attn(em, C, xin, xout, wqp, wkp, wvp, wop,
                 vecs["bq"], vecs["bk"], vecs["bv"], vecs["bo"], tag="l3a")
    em.finalize()
    stack.close()
    return nc


def build_l0_attn_program():
    nc = bass.Bass("TRN2", target_bir_lowering=False)
    stack = ExitStack()
    xin = nc.dram_tensor("xin", [D, S], F32, kind="ExternalInput").ap()
    wqp = nc.dram_tensor("wqp", [4, 4, 128, 256], F32, kind="ExternalInput").ap()
    wkp = nc.dram_tensor("wkp", [4, 4, 128, 256], F32, kind="ExternalInput").ap()
    wvp = nc.dram_tensor("wvp", [4, 4, 128, 256], F32, kind="ExternalInput").ap()
    wop = nc.dram_tensor("wop", [8, 2, 128, 256], F32, kind="ExternalInput").ap()
    vecs = {}
    for name, n in (("bq", 512), ("bk", 512), ("bv", 512), ("bo", 1024),
                    ("alpha", 512), ("augp", 512), ("lns", 1024), ("lnb", 1024)):
        vecs[name] = nc.dram_tensor(name, [n], F32, kind="ExternalInput").ap()
    xout = nc.dram_tensor("xout", [D, S], F32, kind="ExternalOutput").ap()
    em = Em(nc, stack)
    C = declare_consts(nc, em, need=("ident_f32", "ehq", "ehb", "eps_ln2"))
    emit_l0_attn(em, C, xin, xout, wqp, wkp, wvp, wop,
                 vecs["bq"], vecs["bk"], vecs["bv"], vecs["bo"],
                 vecs["alpha"], vecs["augp"], vecs["lns"], vecs["lnb"], tag="l0a")
    em.finalize()
    stack.close()
    return nc


def build_l2_ffn_program():
    nc = bass.Bass("TRN2", target_bir_lowering=False)
    stack = ExitStack()
    xin = nc.dram_tensor("xin", [D, S], F32, kind="ExternalInput").ap()
    wg = nc.dram_tensor("wg", [2816, 1024], F32, kind="ExternalInput").ap()
    wu = nc.dram_tensor("wu", [2816, 1024], F32, kind="ExternalInput").ap()
    wd = nc.dram_tensor("wd", [1024, 2816], F32, kind="ExternalInput").ap()
    xout = nc.dram_tensor("xout", [D, S], F32, kind="ExternalOutput").ap()
    em = Em(nc, stack)
    C = declare_consts(nc, em, need=("ident_f32", "ident_bf16", "e16x", "e176a", "e176b"))
    emit_l2_ffn(em, C, xin, xout, wg, wu, wd, tag="l2")
    em.finalize()
    stack.close()
    return nc


def build_l0_ffn_program():
    nc = bass.Bass("TRN2", target_bir_lowering=False)
    stack = ExitStack()
    xin = nc.dram_tensor("xin", [D, S], F32, kind="ExternalInput").ap()
    w1 = nc.dram_tensor("w1", [32, 4, 128, 256], F32, kind="ExternalInput").ap()
    w2 = nc.dram_tensor("w2", [32, 4, 128, 256], F32, kind="ExternalInput").ap()
    h1 = nc.dram_tensor("h1", [32, 16, 128, 256], F32, kind="ExternalInput").ap()
    h2 = nc.dram_tensor("h2", [32, 16, 128, 256], F32, kind="ExternalInput").ap()
    w3 = nc.dram_tensor("w3", [8, 16, 128, 256], F32, kind="ExternalInput").ap()
    s1 = nc.dram_tensor("s1", [4096], F32, kind="ExternalInput").ap()
    s2 = nc.dram_tensor("s2", [4096], F32, kind="ExternalInput").ap()
    sh1 = nc.dram_tensor("sh1", [4096], F32, kind="ExternalInput").ap()
    sh2 = nc.dram_tensor("sh2", [4096], F32, kind="ExternalInput").ap()
    s3 = nc.dram_tensor("s3", [1024], F32, kind="ExternalInput").ap()
    xout = nc.dram_tensor("xout", [D, S], F32, kind="ExternalOutput").ap()
    em = Em(nc, stack)
    C = declare_consts(nc, em, need=())
    emit_l0_ffn(em, C, xin, xout, w1, w2, h1, h2, w3, s1, s2, sh1, sh2, s3, tag="l0")
    em.finalize()
    stack.close()
    return nc


def build_l3_ffn_program(debug=False):
    nc = bass.Bass("TRN2", target_bir_lowering=False)
    stack = ExitStack()
    xin = nc.dram_tensor("xin", [D, S], F32, kind="ExternalInput").ap()
    w = nc.dram_tensor("w", [D, D], F32, kind="ExternalInput").ap()
    bias = nc.dram_tensor("bias", [D], F32, kind="ExternalInput").ap()
    xout = nc.dram_tensor("xout", [D, S], F32, kind="ExternalOutput").ap()
    dbg = None
    if debug:
        dbg = {
            "xdb": nc.dram_tensor("dbg_xdb", [128, ND * TT], BF16, kind="ExternalOutput").ap(),
            "xcf": nc.dram_tensor("dbg_xcf", [128, ND * TT], F32, kind="ExternalOutput").ap(),
            "sfr": nc.dram_tensor("dbg_sfr", [32, TT], F32, kind="ExternalOutput").ap(),
            "sfs": nc.dram_tensor("dbg_sfs", [32, TT], F32, kind="ExternalOutput").ap(),
            "wqT": nc.dram_tensor("dbg_wqT", [128, ND * D], BF16, kind="ExternalOutput").ap(),
        }
    em = Em(nc, stack)
    C = declare_consts(nc, em, need=("ident_f32", "ident_bf16", "e32"))
    emit_l3_ffn(em, C, xin, xout, w, bias, tag="l3", dbg=dbg)
    em.finalize()
    stack.close()
    return nc


def build_fused_program():
    nc = bass.Bass("TRN2", target_bir_lowering=False)
    stack = ExitStack()
    em = Em(nc, stack, pool_bytes=118784)

    def din(name, shape, dt=F32):
        return nc.dram_tensor(name, list(shape), dt, kind="ExternalInput").ap()

    xin = din("xin", [D, S])
    xout = nc.dram_tensor("xout", [D, S], F32, kind="ExternalOutput").ap()
    resid = [xin] + [nc.dram_tensor(f"r{i}", [D, S], F32, kind="Internal").ap()
                     for i in range(11)] + [xout]

    l0f = {k: din("l0f_" + k, s, dt) for k, s, dt in (
        ("w1", [32, 4, 128, 256], F32), ("w2", [32, 4, 128, 256], F32),
        ("h1", [32, 16, 128, 256], F32), ("h2", [32, 16, 128, 256], F32),
        ("w3", [8, 16, 128, 256], F32), ("s1", [4096], F32), ("s2", [4096], F32),
        ("sh1", [4096], F32), ("sh2", [4096], F32), ("s3", [1024], F32))}
    l0a = {k: din("l0a_" + k, s, dt) for k, s, dt in (
        ("wqp", [4, 4, 128, 256], F32), ("wkp", [4, 4, 128, 256], F32),
        ("wvp", [4, 4, 128, 256], F32), ("wop", [8, 2, 128, 256], F32),
        ("bq", [512], F32), ("bk", [512], F32), ("bv", [512], F32), ("bo", [1024], F32),
        ("alpha", [512], F32), ("augp", [512], F32), ("lns", [1024], F32),
        ("lnb", [1024], F32))}
    l1f = {k: din("l1f_" + k, s, dt) for k, s, dt in (
        ("wg", [24, 4, 128, 256], I32), ("wu", [24, 4, 128, 256], I32),
        ("wd", [8, 12, 128, 256], I32))}
    l1a = {k: din("l1a_" + k, s, dt) for k, s, dt in (
        ("wqkvp", [24, 4, 128, 256], F32), ("bq", [1024], F32), ("bk", [1024], F32),
        ("bv", [1024], F32), ("gam", [16], F32), ("bet", [16], F32))}
    l2f = {k: din("l2f_" + k, s, dt) for k, s, dt in (
        ("wg", [2816, 1024], F32), ("wu", [2816, 1024], F32), ("wd", [1024, 2816], F32))}
    l2a = {k: din("l2a_" + k, s, dt) for k, s, dt in (
        ("wqp", [8, 4, 128, 256], F32), ("wkp", [8, 4, 128, 256], F32),
        ("wvp", [8, 4, 128, 256], F32), ("wop", [8, 4, 128, 256], F32),
        ("bq", [1024], F32), ("bk", [1024], F32), ("bv", [1024], F32),
        ("bo", [1024], F32), ("cosr", [128, S], F32), ("sinr", [128, S], F32))}
    l3f = {k: din("l3f_" + k, s, dt) for k, s, dt in (
        ("w", [1024, 1024], F32), ("bias", [1024], F32))}
    l3a = {k: din("l3a_" + k, s, dt) for k, s, dt in (
        ("wqp", [8, 4, 128, 256], F32), ("wkp", [8, 4, 128, 256], F32),
        ("wvp", [8, 4, 128, 256], F32), ("wop", [8, 4, 128, 256], F32),
        ("bq", [1024], F32), ("bk", [1024], F32), ("bv", [1024], F32),
        ("bo", [1024], F32))}

    C = declare_consts(nc, em, need=("ident_f32", "ident_bf16", "e32", "e16x",
                                     "e176a", "e176b", "ehq", "ehb", "eps_ln2",
                                     "ehq16", "ehb16", "tri", "ones_col1"))
    ws = {"l0f": {}, "l1f": {}, "l2f": {}}

    em.pool()
    for nm in ("ln_s1", "ln_var", "ln_rstd", "ln_scr"):
        em.sb_global(nm, [1, TT], F32)
    for bank in ("ps_g0", "ps_g1", "ps_u0", "ps_u1", "ps_lnS", "ps_lnB", "ps_lnR"):
        em.ps(bank)

    steps = [
        lambda i, o, t: emit_l0_ffn(em, C, i, o, l0f["w1"], l0f["w2"], l0f["h1"],
                                    l0f["h2"], l0f["w3"], l0f["s1"], l0f["s2"],
                                    l0f["sh1"], l0f["sh2"], l0f["s3"], tag=t,
                                    wstate=ws["l0f"]),
        lambda i, o, t: emit_l0_attn(em, C, i, o, l0a["wqp"], l0a["wkp"], l0a["wvp"],
                                     l0a["wop"], l0a["bq"], l0a["bk"], l0a["bv"],
                                     l0a["bo"], l0a["alpha"], l0a["augp"],
                                     l0a["lns"], l0a["lnb"], tag=t),
        lambda i, o, t: emit_l0_ffn(em, C, i, o, l0f["w1"], l0f["w2"], l0f["h1"],
                                    l0f["h2"], l0f["w3"], l0f["s1"], l0f["s2"],
                                    l0f["sh1"], l0f["sh2"], l0f["s3"], tag=t,
                                    wstate=ws["l0f"]),
        lambda i, o, t: emit_l1_ffn(em, C, i, o, l1f["wg"], l1f["wu"], l1f["wd"],
                                    tag=t, wstate=ws["l1f"]),
        lambda i, o, t: emit_l1_attn(em, C, i, o, l1a["wqkvp"], l1a["bq"], l1a["bk"],
                                     l1a["bv"], l1a["gam"], l1a["bet"], tag=t),
        lambda i, o, t: emit_l1_ffn(em, C, i, o, l1f["wg"], l1f["wu"], l1f["wd"],
                                    tag=t, wstate=ws["l1f"]),
        lambda i, o, t: emit_l2_ffn(em, C, i, o, l2f["wg"], l2f["wu"], l2f["wd"],
                                    tag=t, wstate=ws["l2f"]),
        lambda i, o, t: emit_l2_attn(em, C, i, o, l2a["wqp"], l2a["wkp"], l2a["wvp"],
                                     l2a["wop"], l2a["bq"], l2a["bk"], l2a["bv"],
                                     l2a["bo"], l2a["cosr"], l2a["sinr"], tag=t),
        lambda i, o, t: emit_l2_ffn(em, C, i, o, l2f["wg"], l2f["wu"], l2f["wd"],
                                    tag=t, wstate=ws["l2f"]),
        lambda i, o, t: emit_l3_ffn(em, C, i, o, l3f["w"], l3f["bias"], tag=t),
        lambda i, o, t: emit_l3_attn(em, C, i, o, l3a["wqp"], l3a["wkp"], l3a["wvp"],
                                     l3a["wop"], l3a["bq"], l3a["bk"], l3a["bv"],
                                     l3a["bo"], tag=t),
        lambda i, o, t: emit_l3_ffn(em, C, i, o, l3f["w"], l3f["bias"], tag=t),
    ]
    for idx, fn in enumerate(steps):
        em.push_scope()
        fn(resid[idx], resid[idx + 1], f"s{idx}")
        em.join()
        em.pop_scope()
    em.finalize()
    stack.close()
    return nc



_PROG_CACHE = {}


def _prog(name, builder):
    if name not in _PROG_CACHE:
        _PROG_CACHE[name] = builder()
    return _PROG_CACHE[name]


def kernel(**inputs):
    ii = inputs
    consts = {"c_" + k: v for k, v in build_consts().items()}
    ncores = 8

    T_ = np.ascontiguousarray
    packs = {}
    packs["l0f"] = {
        "w1": pack_w(ii["L0_ffn_w1_w"]), "w2": pack_w(ii["L0_ffn_w2_w"]),
        "h1": pack_w(ii["L0_ffn_h1_w"]), "h2": pack_w(ii["L0_ffn_h2_w"]),
        "w3": pack_w(ii["L0_ffn_w3_w"]),
        "s1": ii["L0_ffn_w1_s"], "s2": ii["L0_ffn_w2_s"],
        "sh1": ii["L0_ffn_h1_s"], "sh2": ii["L0_ffn_h2_s"], "s3": ii["L0_ffn_w3_s"]}
    packs["l0a"] = {
        "wqp": pack_w(T_(ii["L0_attn_Wq"].T)), "wkp": pack_w(T_(ii["L0_attn_Wk"].T)),
        "wvp": pack_w(T_(ii["L0_attn_Wv"].T)), "wop": pack_w(T_(ii["L0_attn_Wo"].T)),
        "bq": ii["L0_attn_bq"], "bk": ii["L0_attn_bk"], "bv": ii["L0_attn_bv"],
        "bo": ii["L0_attn_bo"],
        "alpha": T_(ii["L0_attn_alpha"].reshape(-1)),
        "augp": T_(ii["L0_attn_aug_param"].reshape(-1)),
        "lns": ii["L0_attn_ln_scale"], "lnb": ii["L0_attn_ln_bias"]}
    perm = []
    for part in range(3):
        for h in range(16):
            perm.extend(range(192 * h + 64 * part, 192 * h + 64 * part + 64))
    perm = np.array(perm)
    bperm = ii["L1_attn_bqkv"][perm]
    packs["l1a"] = {
        "wqkvp": pack_w(T_(ii["L1_attn_Wqkv"][perm])),
        "bq": T_(bperm[0:1024]), "bk": T_(bperm[1024:2048]), "bv": T_(bperm[2048:3072]),
        "gam": T_(ii["L1_attn_seq_gamma"].reshape(-1)),
        "bet": T_(ii["L1_attn_seq_beta"].reshape(-1))}
    packs["l1f"] = {"wg": pack_w(ii["L1_ffn_w_gate"]), "wu": pack_w(ii["L1_ffn_w_up"]),
                    "wd": pack_w(ii["L1_ffn_w_down"])}
    packs["l2f"] = {"wg": ii["L2_ffn_w_gate"], "wu": ii["L2_ffn_w_up"],
                    "wd": ii["L2_ffn_w_down"]}
    ang = np.pi / (2.0 * S) * np.arange(S, dtype=np.float32)
    packs["l2a"] = {
        "wqp": pack_w(T_(ii["L2_attn_Wq"].T)), "wkp": pack_w(T_(ii["L2_attn_Wk"].T)),
        "wvp": pack_w(T_(ii["L2_attn_Wv"].T)), "wop": pack_w(T_(ii["L2_attn_Wo"].T)),
        "bq": ii["L2_attn_bq"], "bk": ii["L2_attn_bk"], "bv": ii["L2_attn_bv"],
        "bo": ii["L2_attn_bo"],
        "cosr": np.broadcast_to(np.cos(ang), (128, S)).copy(),
        "sinr": np.broadcast_to(np.sin(ang), (128, S)).copy()}
    packs["l3f"] = {"w": ii["L3_ffn_weight"], "bias": ii["L3_ffn_bias"]}
    packs["l3a"] = {
        "wqp": pack_w(ii["L3_attn_wq"]), "wkp": pack_w(ii["L3_attn_wk"]),
        "wvp": pack_w(ii["L3_attn_wv"]), "wop": pack_w(ii["L3_attn_wo"]),
        "bq": ii["L3_attn_bq"], "bk": ii["L3_attn_bk"], "bv": ii["L3_attn_bv"],
        "bo": ii["L3_attn_bo"]}

    x = np.asarray(ii["L0_attn_x"], np.float32)
    xs = [feat_major(x[c]) for c in range(ncores)]

    nc = _prog("fused", build_fused_program)
    base = {}
    for pk, pref in (("l0f", "l0f_"), ("l0a", "l0a_"), ("l1f", "l1f_"),
                     ("l1a", "l1a_"), ("l2f", "l2f_"), ("l2a", "l2a_"),
                     ("l3f", "l3f_"), ("l3a", "l3a_")):
        for k, v in packs[pk].items():
            base[pref + k] = v
    base.update(consts)
    in_maps = []
    for c in range(ncores):
        im = dict(base)
        im["xin"] = xs[c]
        in_maps.append(im)
    res = run_bass_kernel_spmd(nc, in_maps, core_ids=list(range(ncores)))
    out = np.stack([np.asarray(res.results[c]["xout"], np.float32).T
                    for c in range(ncores)], axis=0)
    return np.ascontiguousarray(out.astype(np.float32))
